# version 17
# baseline (speedup 1.0000x reference)
import numpy as np
import ml_dtypes
from contextlib import ExitStack

import concourse.bass as bass
import concourse.mybir as mybir
from concourse.bass_utils import run_bass_kernel_spmd

F32 = mybir.dt.float32
BF16 = mybir.dt.bfloat16
I32 = mybir.dt.int32
U8 = mybir.dt.uint8
AF = mybir.ActivationFunctionType
ALU = mybir.AluOpType
AX = mybir.AxisListType.X
BF = ml_dtypes.bfloat16

P = 128
T = 1024
D = 2048
NH = 16
HD = 128
FC = D // P
I0, NIC0 = 8192, 64
I1, NIC1 = 5504, 43
NEG = -1.0e38
SCALE0 = 1.0 / float(np.sqrt(128.0))
SHARD = 65

class Buf:
    __slots__ = ("w", "r")
    def __init__(self):
        self.w = None
        self.r = {}

NLANE = 8

class Sched:
    def __init__(self, nc):
        self.nc = nc
        self.q = {"pe": nc.tensor, "dve": nc.vector, "act": nc.scalar,
                  "pool": nc.gpsimd, "sp": nc.sync}
        self.sem = {}
        self.cnt = {}
        for n in ("pe", "dve", "act", "pool", "sp", "cc"):
            self.sem[n] = nc.alloc_semaphore(f"s_{n}")
            self.cnt[n] = 0
        self.rr = {}
        for qn in ("sp", "pool", "act"):
            for l in range(NLANE):
                n = f"D{qn}{l}"
                self.sem[n] = nc.alloc_semaphore(f"s_{n}")
                self.cnt[n] = 0
            self.rr[qn] = 0
        self.lastw = {}
        self.bufs = {}

    def buf(self, key):
        b = self.bufs.get(key)
        if b is None:
            b = self.bufs[key] = Buf()
        return b

    def _wait(self, eng, ctr, val):
        if val <= 0:
            return
        k = (eng, ctr)
        if self.lastw.get(k, 0) >= val:
            return
        self.q[eng].wait_ge(self.sem[ctr], val)
        self.lastw[k] = val

    def step(self, eng, r=(), w=(), emit=None, dma=False, cc=False):
        need = {}
        for key in r:
            b = self.buf(key)
            if b.w and need.get(b.w[0], 0) < b.w[1]:
                need[b.w[0]] = b.w[1]
        for key in w:
            b = self.buf(key)
            if b.w and need.get(b.w[0], 0) < b.w[1]:
                need[b.w[0]] = b.w[1]
            for c, v in b.r.items():
                if need.get(c, 0) < v:
                    need[c] = v
        if dma:
            lane = self.rr[eng] % NLANE
            self.rr[eng] += 1
            ctr = f"D{eng}{lane}"
            if self.cnt[ctr] > 0:
                self._wait(eng, ctr, self.cnt[ctr])
        for c in sorted(need):
            self._wait(eng, c, need[c])
        insts = emit(self.q[eng])
        if not isinstance(insts, (list, tuple)):
            insts = [insts]
        if cc:
            ctr = "cc"
            for inst in insts:
                self.cnt[ctr] += 1
                inst.then_inc(self.sem[ctr])
        elif dma:
            assert len(insts) == 1, "one dma_start per dma step"
            self.cnt[ctr] += 16
            insts[0].then_inc(self.sem[ctr], 16)
        else:
            ctr = eng
            self.cnt[ctr] += 1
            insts[-1].then_inc(self.sem[ctr], 1)
        v = self.cnt[ctr]
        for key in r:
            self.buf(key).r[ctr] = v
        for key in w:
            b = self.buf(key)
            b.w = (ctr, v)
            b.r = {}
        return v

    def barrier(self):
        for e in self.q:
            for c, n in self.cnt.items():
                self._wait(e, c, n)


def emit_stage0(nc, s, src, xres, xnT, id_s, halo_src=None, haloT=None, pfx="ln", load=True):
    assert load or halo_src is None
    with ExitStack() as ph:
        sb = lambda n, sh, dt: ph.enter_context(nc.sbuf_tensor(n, sh, dt))
        tp = [ph.enter_context(nc.psum_tensor(f"{pfx}tp{i}", [128, 1024], BF16)) for i in range(2)]
        sc = sb(f"{pfx}_sc", [P, 2, D], F32)
        sq = sb(f"{pfx}_sq", [P, 2, D], F32)
        xb = sb(f"{pfx}_xb", [P, 2, D], BF16)
        scol = sb(f"{pfx}_scol", [P, 16], F32)
        mcol = sb(f"{pfx}_mcol", [P, 16], F32)
        vcol = sb(f"{pfx}_vcol", [P, 16], F32)
        sqv = sb(f"{pfx}_sqv", [P, 16], F32)
        rstd = sb(f"{pfx}_rstd", [P, 16], F32)
        nc._lncol = (scol, mcol, vcol, rstd)
        for tn, kk in ((scol, "scol"), (mcol, "mcol"), (vcol, "vcol"), (sqv, "sqv"), (rstd, "rstd")):
            s.step("dve", w=[(kk, tt) for tt in range(16)] + [(kk,)],
                   emit=lambda q, tn=tn: q.memset(tn[:, :], 0.0))
        hx = sb(f"{pfx}_hx", [P, 1, D], F32) if halo_src is not None else None
        ntile = 8 + (4 if halo_src is not None else 0)

        def load_tile(tt, tag):
            if tt < 8:
                if load:
                    s.step("sp", w=[("xres", tt)], dma=True,
                           emit=lambda q, tt=tt: q.dma_start(
                               out=xres[:, tt, :], in_=src[tt * 128:(tt + 1) * 128, :]))
                return xres[:, tt, :], ("xres", tt)
            th = tt - 8
            slot = 0
            s.step("sp", w=[(tag, slot)], dma=True,
                   emit=lambda q, th=th, slot=slot: q.dma_start(
                       out=hx[:, slot, :], in_=halo_src[th * 128:(th + 1) * 128, :]))
            return hx[:, slot, :], (tag, slot)

        for tt in range(ntile):
            xin, key = load_tile(tt, "hx")
            slot = tt % 2
            s.step("dve", r=[key], w=[("scol", tt)],
                   emit=lambda q, xin=xin, tt=tt: q.reduce_sum(scol[:, tt:tt + 1], xin, axis=AX))
            s.step("dve", r=[("scol", tt)], w=[("mcol", tt)],
                   emit=lambda q, tt=tt: q.tensor_scalar_mul(mcol[:, tt:tt + 1], scol[:, tt:tt + 1], 1.0 / D))
            s.step("dve", r=[key, ("mcol", tt)], w=[("sc", slot)],
                   emit=lambda q, xin=xin, slot=slot, tt=tt: q.tensor_single_scalar(
                       sc[:, slot, :], xin, mcol[:, tt:tt + 1], ALU.subtract))
            s.step("act", r=[("sc", slot)], w=[("sq", slot), ("vcol", tt)],
                   emit=lambda q, slot=slot, tt=tt: q.activation(
                       out=sq[:, slot, :], in_=sc[:, slot, :], func=AF.Square,
                       accum_out=vcol[:, tt:tt + 1]))
        s.step("act", r=[("vcol", tt) for tt in range(ntile)] + [("cst",)], w=[("sqv",)],
               emit=lambda q: q.activation(out=sqv[:, 0:ntile], in_=vcol[:, 0:ntile],
                                           func=AF.Sqrt, scale=1.0 / D, bias=nc._cst[:, 0:1]))
        s.step("dve", r=[("sqv",)], w=[("rstd",)],
               emit=lambda q: q.reciprocal(rstd[:, 0:ntile], sqv[:, 0:ntile]))
        for tt in range(ntile):
            if tt < 8:
                xin, key = xres[:, tt, :], ("xres", tt)
            else:
                xin, key = load_tile(tt, "hx")
            slot = tt % 2
            s.step("dve", r=[key, ("mcol", tt), ("rstd",)], w=[("xb", slot)],
                   emit=lambda q, xin=xin, slot=slot, tt=tt: q.tensor_scalar(
                       xb[:, slot, :], xin, mcol[:, tt:tt + 1], rstd[:, tt:tt + 1],
                       ALU.subtract, ALU.mult))
            for fc in range(FC):
                bank = 6 + (fc % 2)
                dstT = xnT if tt < 8 else haloT
                dcol = (tt * 128) if tt < 8 else ((tt - 8) * 128)
                dkey = ("xnT", fc) if tt < 8 else ("haloT", fc)
                s.step("pe", r=[("xb", slot), ("id",)], w=[("ps", bank)],
                       emit=lambda q, slot=slot, fc=fc, bank=bank: q.transpose(
                           tp[bank - 6][:, 0:128], xb[:, slot, fc * 128:(fc + 1) * 128], id_s[:, :]))
                if fc % 2 == 0:
                    s.step("dve", r=[("ps", bank)], w=[dkey],
                           emit=lambda q, bank=bank, dstT=dstT, fc=fc, dcol=dcol: q.tensor_copy(
                               dstT[:, fc, dcol:dcol + 128], tp[bank - 6][:, 0:128]))
                else:
                    s.step("act", r=[("ps", bank)], w=[dkey],
                           emit=lambda q, bank=bank, dstT=dstT, fc=fc, dcol=dcol: q.copy(
                               dstT[:, fc, dcol:dcol + 128], tp[bank - 6][:, 0:128]))
        s.barrier()


XNKEYS = [("xnT", fc) for fc in range(FC)]
HALOKEYS = [("haloT", fc) for fc in range(FC)]


def emit_l0_attn(nc, s, xres, xnT, haloT, id_s, pb, attn_st, dram, bg=None):
    with ExitStack() as ph:
        sb = lambda n, sh, dt: ph.enter_context(nc.sbuf_tensor(n, sh, dt))
        tp = [ph.enter_context(nc.psum_tensor(f"atp{i}", [128, 1024], BF16)) for i in range(2)]
        masks = sb("a_masks", [P, 8, 640], BF16)
        wv_s = sb("a_wv", [P, FC, 512], BF16)
        wq_s = sb("a_wq", [P, 2, FC, 128], BF16)
        wk_s = sb("a_wk", [P, 2, FC, 128], BF16)
        vg = sb("a_vg", [P, 12, 512], BF16)
        qh = sb("a_qh", [P, 2, T], BF16)
        kh = sb("a_kh", [P, 2, 1536], BF16)
        sm = sb("a_sm", [P, 2, 640], F32)
        probs = sb("a_probs", [P, 2, 640], BF16)
        probsT = sb("a_probsT", [P, 2, 640], BF16)
        mx = sb("a_mx", [P, 8], F32)
        ath = sb("a_ath", [P, 2, T], BF16)

        s.step("sp", w=[("masks",)], dma=True,
               emit=lambda q: q.dma_start(out=masks[:, :, :],
                                          in_=dram["masks"].ap().rearrange("t p j -> p t j")))

        def load_wv(hg):
            s.step("pool", w=[("wvw",)], dma=True,
                   emit=lambda q, hg=hg: q.dma_start(out=wv_s[:, :, :], in_=dram["wv"].ap()[hg]))

        def load_wqk(h):
            hp = h % 2
            s.step("pool", w=[("wqw", hp)], dma=True,
                   emit=lambda q, h=h, hp=hp: q.dma_start(out=wq_s[:, hp, :, :], in_=dram["wq"].ap()[h]))
            s.step("pool", w=[("wkw", hp)], dma=True,
                   emit=lambda q, h=h, hp=hp: q.dma_start(out=wk_s[:, hp, :, :], in_=dram["wk"].ap()[h]))

        def proj_steps(h):
            hp = h % 2
            for th in range(2):
                bank = th % 2
                yield lambda th=th, hp=hp, bank=bank: (
                    s.step("pe", r=XNKEYS + [("wqw", hp)], w=[("ps", bank)],
                           emit=lambda q: [
                               q.matmul(pb[bank][:, :], wq_s[:, hp, hc, :],
                                        xnT[:, hc, th * 512:(th + 1) * 512],
                                        start=(hc == 0), stop=(hc == 15)) for hc in range(FC)]),
                    s.step("dve", r=[("ps", bank)], w=[("qh", hp)],
                           emit=lambda q: q.tensor_copy(
                               qh[:, hp, th * 512:(th + 1) * 512], pb[bank][:, :])))
            for kp in range(3):
                bank = kp % 2
                lsrc = haloT if kp == 0 else xnT
                lkeys = HALOKEYS if kp == 0 else XNKEYS
                tcol = 0 if kp == 0 else (kp - 1) * 512
                yield lambda kp=kp, hp=hp, bank=bank, lsrc=lsrc, lkeys=lkeys, tcol=tcol: (
                    s.step("pe", r=lkeys + [("wkw", hp)], w=[("ps", bank)],
                           emit=lambda q: [
                               q.matmul(pb[bank][:, :], wk_s[:, hp, hc, :],
                                        lsrc[:, hc, tcol:tcol + 512],
                                        start=(hc == 0), stop=(hc == 15)) for hc in range(FC)]),
                    s.step("dve", r=[("ps", bank)], w=[("kh", hp)],
                           emit=lambda q: q.tensor_copy(
                               kh[:, hp, kp * 512:(kp + 1) * 512], pb[bank][:, :])))

        def stA(h, tt):
            hp = h % 2
            bA, bC = 2 + tt % 2, 4 + tt % 2
            s.step("pe", r=[("qh", hp), ("kh", hp)], w=[("ps", bA)],
                   emit=lambda q: q.matmul(
                       pb[bA][:, :], qh[:, hp, tt * 128:(tt + 1) * 128],
                       kh[:, hp, tt * 128:tt * 128 + 512], start=True, stop=True))
            s.step("pe", r=[("qh", hp), ("kh", hp)], w=[("ps", bC)],
                   emit=lambda q: q.matmul(
                       pb[bC][:, 128:256], qh[:, hp, tt * 128:(tt + 1) * 128],
                       kh[:, hp, tt * 128 + 512:tt * 128 + 640], start=True, stop=True))

        def stB(h, tt):
            pp = tt % 2
            bA, bC = 2 + tt % 2, 4 + tt % 2
            s.step("dve", r=[("ps", bA), ("masks",)], w=[("sm", pp)],
                   emit=lambda q: q.scalar_tensor_tensor(
                       sm[:, pp, 0:512], pb[bA][:, :], SCALE0, masks[:, tt, 0:512],
                       ALU.mult, ALU.add))
            s.step("dve", r=[("ps", bC), ("masks",)], w=[("sm", pp)],
                   emit=lambda q: q.scalar_tensor_tensor(
                       sm[:, pp, 512:640], pb[bC][:, 128:256], SCALE0, masks[:, tt, 512:640],
                       ALU.mult, ALU.add))
            s.step("dve", r=[("sm", pp)], w=[("mx", pp)],
                   emit=lambda q: q.reduce_max(mx[:, 4 * pp:4 * pp + 1], sm[:, pp, :], axis=AX))
            s.step("dve", r=[("mx", pp)], w=[("mx", pp)],
                   emit=lambda q: q.tensor_scalar_mul(
                       mx[:, 4 * pp + 1:4 * pp + 2], mx[:, 4 * pp:4 * pp + 1], -1.0))

        def stC(h, tt):
            pp = tt % 2
            s.step("act", r=[("sm", pp), ("mx", pp)], w=[("probs", pp), ("mx", pp)],
                   emit=lambda q: q.activation(
                       out=probs[:, pp, :], in_=sm[:, pp, :], func=AF.Exp,
                       scale=1.0, bias=mx[:, 4 * pp + 1:4 * pp + 2],
                       accum_out=mx[:, 4 * pp + 2:4 * pp + 3]))

        def stD(h, tt):
            pp = tt % 2
            s.step("dve", r=[("mx", pp)], w=[("mx", pp)],
                   emit=lambda q: q.reciprocal(
                       mx[:, 4 * pp + 3:4 * pp + 4], mx[:, 4 * pp + 2:4 * pp + 3]))
            s.step("dve", r=[("probs", pp), ("mx", pp)], w=[("probs", pp)],
                   emit=lambda q: q.tensor_single_scalar(
                       probs[:, pp, :], probs[:, pp, :], mx[:, 4 * pp + 3:4 * pp + 4], ALU.mult))

        def stE(h, tt):
            pp = tt % 2
            for jj in range(5):
                s.step("pe", r=[("probs", pp), ("id",)], w=[("ps", 6 + pp)],
                       emit=lambda q, jj=jj: q.transpose(
                           tp[pp][:, jj * 128:(jj + 1) * 128],
                           probs[:, pp, jj * 128:(jj + 1) * 128], id_s[:, :]))

        def stF(h, tt):
            pp = tt % 2
            s.step("act", r=[("ps", 6 + pp)], w=[("probsT", pp)],
                   emit=lambda q: q.copy(probsT[:, pp, :], tp[pp][:, 0:640]))

        def stG(h, hs, tt):
            pp = tt % 2
            bC = 4 + tt % 2
            s.step("pe", r=[("probsT", pp)] + [("vg", tt + jj) for jj in range(5)],
                   w=[("ps", bC)],
                   emit=lambda q: [
                       q.matmul(pb[bC][:, 0:128], vg[:, tt + jj, hs * 128:(hs + 1) * 128],
                                probsT[:, pp, jj * 128:(jj + 1) * 128],
                                start=(jj == 0), stop=(jj == 4)) for jj in range(5)])

        def stH(h, tt):
            hp = h % 2
            bC = 4 + tt % 2
            s.step("dve", r=[("ps", bC)], w=[("ath", hp)],
                   emit=lambda q: q.tensor_copy(
                       ath[:, hp, tt * 128:(tt + 1) * 128], pb[bC][:, 0:128]))

        load_wv(0)
        load_wqk(0)
        pending = list(proj_steps(0))
        for st in pending:
            st()
        for hg in range(4):
            for tc in range(12):
                bank = tc % 2
                if tc < 4:
                    lsrc, lkeys, tcol = haloT, HALOKEYS, tc * 128
                else:
                    lsrc, lkeys, tcol = xnT, XNKEYS, (tc - 4) * 128
                s.step("pe", r=lkeys + [("wvw",)], w=[("ps", bank)],
                       emit=lambda q, lsrc=lsrc, tcol=tcol, bank=bank: [
                           q.matmul(pb[bank][:, :], lsrc[:, hc, tcol:tcol + 128],
                                    wv_s[:, hc, :], start=(hc == 0), stop=(hc == 15))
                           for hc in range(FC)])
                s.step("dve", r=[("ps", bank)], w=[("vg", tc)],
                       emit=lambda q, tc=tc, bank=bank: q.tensor_copy(vg[:, tc, :], pb[bank][:, :]))
                if bg and tc % 3 == 2:
                    bg()
            if hg < 3:
                load_wv(hg + 1)
            for hs in range(4):
                h = hg * 4 + hs
                hp = h % 2
                if h < 15:
                    load_wqk(h + 1)
                nxt = list(proj_steps(h + 1)) if h + 1 < 16 else []
                for i in range(10):
                    if i < 8:
                        stA(h, i)
                        stB(h, i)
                    if i - 1 >= 0 and i - 1 < 8:
                        stD(h, i - 1)
                    if i - 2 >= 0:
                        stE(h, i - 2)
                        stF(h, i - 2)
                        stG(h, hs, i - 2)
                        stH(h, i - 2)
                    if i < 8:
                        stC(h, i)
                    if i in (1, 2, 3, 4, 5) and nxt:
                        nxt.pop(0)()
                    if bg and i in (3, 7):
                        bg()
                while nxt:
                    nxt.pop(0)()
                s.step("sp", r=[("ath", hp)], w=[("ast", h)], dma=True,
                       emit=lambda q, h=h, hp=hp: q.dma_start(out=attn_st.ap()[h], in_=ath[:, hp, :]))
        s.barrier()
    with ExitStack() as ph:
        sb = lambda n, sh, dt: ph.enter_context(nc.sbuf_tensor(n, sh, dt))
        wo_s = sb("a_wo", [P, 2, FC, 512], BF16)
        atile = sb("a_atile", [P, 2, FC, 128], BF16)
        astk = [("ast", h) for h in range(NH)]
        for ps2 in range(2):
            for o2 in range(2):
                s.step("pool", w=[("wow", o2)], dma=True,
                       emit=lambda q, ps2=ps2, o2=o2: q.dma_start(
                           out=wo_s[:, o2, :, :], in_=dram["wo"].ap()[ps2 * 2 + o2]))
            for tt in range(8):
                s.step("sp", r=astk, w=[("atile", tt % 2)], dma=True,
                       emit=lambda q, tt=tt: q.dma_start(
                           out=atile[:, tt % 2, :, :],
                           in_=attn_st.ap()[:, :, tt * 128:(tt + 1) * 128].rearrange("h d t -> d h t")))
                for o2 in range(2):
                    bank = 2 * (tt % 2) + o2
                    s.step("pe", r=[("atile", tt % 2), ("wow", o2)], w=[("ps", bank)],
                           emit=lambda q, tt=tt, o2=o2, bank=bank: [
                               q.matmul(pb[bank][:, :], atile[:, tt % 2, fc, :],
                                        wo_s[:, o2, fc, :], start=(fc == 0), stop=(fc == 15))
                               for fc in range(FC)])
                    s.step("dve", r=[("ps", bank)], w=[("xres", tt)],
                           emit=lambda q, tt=tt, ps2=ps2, o2=o2, bank=bank: q.tensor_tensor(
                               xres[:, tt, (ps2 * 2 + o2) * 512:(ps2 * 2 + o2 + 1) * 512],
                               xres[:, tt, (ps2 * 2 + o2) * 512:(ps2 * 2 + o2 + 1) * 512],
                               pb[bank][:, :], ALU.add))
                if bg:
                    bg()
        s.barrier()


def emit_qffn(nc, s, xres, xnT, pb, h_st, nic, corrections, wcfg, pfx="q", bg=None, after_a=None):
    with ExitStack() as ph:
        sb = lambda n, sh, dt: ph.enter_context(nc.sbuf_tensor(n, sh, dt))
        pext = [ph.enter_context(nc.psum_tensor(f"{pfx}px{i}", [128, 512], F32)) for i in range(2)]
        pb8 = list(pb) + pext
        wg_s = sb(f"{pfx}_wg", wcfg["gu_shape"], BF16)
        wu_s = sb(f"{pfx}_wu", wcfg["gu_shape"], BF16)
        wd_s = sb(f"{pfx}_wd", [P, 3, 512], BF16)
        wdr = sb(f"{pfx}_wdr", [P, 3, 512], U8) if wcfg.get("wd_raw") else None
        s._wdr = wdr
        s._wg, s._wu, s._wd = wg_s, wu_s, wd_s
        h_s = sb(f"{pfx}_h", [P, 3, T], BF16)
        hl = sb(f"{pfx}_hl", [P, 3, T], BF16)
        sg = sb(f"{pfx}_sgb", [P, 2, 512], BF16)
        su = sb(f"{pfx}_sub", [P, 2, 512], BF16)
        ones_c = sb(f"{pfx}_ones", [P, 1], BF16)
        m128 = sb(f"{pfx}_m128", [1, 512], BF16)
        xnsum = sb(f"{pfx}_xnsum", [1, T], BF16)
        hsum = sb(f"{pfx}_hsum", [1, T], BF16)
        if corrections:
            ytmp = sb(f"{pfx}_ytmp", [P, 2, 512], F32)
            gs_c = sb(f"{pfx}_gs", [P, nic], F32)
            us_c = sb(f"{pfx}_us", [P, nic], F32)
            dsrep = sb(f"{pfx}_dsrep", [P, D], F32)
            s.step("sp", w=[("gsc",)], dma=True,
                   emit=lambda q: q.dma_start(out=gs_c[:, :], in_=wcfg["gs"].ap()[:, :]))
            s.step("sp", w=[("usc",)], dma=True,
                   emit=lambda q: q.dma_start(out=us_c[:, :], in_=wcfg["us"].ap()[:, :]))
            s.step("sp", w=[("dsr",)], dma=True,
                   emit=lambda q: q.dma_start(out=dsrep[:, :], in_=wcfg["dsrep"].ap()[:, :]))
        s.step("dve", w=[("ones",)], emit=lambda q: q.memset(ones_c[:, :], 1.0))
        s.step("dve", w=[("m128",)], emit=lambda q: q.memset(m128[:, :], -128.0))
        gsl = wcfg["gu_slice"]

        if corrections:
            for th in range(2):
                s.step("pe", r=XNKEYS + [("ones",)], w=[("ps", 4 + th)],
                       emit=lambda q, th=th: [
                           q.matmul(pb[4 + th][0:1, :], ones_c[:, :],
                                    xnT[:, hc, th * 512:(th + 1) * 512],
                                    start=(hc == 0), stop=(hc == 15)) for hc in range(FC)])
                s.step("dve", r=[("ps", 4 + th)], w=[("xnsum",)],
                       emit=lambda q, th=th: q.tensor_copy(
                           xnsum[0:1, th * 512:(th + 1) * 512], pb[4 + th][0:1, :]))

        def emit_hsum(ic):
            for th in range(2):
                s.step("pe", r=[("h", ic % 3), ("ones",)], w=[("ps", 4 + th)],
                       emit=lambda q, ic=ic, th=th: q.matmul(
                           pb[4 + th][0:1, :], ones_c[:, :],
                           h_s[:, ic % 3, th * 512:(th + 1) * 512],
                           start=(ic == 0), stop=(ic == nic - 1)))

        wcfg["wg_load"](s, 0)
        wcfg["wu_load"](s, 0)
        for ic in range(nic):
            for gu in range(2):
                wkey = ("wg" if gu == 0 else "wu", ic % 2)
                ws = wg_s if gu == 0 else wu_s
                for th in range(2):
                    bank = gu * 2 + th
                    def mmgu(q, ws=ws, ic=ic, th=th, bank=bank):
                        ins = [q.matmul(pb[bank][:, :], gsl(ws, ic % 2, hc),
                                        xnT[:, hc, th * 512:(th + 1) * 512],
                                        start=(hc == 0),
                                        stop=(hc == 15 and not corrections))
                               for hc in range(FC)]
                        if corrections:
                            ins.append(q.matmul(pb[bank][:, :], m128[0:1, 0:128],
                                                xnsum[0:1, th * 512:(th + 1) * 512],
                                                start=False, stop=True))
                        return ins
                    rdk = XNKEYS + [wkey]
                    if corrections:
                        rdk = rdk + [("xnsum",), ("m128",)]
                    s.step("pe", r=rdk, w=[("ps", bank)], emit=mmgu)
            if ic + 1 < nic:
                wcfg["wg_load"](s, ic + 1)
                wcfg["wu_load"](s, ic + 1)
            for th in range(2):
                if corrections:
                    s.step("act", r=[("ps", th), ("gsc",)], w=[("sg", th)],
                           emit=lambda q, ic=ic, th=th: q.activation(
                               out=sg[:, th, :], in_=pb[th][:, :], func=AF.Silu,
                               scale=gs_c[:, ic:ic + 1]))
                    s.step("act", r=[("ps", 2 + th), ("usc",)], w=[("su", th)],
                           emit=lambda q, ic=ic, th=th: q.activation(
                               out=su[:, th, :], in_=pb[2 + th][:, :], func=AF.Copy,
                               scale=us_c[:, ic:ic + 1]))
                else:
                    s.step("act", r=[("ps", th)], w=[("sg", th)],
                           emit=lambda q, th=th: q.activation(
                               out=sg[:, th, :], in_=pb[th][:, :], func=AF.Silu))
                    s.step("act", r=[("ps", 2 + th)], w=[("su", th)],
                           emit=lambda q, th=th: q.activation(
                               out=su[:, th, :], in_=pb[2 + th][:, :], func=AF.Copy))
                s.step("dve", r=[("sg", th), ("su", th)], w=[("h", ic % 3)],
                       emit=lambda q, ic=ic, th=th: q.tensor_tensor(
                           h_s[:, ic % 3, th * 512:(th + 1) * 512],
                           sg[:, th, :], su[:, th, :], ALU.mult))
            s.step("sp", r=[("h", ic % 3)], w=[("hst", ic)], dma=True,
                   emit=lambda q, ic=ic: q.dma_start(out=h_st.ap()[ic], in_=h_s[:, ic % 3, :]))
            if bg:
                bg()
            if corrections and ic >= 2:
                emit_hsum(ic - 2)
        if corrections:
            emit_hsum(nic - 2)
            emit_hsum(nic - 1)
            for th in range(2):
                s.step("dve", r=[("ps", 4 + th)], w=[("hsum",)],
                       emit=lambda q, th=th: q.tensor_copy(
                           hsum[0:1, th * 512:(th + 1) * 512], pb[4 + th][0:1, :]))
        if after_a is not None:
            while bg and (after_a.__self__.n < SHARD):
                bg()
            after_a()
        HLD = 3
        WDD = 3
        for oc in range(4):
            for ic in range(nic):
                s.step("sp", r=[("hst", ic)], w=[("hl", ic % HLD)], dma=True,
                       emit=lambda q, ic=ic: q.dma_start(out=hl[:, ic % HLD, :], in_=h_st.ap()[ic]))
                wcfg["wd_load"](s, ic, oc)
                last = (ic == nic - 1) and not corrections
                s.step("pe", r=[("hl", ic % HLD), ("wd", ic % WDD)],
                       w=[("ps", tt2) for tt2 in range(8)],
                       emit=lambda q, ic=ic, last=last: [
                           q.matmul(pb8[tt2][:, :], hl[:, ic % HLD, tt2 * 128:(tt2 + 1) * 128],
                                    wd_s[:, ic % WDD, :], start=(ic == 0), stop=last)
                           for tt2 in range(8)])
                if bg:
                    bg()
            if corrections:
                s.step("pe", r=[("hsum",), ("m128",)], w=[("ps", tt2) for tt2 in range(8)],
                       emit=lambda q: [
                           q.matmul(pb8[tt2][:, :], hsum[0:1, tt2 * 128:(tt2 + 1) * 128],
                                    m128[0:1, :], start=False, stop=True) for tt2 in range(8)])
            for tt in range(8):
                if corrections:
                    s.step("dve", r=[("ps", tt), ("dsr",)], w=[("ytmp", tt % 2)],
                           emit=lambda q, tt=tt, oc=oc: q.tensor_tensor(
                               ytmp[:, tt % 2, :], pb8[tt][:, :],
                               dsrep[:, oc * 512:(oc + 1) * 512], ALU.mult))
                    s.step("dve", r=[("ytmp", tt % 2)], w=[("xres", tt)],
                           emit=lambda q, tt=tt, oc=oc: q.tensor_tensor(
                               xres[:, tt, oc * 512:(oc + 1) * 512],
                               xres[:, tt, oc * 512:(oc + 1) * 512],
                               ytmp[:, tt % 2, :], ALU.add))
                else:
                    s.step("dve", r=[("ps", tt)], w=[("xres", tt)],
                           emit=lambda q, tt=tt, oc=oc: q.tensor_tensor(
                               xres[:, tt, oc * 512:(oc + 1) * 512],
                               xres[:, tt, oc * 512:(oc + 1) * 512], pb8[tt][:, :], ALU.add))
        s.barrier()


def _mk_common(nc):
    s = Sched(nc)
    pb = [nc.alloc_psum_tensor(f"pb{i}", [128, 512], F32) for i in range(6)]
    xres = nc.alloc_sbuf_tensor("xres", [P, 8, D], F32)
    xnT = nc.alloc_sbuf_tensor("xnT", [P, FC, T], BF16)
    id_d = nc.inline_tensor(np.eye(128, dtype=BF), name="ident")
    id_s = nc.alloc_sbuf_tensor("id_s", [P, 128], BF16)
    s.step("sp", w=[("id",)], dma=True,
           emit=lambda q: q.dma_start(out=id_s[:, :], in_=id_d.ap()[:, :]))
    cst = nc.alloc_sbuf_tensor("cst", [P, 2], F32)
    s.step("dve", w=[("cst",)], emit=lambda q: q.memset(cst[:, 0:1], 1e-5))
    s.step("dve", w=[("cst",)], emit=lambda q: q.memset(cst[:, 1:2], 1e-6))
    nc._cst = cst
    return s, pb, xres, xnT, id_s


def _store_x(nc, s, xres, dst):
    for tt in range(8):
        s.step("sp", r=[("xres", tt)], dma=True,
               emit=lambda q, tt=tt: q.dma_start(
                   out=dst.ap()[tt * 128:(tt + 1) * 128, :], in_=xres[:, tt, :]))
    s.barrier()


def build_l0(parts=("attn", "ffn"), debug=False):
    nc = bass.Bass("TRN2", target_bir_lowering=False)
    din = lambda n, sh, dt: nc.dram_tensor(n, sh, dt, kind="ExternalInput")
    x_own = din("x_own", [T, D], F32)
    x_halo = din("x_halo", [512, D], F32)
    dram = {
        "wq": din("wq_h", [NH, 128, FC, 128], F32),
        "wk": din("wk_h", [NH, 128, FC, 128], F32),
        "wv": din("wv_g", [4, 128, FC, 512], F32),
        "wo": din("wo_oc", [4, 128, FC, 512], F32),
        "masks": din("masks", [8, 128, 640], BF16),
    }
    gT_d = din("gT_h", [NIC0, 128, FC, 128], U8)
    uT_d = din("uT_h", [NIC0, 128, FC, 128], U8)
    dT_d = din("dT_h", [NIC0, 4, 128, 512], U8)
    gs_d = din("gs_c", [128, NIC0], F32)
    us_d = din("us_c", [128, NIC0], F32)
    dsrep_d = din("dsrep", [128, D], F32)
    x1 = nc.dram_tensor("x1", [T, D], F32, kind="ExternalOutput")
    kind_dbg = "ExternalOutput" if debug else "Internal"
    attn_st = nc.dram_tensor("attn_st", [NH, 128, T], BF16, kind=kind_dbg)
    h_st = nc.dram_tensor("h_st", [NIC0, 128, T], BF16, kind=kind_dbg)

    s, pb, xres, xnT, id_s = _mk_common(nc)
    haloT = nc.alloc_sbuf_tensor("haloT", [P, FC, 512], BF16)
    emit_stage0(nc, s, x_own.ap(), xres, xnT, id_s, x_halo.ap(), haloT)
    if debug:
        xnT_dbg = nc.dram_tensor("xnT_dbg", [128, FC, T], BF16, kind="ExternalOutput")
        s.step("sp", r=XNKEYS, dma=True,
               emit=lambda q: q.dma_start(out=xnT_dbg.ap()[:, :, :], in_=xnT[:, :, :]))
        col_dbg = nc.dram_tensor("col_dbg", [128, 64], F32, kind="ExternalOutput")
        for di, tn in enumerate(nc._lncol):
            s.step("sp", r=[("rstd",)], dma=True,
                   emit=lambda q, di=di, tn=tn: q.dma_start(
                       out=col_dbg.ap()[:, di * 16:(di + 1) * 16], in_=tn[:, :]))


    if "attn" in parts:
        emit_l0_attn(nc, s, xres, xnT, haloT, id_s, pb, attn_st, dram)

    if "ffn" in parts:
        def wg_load(s, ic):
            s.step("pool", w=[("wg", ic % 2)], dma=True,
                   emit=lambda q, ic=ic: q.dma_start(
                       out=s._wg[:, ic % 2, :, :], in_=gT_d.ap()[ic]))
        def wu_load(s, ic):
            s.step("pool", w=[("wu", ic % 2)], dma=True,
                   emit=lambda q, ic=ic: q.dma_start(
                       out=s._wu[:, ic % 2, :, :], in_=uT_d.ap()[ic]))
        def wd_load(s, ic, oc):
            s.step("pool", w=[("wd", ic % 3)], dma=True,
                   emit=lambda q, ic=ic, oc=oc: q.dma_start(
                       out=s._wd[:, ic % 3, :], in_=dT_d.ap()[ic, oc]))
        wcfg = {
            "gu_shape": [P, 2, FC, 128],
            "gu_slice": lambda ws, slot, hc: ws[:, slot, hc, :],
            "wg_load": wg_load, "wu_load": wu_load, "wd_load": wd_load,
            "gs": gs_d, "us": us_d, "dsrep": dsrep_d,
        }
        emit_qffn(nc, s, xres, xnT, pb, h_st, NIC0, True, wcfg)

    _store_x(nc, s, xres, x1)
    return nc


def build_deq():
    nc = bass.Bass("TRN2", target_bir_lowering=False)
    idx_d = nc.dram_tensor("l1idx_shard", [SHARD, 128, 512], I32, kind="ExternalInput")
    lut_d = nc.dram_tensor("lutrep", [128, 16], F32, kind="ExternalInput")
    out_d = nc.dram_tensor("wshard", [SHARD, 128, 512], BF16, kind="ExternalOutput")
    s = Sched(nc)
    lut_s = nc.alloc_sbuf_tensor("lut_s", [P, 16], F32)
    ib = nc.alloc_sbuf_tensor("ib", [P, 4, 512], I32)
    wb = nc.alloc_sbuf_tensor("wb", [P, 4, 512], BF16)
    tb = nc.alloc_sbuf_tensor("tb", [P, 4, 512], BF16)
    s.step("sp", w=[("lut",)], dma=True,
           emit=lambda q: q.dma_start(out=lut_s[:, :], in_=lut_d.ap()[:, :]))
    dvn = [0]
    pon = [0]
    for n in range(SHARD):
        eng = "dve" if (n % 3) != 2 else "pool"
        if eng == "dve":
            slot = dvn[0] % 2
            dvn[0] += 1
        else:
            slot = 2 + pon[0] % 2
            pon[0] += 1
        s.step("sp", w=[("ib", slot)], dma=True,
               emit=lambda q, n=n, slot=slot: q.dma_start(out=ib[:, slot, :], in_=idx_d.ap()[n]))
        def chain(q, slot=slot):
            ins = [q.tensor_scalar(wb[:, slot, :], ib[:, slot, :], 0.0, lut_s[:, 0:1],
                                   ALU.is_equal, ALU.mult)]
            for j in range(1, 16):
                ins.append(q.tensor_scalar(tb[:, slot, :], ib[:, slot, :], float(j),
                                           lut_s[:, j:j + 1], ALU.is_equal, ALU.mult))
                ins.append(q.tensor_tensor(wb[:, slot, :], wb[:, slot, :], tb[:, slot, :], ALU.add))
            return ins
        s.step(eng, r=[("ib", slot), ("lut",)], w=[("wb", slot), ("tb", slot)], emit=chain)
        s.step("sp", r=[("wb", slot)], dma=True,
               emit=lambda q, n=n, slot=slot: q.dma_start(out=out_d.ap()[n], in_=wb[:, slot, :]))
    s.barrier()
    return nc


LL = [-(2.0 ** (-8.0 * (h + 1) / NH)) for h in range(NH)]
LAMC = [float(np.exp(np.float32(128.0) * np.float32(LL[h]))) for h in range(NH)]


def _mk_rope_bufs(nc, sb, pfx, pb, tag=0, shared=None):
    R = {"pb": pb, "tg": tag}
    if shared is None:
        R["cos2"] = sb(f"{pfx}_cos", [P, T], BF16)
        R["sin2"] = sb(f"{pfx}_sin", [P, T], BF16)
    else:
        R["cos2"] = shared["cos2"]
        R["sin2"] = shared["sin2"]
    R["ksil"] = sb(f"{pfx}_ksil", [P, T], BF16)
    R["t1"] = sb(f"{pfx}_t1", [P, T], BF16)
    R["t2"] = sb(f"{pfx}_t2", [P, T], BF16)
    R["t1sh"] = sb(f"{pfx}_t1sh", [64, T], BF16)
    R["t2sh"] = sb(f"{pfx}_t2sh", [64, T], BF16)
    R["obuf"] = sb(f"{pfx}_obuf", [64, T], BF16)
    return R


def _rope1_silu(nc, s, R, bank, th):
    g = R["tg"]
    s.step("act", r=[("ps", bank)], w=[("ksil", g, th)],
           emit=lambda q, th=th, b=bank: q.activation(
               out=R["ksil"][:, th * 512:(th + 1) * 512], in_=R["pb"][b][:, :], func=AF.Silu))


def _rope1_tail(nc, s, R):
    g = R["tg"]
    s.step("dve", r=[("ksil", g, 0), ("ksil", g, 1), ("cos2",)], w=[("t1", g)],
           emit=lambda q: q.tensor_tensor(R["t1"][:, :], R["ksil"][:, :], R["cos2"][:, :], ALU.mult))
    s.step("dve", r=[("ksil", g, 0), ("ksil", g, 1), ("sin2",)], w=[("t2", g)],
           emit=lambda q: q.tensor_tensor(R["t2"][:, :], R["ksil"][:, :], R["sin2"][:, :], ALU.mult))
    s.step("sp", r=[("t1", g)], w=[("t1sh", g)], dma=True,
           emit=lambda q: q.dma_start(out=R["t1sh"][:, :], in_=R["t1"][64:128, :]))
    s.step("sp", r=[("t2", g)], w=[("t2sh", g)], dma=True,
           emit=lambda q: q.dma_start(out=R["t2sh"][:, :], in_=R["t2"][64:128, :]))


def _rope1(nc, s, R, psrc0):
    for th in range(2):
        _rope1_silu(nc, s, R, psrc0 + th, th)
    _rope1_tail(nc, s, R)


def _rope2(nc, s, R, dst, dst_key):
    g = R["tg"]
    s.step("dve", r=[("t1", g), ("t2sh", g)], w=[dst_key],
           emit=lambda q: q.tensor_tensor(dst[0:64, :], R["t1"][0:64, :], R["t2sh"][:, :], ALU.subtract))
    s.step("dve", r=[("t2", g), ("t1sh", g)], w=[("obuf", g)],
           emit=lambda q: q.tensor_tensor(R["obuf"][:, :], R["t2"][0:64, :], R["t1sh"][:, :], ALU.add))
    s.step("sp", r=[("obuf", g)], w=[dst_key], dma=True,
           emit=lambda q, dst=dst: q.dma_start(out=dst[64:128, :], in_=R["obuf"][:, :]))


def _rope(nc, s, R, psrc0, dst, dst_key):
    _rope1(nc, s, R, psrc0)
    _rope2(nc, s, R, dst, dst_key)


def emit_l1pre(nc, s, pb, xnT, id_s, hd, fused=False, bg=None):
    wk_d, wv_d, cos_d, sin_d, dk_d = hd["wk"], hd["wv"], hd["cos"], hd["sin"], hd["dk"]
    krot_d, vst_d, pst_d = hd["krot_st"], hd["vst"], hd["pst"]
    with ExitStack() as ph:
        sb = lambda n, sh, dt: ph.enter_context(nc.sbuf_tensor(n, sh, dt))
        tp = [ph.enter_context(nc.psum_tensor(f"ptp{i}", [128, 1024], BF16)) for i in range(2)]
        R = _mk_rope_bufs(nc, sb, "pr", pb)
        wv_s = sb("p_wv", [P, FC, 512], BF16)
        wk_s = sb("p_wk", [P, 2, FC, 128], BF16)
        vg = sb("p_vg", [P, 8, 512], BF16)
        krot2 = sb("p_krot", [P, 2, T], BF16)
        Ssb2 = sb("p_S", [P, 2, 128], F32)
        pcb = sb("p_pcb", [P, 2, 128], BF16)
        ktdb = sb("p_ktdb", [P, 2, 128], BF16)
        dk_s = sb("p_dk", [P, NH], F32)
        if fused:
            fl2 = sb("p_fl2", [P, 1], F32)
            cbuf = sb("p_cbuf", [P, 2, 128], F32)
            s.step("sp", w=[("fl2",)], dma=True,
                   emit=lambda q: q.dma_start(out=fl2[:, :], in_=hd["flag2"].ap()[:, :]))
        s.step("sp", w=[("cos2",)], dma=True, emit=lambda q: q.dma_start(out=R["cos2"][:, :], in_=cos_d.ap()[:, :]))
        s.step("sp", w=[("sin2",)], dma=True, emit=lambda q: q.dma_start(out=R["sin2"][:, :], in_=sin_d.ap()[:, :]))
        s.step("sp", w=[("dk",)], dma=True, emit=lambda q: q.dma_start(out=dk_s[:, :], in_=dk_d.ap()[:, :]))

        def load_wv(hg):
            s.step("pool", w=[("wvw",)], dma=True,
                   emit=lambda q, hg=hg: q.dma_start(out=wv_s[:, :, :], in_=wv_d.ap()[hg]))
        def load_wk(h):
            s.step("pool", w=[("wkw", h % 2)], dma=True,
                   emit=lambda q, h=h: q.dma_start(out=wk_s[:, h % 2, :, :], in_=wk_d.ap()[h]))

        def kprep_steps(h):
            hp = h % 2
            for th in range(2):
                yield lambda th=th, hp=hp: (
                    s.step("pe", r=XNKEYS + [("wkw", hp)], w=[("ps", 4)],
                           emit=lambda q: [
                               q.matmul(pb[4][:, :], wk_s[:, hp, hc, :],
                                        xnT[:, hc, th * 512:(th + 1) * 512],
                                        start=(hc == 0), stop=(hc == 15)) for hc in range(FC)]),
                    _rope1_silu(nc, s, R, 4, th))
            yield lambda hp=hp: (_rope1_tail(nc, s, R),
                                 load_wk(h + 1) if h + 1 < 16 else None)
            def r2(hp=hp, h=h):
                _rope2(nc, s, R, krot2[:, hp, :], ("krot", hp))
                s.step("sp", r=[("krot", hp)], dma=True,
                       emit=lambda q: q.dma_start(out=krot_d.ap()[h], in_=krot2[:, hp, :]))
            yield r2
        load_wv(0)
        load_wk(0)
        for hg in range(4):
            for tc in range(8):
                bank = 4
                s.step("pe", r=XNKEYS + [("wvw",)], w=[("ps", bank)],
                       emit=lambda q, tc=tc, bank=bank: [
                           q.matmul(pb[bank][:, :], xnT[:, hc, tc * 128:(tc + 1) * 128],
                                    wv_s[:, hc, :], start=(hc == 0), stop=(hc == 15))
                           for hc in range(FC)])
                s.step("dve", r=[("ps", bank)], w=[("vg", tc)],
                       emit=lambda q, tc=tc, bank=bank: q.tensor_copy(vg[:, tc, :], pb[bank][:, :]))
                if bg:
                    bg()
                    bg()
            if hg < 3:
                load_wv(hg + 1)
            s.step("sp", r=[("vg", tc) for tc in range(8)], dma=True,
                   emit=lambda q, hg=hg: q.dma_start(
                       out=vst_d.ap()[hg].rearrange("tc p j -> p tc j"), in_=vg[:, :, :]))
            for hs in range(4):
                h = hg * 4 + hs
                hp = h % 2
                krot = krot2[:, hp, :]
                Ssb = Ssb2[:, hp, :]
                if h == 0:
                    for st in kprep_steps(0):
                        st()
                nxt = list(kprep_steps(h + 1)) if h + 1 < 16 else []
                s.step("dve", w=[("S", hp)], emit=lambda q, Ssb=Ssb: q.memset(Ssb[:, :], 0.0))
                for c in range(8):
                    if c in (1, 3, 5, 7) and nxt:
                        nxt.pop(0)()
                    cp = c % 2
                    s.step("dve", r=[("S", hp)], w=[("pcb", cp)],
                           emit=lambda q, cp=cp, Ssb=Ssb: q.tensor_copy(pcb[:, cp, :], Ssb[:, :]))
                    s.step("sp", r=[("pcb", cp)], dma=True,
                           emit=lambda q, h=h, c=c, cp=cp: q.dma_start(
                               out=pst_d.ap()[h, c], in_=pcb[:, cp, :]))
                    s.step("pe", r=[("krot", hp), ("id",)], w=[("ps", 6 + cp)],
                           emit=lambda q, c=c, cp=cp, krot=krot: q.transpose(
                               tp[cp][:, 0:128], krot[:, c * 128:(c + 1) * 128], id_s[:, :]))
                    s.step("dve", r=[("ps", 6 + cp), ("dk",)], w=[("ktdb", cp)],
                           emit=lambda q, h=h, cp=cp: q.tensor_single_scalar(
                               ktdb[:, cp, :], tp[cp][:, 0:128], dk_s[:, h:h + 1], ALU.mult))
                    s.step("pe", r=[("ktdb", cp), ("vg", c)], w=[("ps", 5)],
                           emit=lambda q, c=c, hs=hs, cp=cp: q.matmul(
                               pb[5][:, 0:128], ktdb[:, cp, :],
                               vg[:, c, hs * 128:(hs + 1) * 128], start=True, stop=True))
                    s.step("dve", r=[("ps", 5)], w=[("S", hp)],
                           emit=lambda q, h=h, cp=cp, Ssb=Ssb: q.scalar_tensor_tensor(
                               Ssb[:, :], Ssb[:, :], LAMC[h], pb[5][:, 0:128],
                               ALU.mult, ALU.add))
                    if bg:
                        bg()
                while nxt:
                    nxt.pop(0)()
                if fused:
                    s.step("dve", r=[("S", hp), ("fl2",)], w=[("cb", h % 2)],
                           emit=lambda q, h=h, Ssb=Ssb: q.tensor_single_scalar(
                               cbuf[:, h % 2, :], Ssb[:, :], fl2[:, 0:1], ALU.mult))
                    s.step("sp", r=[("cb", h % 2)], w=[("ctb", h)], dma=True,
                           emit=lambda q, h=h: q.dma_start(
                               out=hd["contrib"].ap()[h], in_=cbuf[:, h % 2, :]))
                else:
                    s.step("sp", r=[("S", hp)], dma=True,
                           emit=lambda q, h=h, Ssb=Ssb: q.dma_start(out=hd["sfin"].ap()[h], in_=Ssb[:, :]))
        if fused:
            s.step("pool", r=[("ctb", h) for h in range(NH)], w=[("rstD",)], cc=True,
                   emit=lambda q: q.collective_compute(
                       "AllReduce", ALU.add, [[0, 1], [2, 3], [4, 5], [6, 7]],
                       [hd["contrib"].ap().opt()], [hd["rst"].ap().opt()]))
        s.barrier()


def build_l1pre():
    nc = bass.Bass("TRN2", target_bir_lowering=False)
    din = lambda n, sh, dt: nc.dram_tensor(n, sh, dt, kind="ExternalInput")
    x1_d = din("x1", [T, D], F32)
    hd = {
        "wk": din("l1wk", [NH, 128, FC, 128], F32),
        "wv": din("l1wv", [4, 128, FC, 512], F32),
        "cos": din("cos2", [128, T], BF16),
        "sin": din("sin2", [128, T], BF16),
        "dk": din("dkcol", [128, NH], F32),
        "sfin": nc.dram_tensor("S_fin", [NH, 128, 128], F32, kind="ExternalOutput"),
        "krot_st": nc.dram_tensor("krot", [NH, 128, T], BF16, kind="ExternalOutput"),
        "vst": nc.dram_tensor("vst", [4, 8, 128, 512], BF16, kind="ExternalOutput"),
        "pst": nc.dram_tensor("pst", [NH, 8, 128, 128], BF16, kind="ExternalOutput"),
    }
    s, pb, xres, xnT, id_s = _mk_common(nc)
    emit_stage0(nc, s, x1_d.ap(), xres, xnT, id_s)
    emit_l1pre(nc, s, pb, xnT, id_s, hd)
    return nc


def emit_l1g1(nc, s, pb, xnT, hd):
    g1 = nc.alloc_sbuf_tensor("m_g1", [P, T], BF16)
    wg2_s = nc.alloc_sbuf_tensor("m_wg2", [P, D], BF16)
    s.step("pool", w=[("wg2",)], dma=True,
           emit=lambda q: q.dma_start(out=wg2_s[:, :], in_=hd["wg2"].ap()[:, :]))
    with ExitStack() as ph:
        wg1_s = ph.enter_context(nc.sbuf_tensor("m_wg1", [P, FC, 128], BF16))
        s.step("pool", w=[("wg1",)], dma=True,
               emit=lambda q: q.dma_start(out=wg1_s[:, :, :], in_=hd["wg1"].ap()[:, :, :]))
        for th in range(2):
            s.step("pe", r=XNKEYS + [("wg1",)], w=[("ps", th)],
                   emit=lambda q, th=th: [
                       q.matmul(pb[th][:, :], wg1_s[:, hc, :], xnT[:, hc, th * 512:(th + 1) * 512],
                                start=(hc == 0), stop=(hc == 15)) for hc in range(FC)])
            s.step("dve", r=[("ps", th)], w=[("g1",)],
                   emit=lambda q, th=th: q.tensor_copy(g1[:, th * 512:(th + 1) * 512], pb[th][:, :]))
        s.barrier()
    return g1, wg2_s


def emit_l1ffn(nc, s, xres, xnT, pb, h_st, wgath_d, wgath_key=None, bg=None):
    rk = [wgath_key] if wgath_key else []
    def wg_load(s2, ic):
        s2.step("sp", r=rk, w=[("wg", ic % 2)], dma=True,
                emit=lambda q, ic=ic: q.dma_start(
                    out=s2._wg[:, ic % 2, :, :],
                    in_=wgath_d.ap()[4 * ic:4 * ic + 4].rearrange("b p j -> p b j")))
    def wu_load(s2, ic):
        s2.step("sp", r=rk, w=[("wu", ic % 2)], dma=True,
                emit=lambda q, ic=ic: q.dma_start(
                    out=s2._wu[:, ic % 2, :, :],
                    in_=wgath_d.ap()[172 + 4 * ic:172 + 4 * ic + 4].rearrange("b p j -> p b j")))
    def wd_load(s2, ic, oc):
        s2.step("act", r=rk, w=[("wd", ic % 3)], dma=True,
                emit=lambda q, ic=ic, oc=oc: q.dma_start(
                    out=s2._wd[:, ic % 3, :], in_=wgath_d.ap()[344 + 4 * ic + oc]))
    wcfg = {
        "gu_shape": [P, 2, 4, 512],
        "gu_slice": lambda ws, slot, hc: ws[:, slot, hc // 4, (hc % 4) * 128:((hc % 4) + 1) * 128],
        "wg_load": wg_load, "wu_load": wu_load, "wd_load": wd_load,
    }
    emit_qffn(nc, s, xres, xnT, pb, h_st, NIC1, False, wcfg, pfx="q1", bg=bg)


def l1ffnA_units(nc, s, ph, xnT, pb, h_st, wgath_d):
    sb = lambda n, sh, dt: ph.enter_context(nc.sbuf_tensor(n, sh, dt))
    wg_s = sb("q1_wg", [P, 2, 4, 512], BF16)
    wu_s = sb("q1_wu", [P, 2, 4, 512], BF16)
    h_s = sb("q1_h", [P, 3, T], BF16)
    sg = sb("q1_sgb", [P, 2, 512], BF16)
    su = sb("q1_sub", [P, 2, 512], BF16)
    gsl = lambda ws, slot, hc: ws[:, slot, hc // 4, (hc % 4) * 128:((hc % 4) + 1) * 128]
    def wg_load(ic):
        s.step("pool", r=[("wgath",)], w=[("wg", ic % 2)], dma=True,
               emit=lambda q, ic=ic: q.dma_start(
                   out=wg_s[:, ic % 2, :, :],
                   in_=wgath_d.ap()[4 * ic:4 * ic + 4].rearrange("b p j -> p b j")))
    def wu_load(ic):
        s.step("pool", r=[("wgath",)], w=[("wu", ic % 2)], dma=True,
               emit=lambda q, ic=ic: q.dma_start(
                   out=wu_s[:, ic % 2, :, :],
                   in_=wgath_d.ap()[172 + 4 * ic:172 + 4 * ic + 4].rearrange("b p j -> p b j")))
    wg_load(0)
    wu_load(0)
    def mmgu(q, ws, ic, th, bank):
        return [q.matmul(pb[bank][:, :], gsl(ws, ic % 2, hc),
                         xnT[:, hc, th * 512:(th + 1) * 512],
                         start=(hc == 0), stop=(hc == 15)) for hc in range(FC)]
    def u_ep(th, ic):
        s.step("act", r=[("ps", th)], w=[("sg", th)],
               emit=lambda q, th=th: q.activation(
                   out=sg[:, th, :], in_=pb[th][:, :], func=AF.Silu))
        s.step("act", r=[("ps", 2 + th)], w=[("su", th)],
               emit=lambda q, th=th: q.activation(
                   out=su[:, th, :], in_=pb[2 + th][:, :], func=AF.Copy))
        s.step("dve", r=[("sg", th), ("su", th)], w=[("h", ic % 3)],
               emit=lambda q, ic=ic, th=th: q.tensor_tensor(
                   h_s[:, ic % 3, th * 512:(th + 1) * 512],
                   sg[:, th, :], su[:, th, :], ALU.mult))
        if th == 1:
            s.step("sp", r=[("h", ic % 3)], w=[("hst", ic)], dma=True,
                   emit=lambda q, ic=ic: q.dma_start(out=h_st.ap()[ic], in_=h_s[:, ic % 3, :]))
    def gen():
        for ic in range(NIC1):
            def u_g(ic=ic):
                for th in range(2):
                    s.step("pe", r=XNKEYS + [("wg", ic % 2)], w=[("ps", th)],
                           emit=lambda q, th=th, ic=ic: mmgu(q, wg_s, ic, th, th))
                if ic + 1 < NIC1:
                    wg_load(ic + 1)
            yield u_g
            def u_u(ic=ic):
                for th in range(2):
                    s.step("pe", r=XNKEYS + [("wu", ic % 2)], w=[("ps", 2 + th)],
                           emit=lambda q, th=th, ic=ic: mmgu(q, wu_s, ic, th, 2 + th))
                if ic + 1 < NIC1:
                    wu_load(ic + 1)
            yield u_u
            yield lambda ic=ic: u_ep(0, ic)
            yield lambda ic=ic: u_ep(1, ic)
    return gen()


def emit_l1ffnB(nc, s, xres, pb, h_st, wgath_d):
    with ExitStack() as ph:
        sb = lambda n, sh, dt: ph.enter_context(nc.sbuf_tensor(n, sh, dt))
        pext = [ph.enter_context(nc.psum_tensor(f"q1px{i}", [128, 512], F32)) for i in range(2)]
        pb8 = list(pb) + pext
        wd_s = sb("q1_wd", [P, 3, 512], BF16)
        hl = sb("q1_hl", [P, 3, T], BF16)
        for oc in range(4):
            for ic in range(NIC1):
                s.step("sp", r=[("hst", ic)], w=[("hl", ic % 3)], dma=True,
                       emit=lambda q, ic=ic: q.dma_start(out=hl[:, ic % 3, :], in_=h_st.ap()[ic]))
                s.step("act", r=[("wgath",)], w=[("wd", ic % 3)], dma=True,
                       emit=lambda q, ic=ic, oc=oc: q.dma_start(
                           out=wd_s[:, ic % 3, :], in_=wgath_d.ap()[344 + 4 * ic + oc]))
                s.step("pe", r=[("hl", ic % 3), ("wd", ic % 3)],
                       w=[("ps", tt2) for tt2 in range(8)],
                       emit=lambda q, ic=ic: [
                           q.matmul(pb8[tt2][:, :], hl[:, ic % 3, tt2 * 128:(tt2 + 1) * 128],
                                    wd_s[:, ic % 3, :], start=(ic == 0), stop=(ic == NIC1 - 1))
                           for tt2 in range(8)])
            for tt in range(8):
                s.step("dve", r=[("ps", tt)], w=[("xres", tt)],
                       emit=lambda q, tt=tt, oc=oc: q.tensor_tensor(
                           xres[:, tt, oc * 512:(oc + 1) * 512],
                           xres[:, tt, oc * 512:(oc + 1) * 512], pb8[tt][:, :], ALU.add))
        s.barrier()


def emit_l1attn(nc, s, pb, xres, xnT, id_s, g1, wg2_s, hd, fused=False):
    krot_d, vst_d, pst_d = hd["krot_st"], hd["vst"], hd["pst"]
    wq_d, wo_d, nw_d = hd["wq"], hd["wo"], hd["nw"]
    cos_d, sin_d, dq_d, D_d, lf_d = hd["cos"], hd["sin"], hd["dq"], hd["Dm"], hd["lf"]
    xg_st, oft_st = hd["xg_st"], hd["oft_st"]
    mscols = nc.alloc_sbuf_tensor("m_ms", [P, 128], F32)
    rstd = nc.alloc_sbuf_tensor("m_rstd", [P, 128], F32)
    with ExitStack() as ph:
        sb = lambda n, sh, dt: ph.enter_context(nc.sbuf_tensor(n, sh, dt))
        tp = [ph.enter_context(nc.psum_tensor(f"mtp{i}", [128, 1024], BF16)) for i in range(2)]
        R0 = _mk_rope_bufs(nc, sb, "mr0", pb, tag=0)
        R1 = _mk_rope_bufs(nc, sb, "mr1", pb, tag=1, shared=R0)
        R = R0
        wq_s = sb("m_wq", [P, 2, FC, 128], BF16)
        qrot2 = sb("m_qrot", [P, 2, T], BF16)
        krot2 = sb("m_krot", [P, 4, T], BF16)
        vg = sb("m_vg", [P, 2, 8, 512], BF16)
        pstb2 = sb("m_pst", [P, 4, 8, 128], BF16)
        Rp = sb("m_Rp", [P, NH, 128], BF16)
        D_sb = sb("m_D", [P, NH, 128], BF16)
        dq_sb = sb("m_dq", [P, NH, 128], BF16)
        lf_s = sb("m_lf", [P, 128], F32)
        CPD = 4
        qt = sb("m_qt", [P, CPD, 128], BF16)
        suseb = sb("m_suseb", [P, CPD, 128], BF16)
        scd = sb("m_scd", [P, CPD, 128], BF16)
        scdT = sb("m_scdT", [P, CPD, 128], BF16)
        sgf = sb("m_sgf", [P, T], F32)
        sgb = sb("m_sgb", [P, 2, T], BF16)
        sqt = sb("m_sqt", [P, CPD, 128], F32)
        xgh = sb("m_xgh", [P, 2, 8, 128], BF16)

        s.step("sp", w=[("cos2",)], dma=True, emit=lambda q: q.dma_start(out=R["cos2"][:, :], in_=cos_d.ap()[:, :]))
        s.step("sp", w=[("sin2",)], dma=True, emit=lambda q: q.dma_start(out=R["sin2"][:, :], in_=sin_d.ap()[:, :]))
        rsrc = hd["rst"] if fused else hd["R"]
        s.step("pool", r=([("rstD",)] if fused else []), w=[("Rp",)], dma=True,
               emit=lambda q: q.dma_start(out=Rp[:, :, :], in_=rsrc.ap().rearrange("h p j -> p h j")))
        s.step("sp", w=[("Ds",)], dma=True,
               emit=lambda q: q.dma_start(out=D_sb[:, :, :], in_=D_d.ap().rearrange("h p j -> p h j")))
        s.step("sp", w=[("dq",)], dma=True,
               emit=lambda q: q.dma_start(out=dq_sb[:, :, :], in_=dq_d.ap().rearrange("h p j -> p h j")))
        s.step("sp", w=[("lf",)], dma=True, emit=lambda q: q.dma_start(out=lf_s[:, :], in_=lf_d.ap()[:, :]))

        def load_wq(h):
            s.step("pool", w=[("wqw", h % 2)], dma=True,
                   emit=lambda q, h=h: q.dma_start(out=wq_s[:, h % 2, :, :], in_=wq_d.ap()[h]))

        def head_prep_steps(h):
            hp = h % 2
            for th in range(2):
                yield lambda th=th, hp=hp: s.step(
                    "pe", r=XNKEYS + [("wqw", hp)], w=[("ps", th)],
                    emit=lambda q: [
                        q.matmul(pb[th][:, :], wq_s[:, hp, hc, :],
                                 xnT[:, hc, th * 512:(th + 1) * 512],
                                 start=(hc == 0), stop=(hc == 15)) for hc in range(FC)])
            Rh = R0 if h % 2 == 0 else R1
            yield lambda hp=hp, Rh=Rh: (_rope1(nc, s, Rh, 0),
                                        load_wq(h + 1) if h + 1 < 16 else None)
            yield lambda hp=hp, Rh=Rh: _rope2(nc, s, Rh, qrot2[:, hp, :], ("qrot", hp))
            for b in range(2):
                yield lambda b=b, h=h: s.step(
                    "pe", r=[("g1",), ("wg2",)], w=[("ps", b)],
                    emit=lambda q: [
                        q.matmul(pb[b][:, i * 128:(i + 1) * 128],
                                 g1[:, (b * 4 + i) * 128:(b * 4 + i + 1) * 128],
                                 wg2_s[:, h * 128:(h + 1) * 128],
                                 start=(i == 0), stop=(i == 3)) for i in range(4)])
            def tanhts(hp=hp):
                for b in range(2):
                    s.step("act", r=[("ps", b)], w=[("sgf",)],
                           emit=lambda q, b=b: q.activation(
                               out=sgf[:, b * 512:(b + 1) * 512], in_=pb[b][:, :],
                               func=AF.Tanh, scale=0.5))
                s.step("dve", r=[("sgf",)], w=[("sgb", hp)],
                       emit=lambda q: q.tensor_scalar(
                           sgb[:, hp, :], sgf[:, :], 0.5, 0.5, ALU.mult, ALU.add))
            yield tanhts
        load_wq(0)
        def make_stages(h, hs):
            hp = h % 2
            h4 = h % 4
            g2 = (h // 4) % 2
            qrot = qrot2[:, hp, :]
            krot = krot2[:, h4, :]
            sb_b = 2 + hp
            ob = 4 + hp
            tpb = hp
            st = {}
            def sQ(c):
                cp = hp * 2 + c % 2
                s.step("dve", r=[("qrot", hp), ("dq",)], w=[("qt", cp)],
                       emit=lambda q: q.tensor_tensor(
                           qt[:, cp, :], qrot[:, c * 128:(c + 1) * 128], dq_sb[:, h, :], ALU.mult))
                s.step("dve", r=[("Rp",), ("lf",), ("pstb", h4)], w=[("suseb", cp)],
                       emit=lambda q: q.scalar_tensor_tensor(
                           suseb[:, cp, :], Rp[:, h, :], lf_s[:, h * 8 + c:h * 8 + c + 1],
                           pstb2[:, h4, c, :], ALU.mult, ALU.add))
            def sS(c):
                s.step("pe", r=[("qrot", hp), ("krot", h4)], w=[("ps", sb_b)],
                       emit=lambda q: q.matmul(
                           pb[sb_b][:, 0:128], qrot[:, c * 128:(c + 1) * 128],
                           krot[:, c * 128:(c + 1) * 128], start=True, stop=True))
            def sD(c):
                cp = hp * 2 + c % 2
                s.step("dve", r=[("ps", sb_b), ("Ds",)], w=[("scd", cp)],
                       emit=lambda q: q.tensor_tensor(
                           scd[:, cp, :], pb[sb_b][:, 0:128], D_sb[:, h, :], ALU.mult))
            def sT(c):
                cp = hp * 2 + c % 2
                s.step("pe", r=[("scd", cp), ("id",)], w=[("ps", 6 + tpb)],
                       emit=lambda q: q.transpose(tp[tpb][:, 0:128], scd[:, cp, :], id_s[:, :]))
                s.step("act", r=[("ps", 6 + tpb)], w=[("scdT", cp)],
                       emit=lambda q: q.copy(scdT[:, cp, :], tp[tpb][:, 0:128]))
            def sAV(c):
                cp = hp * 2 + c % 2
                s.step("pe", r=[("scdT", cp), ("vg", g2), ("qt", cp), ("suseb", cp)],
                       w=[("ps", ob)],
                       emit=lambda q: [
                           q.matmul(pb[ob][:, 0:128], scdT[:, cp, :],
                                    vg[:, g2, c, hs * 128:(hs + 1) * 128], start=True, stop=False),
                           q.matmul(pb[ob][:, 0:128], qt[:, cp, :], suseb[:, cp, :],
                                    start=False, stop=True)])
            def sX(c):
                cp = hp * 2 + c % 2
                s.step("dve", r=[("ps", ob), ("sgb", hp)], w=[("xgh", hp, c)],
                       emit=lambda q: q.tensor_tensor(
                           xgh[:, hp, c, :], pb[ob][:, 0:128],
                           sgb[:, hp, c * 128:(c + 1) * 128], ALU.mult))
                s.step("act", r=[("xgh", hp, c)], w=[("sqt", cp), ("ms",)],
                       emit=lambda q: q.activation(
                           out=sqt[:, cp, :], in_=xgh[:, hp, c, :], func=AF.Square,
                           accum_out=mscols[:, h * 8 + c:h * 8 + c + 1]))
            st.update(sQ=sQ, sS=sS, sD=sD, sT=sT, sAV=sAV, sX=sX)
            return st

        def load_vg(hg):
            s.step("sp", w=[("vg", hg % 2)], dma=True,
                   emit=lambda q, hg=hg: q.dma_start(
                       out=vg[:, hg % 2, :, :], in_=vst_d.ap()[hg].rearrange("tc p j -> p tc j")))
        def load_kp(h):
            h4 = h % 4
            s.step("sp", w=[("krot", h4)], dma=True,
                   emit=lambda q, h=h, h4=h4: q.dma_start(out=krot2[:, h4, :], in_=krot_d.ap()[h]))
            s.step("sp", w=[("pstb", h4)], dma=True,
                   emit=lambda q, h=h, h4=h4: q.dma_start(
                       out=pstb2[:, h4, :, :], in_=pst_d.ap()[h].rearrange("c p j -> p c j")))
        for hg in range(4):
            load_vg(hg)
            for pr in range(2):
                h0 = hg * 4 + 2 * pr
                h1 = h0 + 1
                hs0, hs1 = 2 * pr, 2 * pr + 1
                load_kp(h0)
                load_kp(h1)
                if h0 == 0:
                    for st in head_prep_steps(0):
                        st()
                    for st in head_prep_steps(1):
                        st()
                nxt = []
                if h0 + 2 < 16:
                    u2 = list(head_prep_steps(h0 + 2))
                    u3 = list(head_prep_steps(h0 + 3)) if h0 + 3 < 16 else []
                    while u2 or u3:
                        if u2:
                            nxt.append(u2.pop(0))
                        if u3:
                            nxt.append(u3.pop(0))
                st0 = make_stages(h0, hs0)
                st1 = make_stages(h1, hs1)
                for i in range(10):
                    for st in (st0, st1):
                        if i - 2 >= 0:
                            st["sX"](i - 2)
                        if i < 8:
                            st["sQ"](i)
                            st["sS"](i)
                        if 1 <= i <= 8:
                            st["sD"](i - 1)
                            st["sT"](i - 1)
                            st["sAV"](i - 1)
                    if 1 <= i <= 7 and nxt:
                        nxt.pop(0)()
                        if nxt:
                            nxt.pop(0)()
                while nxt:
                    nxt.pop(0)()
                for h in (h0, h1):
                    hp = h % 2
                    s.step("sp", r=[("xgh", hp, c) for c in range(8)], w=[("xgst", h)], dma=True,
                           emit=lambda q, h=h, hp=hp: q.dma_start(
                               out=xg_st.ap()[h].rearrange("c p j -> p c j"), in_=xgh[:, hp, :, :]))
        s.barrier()
    s.step("act", r=[("ms",), ("cst",)], w=[("rstd",)],
           emit=lambda q: q.activation(out=rstd[:, :], in_=mscols[:, :], func=AF.Sqrt,
                                       scale=1.0 / 128.0, bias=nc._cst[:, 1:2]))
    s.step("dve", r=[("rstd",)], w=[("rstd",)],
           emit=lambda q: q.reciprocal(rstd[:, :], rstd[:, :]))
    with ExitStack() as ph:
        sb = lambda n, sh, dt: ph.enter_context(nc.sbuf_tensor(n, sh, dt))
        tp = [ph.enter_context(nc.psum_tensor(f"p2tp{i}", [128, 1024], BF16)) for i in range(2)]
        nw_s = sb("m_nw", [P, D], BF16)
        ofc = sb("m_ofc", [P, 4, 128], BF16)
        s.step("pool", w=[("nw",)], dma=True,
               emit=lambda q: q.dma_start(out=nw_s[:, :], in_=nw_d.ap()[:, :]))
        xgh2b = ph.enter_context(nc.sbuf_tensor("m_xgh2b", [P, 2, 8, 128], BF16))
        ofthb = ph.enter_context(nc.sbuf_tensor("m_ofthb", [P, 2, T], BF16))
        def p2_stt(h, c):
            hp, cp = h % 2, c % 4
            s.step("dve", r=[("xgh2", hp), ("rstd",), ("nw",)], w=[("ofc", cp)],
                   emit=lambda q: q.scalar_tensor_tensor(
                       ofc[:, cp, :], xgh2b[:, hp, c, :], rstd[:, h * 8 + c:h * 8 + c + 1],
                       nw_s[:, h * 128:(h + 1) * 128], ALU.mult, ALU.mult))
        def p2_tr(h, c):
            hp, cp = h % 2, c % 4
            s.step("pe", r=[("ofc", cp), ("id",)], w=[("ps", 6 + c % 2)],
                   emit=lambda q: q.transpose(tp[c % 2][:, 0:128], ofc[:, cp, :], id_s[:, :]))
            s.step("dve", r=[("ps", 6 + c % 2)], w=[("ofth", hp)],
                   emit=lambda q: q.tensor_copy(
                       ofthb[:, hp, c * 128:(c + 1) * 128], tp[c % 2][:, 0:128]))
        for h in range(NH):
            hp = h % 2
            s.step("sp", r=[("xgst", h)], w=[("xgh2", hp)], dma=True,
                   emit=lambda q, h=h, hp=hp: q.dma_start(
                       out=xgh2b[:, hp, :, :], in_=xg_st.ap()[h].rearrange("c p j -> p c j")))
            for i in range(9):
                if i < 8:
                    p2_stt(h, i)
                if i >= 1:
                    p2_tr(h, i - 1)
            s.step("sp", r=[("ofth", hp)], w=[("oft", h)], dma=True,
                   emit=lambda q, h=h, hp=hp: q.dma_start(out=oft_st.ap()[h], in_=ofthb[:, hp, :]))
        s.barrier()
    with ExitStack() as ph:
        sb = lambda n, sh, dt: ph.enter_context(nc.sbuf_tensor(n, sh, dt))
        wo_s = sb("m_wo", [P, 2, FC, 512], BF16)
        otile = sb("m_otile", [P, 2, FC, 128], BF16)
        oftk = [("oft", h) for h in range(NH)]
        for ps2 in range(2):
            for o2 in range(2):
                s.step("pool", w=[("wow", o2)], dma=True,
                       emit=lambda q, ps2=ps2, o2=o2: q.dma_start(
                           out=wo_s[:, o2, :, :], in_=wo_d.ap()[ps2 * 2 + o2]))
            for tt in range(8):
                s.step("sp", r=oftk, w=[("otile", tt % 2)], dma=True,
                       emit=lambda q, tt=tt: q.dma_start(
                           out=otile[:, tt % 2, :, :],
                           in_=oft_st.ap()[:, :, tt * 128:(tt + 1) * 128].rearrange("h d t -> d h t")))
                for o2 in range(2):
                    bank = 2 * (tt % 2) + o2
                    s.step("pe", r=[("otile", tt % 2), ("wow", o2)], w=[("ps", bank)],
                           emit=lambda q, tt=tt, o2=o2, bank=bank: [
                               q.matmul(pb[bank][:, :], otile[:, tt % 2, fc, :],
                                        wo_s[:, o2, fc, :], start=(fc == 0), stop=(fc == 15))
                               for fc in range(FC)])
                    s.step("dve", r=[("ps", bank)], w=[("xres", tt)],
                           emit=lambda q, tt=tt, ps2=ps2, o2=o2, bank=bank: q.tensor_tensor(
                               xres[:, tt, (ps2 * 2 + o2) * 512:(ps2 * 2 + o2 + 1) * 512],
                               xres[:, tt, (ps2 * 2 + o2) * 512:(ps2 * 2 + o2 + 1) * 512],
                               pb[bank][:, :], ALU.add))
        s.barrier()


def build_l1main():
    nc = bass.Bass("TRN2", target_bir_lowering=False)
    din = lambda n, sh, dt: nc.dram_tensor(n, sh, dt, kind="ExternalInput")
    x1_d = din("x1", [T, D], F32)
    hd = {
        "krot_st": din("krot", [NH, 128, T], BF16),
        "vst": din("vst", [4, 8, 128, 512], BF16),
        "pst": din("pst", [NH, 8, 128, 128], BF16),
        "R": din("Rstate", [NH, 128, 128], F32),
        "wq": din("l1wq", [NH, 128, FC, 128], F32),
        "wo": din("l1wo", [4, 128, FC, 512], F32),
        "wg1": din("wg1_s", [128, FC, 128], F32),
        "wg2": din("wg2_s", [128, D], F32),
        "nw": din("nwrep", [128, D], F32),
        "cos": din("cos2", [128, T], BF16),
        "sin": din("sin2", [128, T], BF16),
        "dq": din("dqrep", [NH, 128, 128], BF16),
        "Dm": din("D_s", [NH, 128, 128], BF16),
        "lf": din("lamflag", [128, 128], F32),
    }
    wgath_d = din("w_gath", [8 * SHARD, 128, 512], BF16)
    xout = nc.dram_tensor("xout", [T, D], F32, kind="ExternalOutput")
    h_st = nc.dram_tensor("h_st", [NIC1, 128, T], BF16)
    hd["xg_st"] = nc.dram_tensor("xg_st", [NH, 8, 128, 128], BF16)
    hd["oft_st"] = nc.dram_tensor("oft_st", [NH, 128, T], BF16)

    s, pb, xres, xnT, id_s = _mk_common(nc)
    emit_stage0(nc, s, x1_d.ap(), xres, xnT, id_s)
    g1, wg2_s = emit_l1g1(nc, s, pb, xnT, hd)
    emit_l1ffn(nc, s, xres, xnT, pb, h_st, wgath_d)
    emit_l1attn(nc, s, pb, xres, xnT, id_s, g1, wg2_s, hd)
    _store_x(nc, s, xres, xout)
    return nc


class Deq:
    def __init__(self, nc, s, idx_d, lut_d, wsh_d, wgath_d, sb=None):
        self.nc, self.s = nc, s
        self.idx_d, self.wsh_d, self.wgath_d = idx_d, wsh_d, wgath_d
        if sb is None:
            sb = lambda n, sh, dt: nc.alloc_sbuf_tensor(n, sh, dt)
        self.lut_s = sb("dq_lut", [P, 16], F32)
        self.ib = sb("dq_ib", [P, 2, 512], I32)
        self.wb = sb("dq_wb", [P, 512], BF16)
        self.tb = sb("dq_tb", [P, 512], BF16)
        self.n = 0
        self.piece = 0
        self.triggered = False
        s.step("sp", w=[("dqlut",)], dma=True,
               emit=lambda q: q.dma_start(out=self.lut_s[:, :], in_=lut_d.ap()[:, :]))
        s.step("sp", w=[("dqib", 0)], dma=True,
               emit=lambda q: q.dma_start(out=self.ib[:, 0, :], in_=idx_d.ap()[0]))

    def bg(self):
        if self.n >= SHARD:
            return
        n = self.n
        self.n += 1
        s, ib, wb, tb, lut_s = self.s, self.ib, self.wb, self.tb, self.lut_s
        slot = n % 2
        def chain(q, slot=slot):
            ins = [q.tensor_scalar(wb[:, :], ib[:, slot, :], 0.0, lut_s[:, 0:1],
                                   ALU.is_equal, ALU.mult)]
            for j in range(1, 16):
                ins.append(q.tensor_scalar(tb[:, :], ib[:, slot, :], float(j),
                                           lut_s[:, j:j + 1], ALU.is_equal, ALU.mult))
                ins.append(q.tensor_tensor(wb[:, :], wb[:, :], tb[:, :], ALU.add))
            return ins
        s.step("dve", r=[("dqib", slot), ("dqlut",)], w=[("dqwb",), ("dqtb",)], emit=chain)
        if n + 1 < SHARD:
            s.step("sp", w=[("dqib", (n + 1) % 2)], dma=True,
                   emit=lambda q, n=n: q.dma_start(out=self.ib[:, (n + 1) % 2, :],
                                                   in_=self.idx_d.ap()[n + 1]))
        s.step("sp", r=[("dqwb",)], w=[("dqst", n)], dma=True,
               emit=lambda q, n=n: q.dma_start(out=self.wsh_d.ap()[n], in_=wb[:, :]))
        if self.n >= SHARD:
            self.trigger()

    def trigger(self):
        if self.triggered:
            return
        self.triggered = True
        self.s.step("pool", r=[("dqst", n) for n in range(SHARD)], w=[("wgath",)], cc=True,
                    emit=lambda q: q.collective_compute(
                        "AllGather", ALU.bypass, [[0, 1, 2, 3, 4, 5, 6, 7]],
                        [self.wsh_d.ap().opt()], [self.wgath_d.ap().opt()]))

    def finish(self):
        while self.n < SHARD:
            self.bg()
        self.trigger()


def build_fused():
    nc = bass.Bass("TRN2", target_bir_lowering=False)
    din = lambda n, sh, dt: nc.dram_tensor(n, sh, dt, kind="ExternalInput")
    x_own = din("x_own", [T, D], F32)
    x_halo = din("x_halo", [512, D], F32)
    dram0 = {
        "wq": din("wq_h", [NH, 128, FC, 128], F32),
        "wk": din("wk_h", [NH, 128, FC, 128], F32),
        "wv": din("wv_g", [4, 128, FC, 512], F32),
        "wo": din("wo_oc", [4, 128, FC, 512], F32),
        "masks": din("masks", [8, 128, 640], BF16),
    }
    gT_d = din("gT_h", [NIC0, 128, FC, 128], U8)
    uT_d = din("uT_h", [NIC0, 128, FC, 128], U8)
    dT_d = din("dT_h", [NIC0, 4, 128, 512], U8)
    gs_d = din("gs_c", [128, NIC0], F32)
    us_d = din("us_c", [128, NIC0], F32)
    dsrep_d = din("dsrep", [128, D], F32)
    idx_d = din("l1idx_shard", [SHARD, 128, 512], I32)
    lut_d = din("lutrep", [128, 16], F32)
    hd = {
        "wk": din("l1wk", [NH, 128, FC, 128], F32),
        "wv": din("l1wv", [4, 128, FC, 512], F32),
        "wq": din("l1wq", [NH, 128, FC, 128], F32),
        "wo": din("l1wo", [4, 128, FC, 512], F32),
        "wg1": din("wg1_s", [128, FC, 128], F32),
        "wg2": din("wg2_s", [128, D], F32),
        "nw": din("nwrep", [128, D], F32),
        "cos": din("cos2", [128, T], BF16),
        "sin": din("sin2", [128, T], BF16),
        "dk": din("dkcol", [128, NH], F32),
        "dq": din("dqrep", [NH, 128, 128], BF16),
        "Dm": din("D_s", [NH, 128, 128], BF16),
        "lf": din("lamflag", [128, 128], F32),
        "flag2": din("flag2c", [128, 1], F32),
    }
    xout = nc.dram_tensor("xout", [T, D], F32, kind="ExternalOutput")
    attn_st = nc.dram_tensor("attn_st", [NH, 128, T], BF16)
    h_st = nc.dram_tensor("h_st", [NIC0, 128, T], BF16)
    wsh_d = nc.dram_tensor("wshard", [SHARD, 128, 512], BF16)
    wgath_d = nc.dram_tensor("w_gath", [8 * SHARD, 128, 512], BF16, addr_space="Shared")
    hd["krot_st"] = nc.dram_tensor("krot", [NH, 128, T], BF16)
    hd["vst"] = nc.dram_tensor("vst", [4, 8, 128, 512], BF16)
    hd["pst"] = nc.dram_tensor("pst", [NH, 8, 128, 128], BF16)
    hd["contrib"] = nc.dram_tensor("contrib", [NH, 128, 128], F32)
    hd["rst"] = nc.dram_tensor("rst", [NH, 128, 128], F32)
    hd["xg_st"] = nc.dram_tensor("xg_st", [NH, 8, 128, 128], BF16)
    hd["oft_st"] = nc.dram_tensor("oft_st", [NH, 128, T], BF16)

    s, pb, xres, xnT, id_s = _mk_common(nc)
    phD = ExitStack()
    sbD = lambda n, sh, dt: phD.enter_context(nc.sbuf_tensor(n, sh, dt))
    deq = Deq(nc, s, idx_d, lut_d, wsh_d, wgath_d, sbD)
    with ExitStack() as ph0:
        haloT = ph0.enter_context(nc.sbuf_tensor("haloT", [P, FC, 512], BF16))
        emit_stage0(nc, s, x_own.ap(), xres, xnT, id_s, x_halo.ap(), haloT)
        emit_l0_attn(nc, s, xres, xnT, haloT, id_s, pb, attn_st, dram0)
    def wg_load(s2, ic):
        s2.step("pool", w=[("wg", ic % 2)], dma=True,
                emit=lambda q, ic=ic: q.dma_start(out=s2._wg[:, ic % 2, :, :], in_=gT_d.ap()[ic]))
    def wu_load(s2, ic):
        s2.step("pool", w=[("wu", ic % 2)], dma=True,
                emit=lambda q, ic=ic: q.dma_start(out=s2._wu[:, ic % 2, :, :], in_=uT_d.ap()[ic]))
    def wd_load(s2, ic, oc):
        s2.step("act", w=[("wdr", ic % 3)], dma=True,
                emit=lambda q, ic=ic, oc=oc: q.dma_start(out=s2._wdr[:, ic % 3, :], in_=dT_d.ap()[ic, oc]))
        s2.step("dve", r=[("wdr", ic % 3)], w=[("wd", ic % 3)],
                emit=lambda q, ic=ic: q.tensor_copy(s2._wd[:, ic % 3, :], s2._wdr[:, ic % 3, :]))
    wcfg = {
        "gu_shape": [P, 2, FC, 128],
        "gu_slice": lambda ws, slot, hc: ws[:, slot, hc, :],
        "wg_load": wg_load, "wu_load": wu_load, "wd_load": wd_load,
        "gs": gs_d, "us": us_d, "dsrep": dsrep_d, "wd_raw": True,
    }
    emit_qffn(nc, s, xres, xnT, pb, h_st, NIC0, True, wcfg, pfx="q0", bg=deq.bg,
              after_a=deq.finish)
    phD.close()
    emit_stage0(nc, s, None, xres, xnT, id_s, pfx="l2", load=False)
    g1, wg2_s = emit_l1g1(nc, s, pb, xnT, hd)
    with ExitStack() as phA:
        gen = l1ffnA_units(nc, s, phA, xnT, pb, h_st, wgath_d)
        def pump():
            try:
                next(gen)()
            except StopIteration:
                pass
        emit_l1pre(nc, s, pb, xnT, id_s, hd, fused=True, bg=pump)
        for u in gen:
            u()
    emit_l1ffnB(nc, s, xres, pb, h_st, wgath_d)
    emit_l1attn(nc, s, pb, xres, xnT, id_s, g1, wg2_s, hd, fused=True)
    _store_x(nc, s, xres, xout)
    return nc


FUSED = True


def host_prepare(inp):
    H = {}
    f32 = np.float32
    cc = np.ascontiguousarray
    wqkv = np.asarray(inp["L0_attn_w_qkv"], f32).reshape(3, NH, 128, FC, 128)
    H["wq_h"] = cc(wqkv[0].transpose(0, 3, 2, 1))
    H["wk_h"] = cc(wqkv[1].transpose(0, 3, 2, 1))
    H["wv_g"] = cc(wqkv[2].reshape(4, 4, 128, FC, 128).transpose(0, 4, 3, 1, 2)
                   .reshape(4, 128, FC, 512))
    H["wo_oc"] = cc(np.asarray(inp["L0_attn_w_out"], f32).reshape(4, 512, FC, 128)
                    .transpose(0, 3, 2, 1))
    H["gT_h"] = cc(np.asarray(inp["L0_ffn_gate_w"], np.int32).astype(np.uint8)
                   .reshape(NIC0, 128, FC, 128).transpose(0, 3, 2, 1))
    H["uT_h"] = cc(np.asarray(inp["L0_ffn_up_w"], np.int32).astype(np.uint8)
                   .reshape(NIC0, 128, FC, 128).transpose(0, 3, 2, 1))
    H["dT_h"] = cc(np.asarray(inp["L0_ffn_down_w"], np.int32).astype(np.uint8)
                   .reshape(4, 512, NIC0, 128).transpose(2, 0, 3, 1))
    H["gs_c"] = cc(np.asarray(inp["L0_ffn_gate_scale"], f32).reshape(NIC0, 128).T)
    H["us_c"] = cc(np.asarray(inp["L0_ffn_up_scale"], f32).reshape(NIC0, 128).T)
    H["dsrep"] = cc(np.broadcast_to(np.asarray(inp["L0_ffn_down_scale"], f32)[None, :], (128, D)))
    H["l1wq"] = cc(np.asarray(inp["L1_attn_Wq"], f32).reshape(FC, 128, NH, 128).transpose(2, 1, 0, 3))
    H["l1wk"] = cc(np.asarray(inp["L1_attn_Wk"], f32).reshape(FC, 128, NH, 128).transpose(2, 1, 0, 3))
    H["l1wv"] = cc(np.asarray(inp["L1_attn_Wv"], f32).reshape(FC, 128, 4, 512).transpose(2, 1, 0, 3))
    H["l1wo"] = cc(np.asarray(inp["L1_attn_Wo"], f32).reshape(FC, 128, 4, 512).transpose(2, 1, 0, 3))
    H["wg1_s"] = cc(np.asarray(inp["L1_attn_Wg1"], f32).reshape(FC, 128, 128).transpose(1, 0, 2))
    H["wg2_s"] = cc(np.asarray(inp["L1_attn_Wg2"], f32))
    H["nwrep"] = cc(np.broadcast_to(np.asarray(inp["L1_attn_norm_weight"], f32)[None, :], (128, D)))
    H["lutrep"] = cc(np.broadcast_to(np.asarray(inp["L1_ffn_lut"], f32)[None, :], (128, 16)))

    ll = -(2.0 ** (-8.0 * (np.arange(NH, dtype=np.float64) + 1.0) / NH))
    i = np.arange(128, dtype=np.float64)
    H["dkcol"] = np.exp(ll[None, :] * (127.0 - i[:, None])).astype(f32)
    dq = np.exp(ll[:, None] * (i[None, :] + 1.0)).astype(f32)
    H["dqrep"] = cc(np.broadcast_to(dq[:, None, :], (NH, 128, 128))).astype(BF)
    diff = i[:, None] - i[None, :]
    Dm = np.where(diff[None, :, :] >= 0, np.exp(ll[:, None, None] * diff[None, :, :]), 0.0)
    H["D_s"] = Dm.astype(f32).astype(BF)

    gidxT = np.asarray(inp["L1_ffn_gate_idx"], np.int32).T
    uidxT = np.asarray(inp["L1_ffn_up_idx"], np.int32).T
    didxT = np.asarray(inp["L1_ffn_down_idx"], np.int32).T
    def tile_gu(m):
        return m.reshape(FC, 128, NIC1, 128).transpose(2, 1, 0, 3).reshape(NIC1, 128, D)
    gt, ut = tile_gu(gidxT), tile_gu(uidxT)
    dt = didxT.reshape(NIC1, 128, 4, 512).transpose(0, 2, 1, 3)
    blocks = np.zeros((8 * SHARD, 128, 512), np.int32)
    b = 0
    for src in (gt, ut):
        for ic in range(NIC1):
            for jb in range(4):
                blocks[b] = src[ic, :, jb * 512:(jb + 1) * 512]
                b += 1
    for ic in range(NIC1):
        for oc in range(4):
            blocks[b] = dt[ic, oc]
            b += 1
    H["blocks"] = blocks

    x = np.asarray(inp["L0_attn_x"], f32)
    percore = []
    ii = np.arange(128)[:, None]
    jj = np.arange(640)[None, :]
    th = 10000.0 ** (-2.0 * np.arange(64, dtype=np.float64) / 128.0)
    lam_c = np.exp(np.float32(128.0) * ll.astype(f32)).astype(np.float64)
    for c in range(8):
        bq, half = c // 2, c % 2
        pos0 = half * 1024
        pc = {}
        pc["x_own"] = cc(x[bq, pos0:pos0 + 1024, :])
        hlo = pos0 - 512 if half == 1 else 0
        pc["x_halo"] = cc(x[bq, hlo:hlo + 512, :])
        m = np.zeros((8, 128, 640), np.float32)
        for tt in range(8):
            g = pos0 - 512 + tt * 128 + jj
            valid = (jj >= ii) & (jj <= ii + 512) & (g >= 0)
            m[tt] = np.where(valid, 0.0, NEG)
        pc["masks"] = m.astype(BF)
        tpos = (pos0 + np.arange(1024, dtype=np.float64))[None, :]
        ang = np.concatenate([th, th])[:, None] * tpos
        pc["cos2"] = np.cos(ang).astype(f32).astype(BF)
        pc["sin2"] = np.sin(ang).astype(f32).astype(BF)
        lf = np.zeros((128,), np.float64)
        for h in range(NH):
            for cch in range(8):
                lf[h * 8 + cch] = (lam_c[h] ** cch) * float(half)
        pc["lamflag"] = cc(np.broadcast_to(lf.astype(f32)[None, :], (128, 128)))
        pc["l1idx_shard"] = cc(blocks[c * SHARD:(c + 1) * SHARD])
        percore.append(pc)
    H["percore"] = percore
    return H


def kernel(**inputs):
    H = host_prepare(inputs)
    cores = list(range(8))
    if FUSED:
        ncF = build_fused()
        shared = {k: H[k] for k in ("gT_h", "uT_h", "dT_h", "gs_c", "us_c", "dsrep",
                                    "wq_h", "wk_h", "wv_g", "wo_oc", "l1wq", "l1wk",
                                    "l1wv", "l1wo", "wg1_s", "wg2_s", "nwrep",
                                    "lutrep", "dkcol", "dqrep", "D_s")}
        imF = []
        for c in cores:
            pc = H["percore"][c]
            m = dict(shared)
            m["x_own"] = pc["x_own"]
            m["x_halo"] = pc["x_halo"]
            m["masks"] = pc["masks"]
            m["cos2"] = pc["cos2"]
            m["sin2"] = pc["sin2"]
            m["lamflag"] = pc["lamflag"]
            m["l1idx_shard"] = pc["l1idx_shard"]
            m["flag2c"] = np.full((128, 1), 1.0 - (c % 2), np.float32)
            imF.append(m)
        res = run_bass_kernel_spmd(ncF, imF, core_ids=cores)
        out = np.concatenate([res.results[c]["xout"] for c in cores], axis=0)
        return out.reshape(4, 2048, 2048)
    ncA = build_deq()
    imA = [{"l1idx_shard": H["percore"][c]["l1idx_shard"], "lutrep": H["lutrep"]} for c in cores]
    resA = run_bass_kernel_spmd(ncA, imA, core_ids=cores)
    w_gath = np.concatenate([resA.results[c]["wshard"] for c in cores], axis=0)

    ncB = build_l0()
    shared = {k: H[k] for k in ("gT_h", "uT_h", "dT_h", "gs_c", "us_c", "dsrep")}
    shared["wq_h"] = H["wq_h"]; shared["wk_h"] = H["wk_h"]
    shared["wv_g"] = H["wv_g"]; shared["wo_oc"] = H["wo_oc"]
    imB = []
    for c in cores:
        m = dict(shared)
        m["x_own"] = H["percore"][c]["x_own"]
        m["x_halo"] = H["percore"][c]["x_halo"]
        m["masks"] = H["percore"][c]["masks"]
        imB.append(m)
    resB = run_bass_kernel_spmd(ncB, imB, core_ids=cores)
    x1 = [resB.results[c]["x1"] for c in cores]

    ncC = build_l1pre()
    imC = [{"x1": x1[c], "l1wk": H["l1wk"], "l1wv": H["l1wv"],
            "cos2": H["percore"][c]["cos2"], "sin2": H["percore"][c]["sin2"],
            "dkcol": H["dkcol"]} for c in cores]
    resC = run_bass_kernel_spmd(ncC, imC, core_ids=cores)

    ncD = build_l1main()
    imD = [{"x1": x1[c],
            "krot": resC.results[c]["krot"], "vst": resC.results[c]["vst"],
            "pst": resC.results[c]["pst"], "Rstate": resC.results[c ^ 1]["S_fin"],
            "l1wq": H["l1wq"], "l1wo": H["l1wo"], "wg1_s": H["wg1_s"],
            "wg2_s": H["wg2_s"], "nwrep": H["nwrep"],
            "cos2": H["percore"][c]["cos2"], "sin2": H["percore"][c]["sin2"],
            "dqrep": H["dqrep"], "D_s": H["D_s"], "lamflag": H["percore"][c]["lamflag"],
            "w_gath": w_gath} for c in cores]
    resD = run_bass_kernel_spmd(ncD, imD, core_ids=cores)
    out = np.concatenate([resD.results[c]["xout"] for c in cores], axis=0)
    return out.reshape(4, 2048, 2048)


# version 18
# speedup vs baseline: 1.0026x; 1.0026x over previous
import numpy as np
import ml_dtypes
from contextlib import ExitStack

import concourse.bass as bass
import concourse.mybir as mybir
from concourse.bass_utils import run_bass_kernel_spmd

F32 = mybir.dt.float32
BF16 = mybir.dt.bfloat16
I32 = mybir.dt.int32
U8 = mybir.dt.uint8
AF = mybir.ActivationFunctionType
ALU = mybir.AluOpType
AX = mybir.AxisListType.X
BF = ml_dtypes.bfloat16

P = 128
T = 1024
D = 2048
NH = 16
HD = 128
FC = D // P
I0, NIC0 = 8192, 64
I1, NIC1 = 5504, 43
NEG = -1.0e38
SCALE0 = 1.0 / float(np.sqrt(128.0))
SHARD = 65

class Buf:
    __slots__ = ("w", "r")
    def __init__(self):
        self.w = None
        self.r = {}

NLANE = 8

class Sched:
    def __init__(self, nc):
        self.nc = nc
        self.q = {"pe": nc.tensor, "dve": nc.vector, "act": nc.scalar,
                  "pool": nc.gpsimd, "sp": nc.sync}
        self.sem = {}
        self.cnt = {}
        for n in ("pe", "dve", "act", "pool", "sp", "cc"):
            self.sem[n] = nc.alloc_semaphore(f"s_{n}")
            self.cnt[n] = 0
        self.rr = {}
        for qn in ("sp", "pool", "act"):
            for l in range(NLANE):
                n = f"D{qn}{l}"
                self.sem[n] = nc.alloc_semaphore(f"s_{n}")
                self.cnt[n] = 0
            self.rr[qn] = 0
        self.lastw = {}
        self.bufs = {}

    def buf(self, key):
        b = self.bufs.get(key)
        if b is None:
            b = self.bufs[key] = Buf()
        return b

    def _wait(self, eng, ctr, val):
        if val <= 0:
            return
        k = (eng, ctr)
        if self.lastw.get(k, 0) >= val:
            return
        self.q[eng].wait_ge(self.sem[ctr], val)
        self.lastw[k] = val

    def step(self, eng, r=(), w=(), emit=None, dma=False, cc=False):
        need = {}
        for key in r:
            b = self.buf(key)
            if b.w and need.get(b.w[0], 0) < b.w[1]:
                need[b.w[0]] = b.w[1]
        for key in w:
            b = self.buf(key)
            if b.w and need.get(b.w[0], 0) < b.w[1]:
                need[b.w[0]] = b.w[1]
            for c, v in b.r.items():
                if need.get(c, 0) < v:
                    need[c] = v
        if dma:
            lane = self.rr[eng] % NLANE
            self.rr[eng] += 1
            ctr = f"D{eng}{lane}"
            if self.cnt[ctr] > 0:
                self._wait(eng, ctr, self.cnt[ctr])
        for c in sorted(need):
            self._wait(eng, c, need[c])
        insts = emit(self.q[eng])
        if not isinstance(insts, (list, tuple)):
            insts = [insts]
        if cc:
            ctr = "cc"
            for inst in insts:
                self.cnt[ctr] += 1
                inst.then_inc(self.sem[ctr])
        elif dma:
            assert len(insts) == 1, "one dma_start per dma step"
            self.cnt[ctr] += 16
            insts[0].then_inc(self.sem[ctr], 16)
        else:
            ctr = eng
            self.cnt[ctr] += 1
            insts[-1].then_inc(self.sem[ctr], 1)
        v = self.cnt[ctr]
        for key in r:
            self.buf(key).r[ctr] = v
        for key in w:
            b = self.buf(key)
            b.w = (ctr, v)
            b.r = {}
        return v

    def barrier(self):
        for e in self.q:
            for c, n in self.cnt.items():
                self._wait(e, c, n)


def emit_stage0(nc, s, src, xres, xnT, id_s, halo_src=None, haloT=None, pfx="ln", load=True):
    assert load or halo_src is None
    with ExitStack() as ph:
        sb = lambda n, sh, dt: ph.enter_context(nc.sbuf_tensor(n, sh, dt))
        tp = [ph.enter_context(nc.psum_tensor(f"{pfx}tp{i}", [128, 1024], BF16)) for i in range(2)]
        sc = sb(f"{pfx}_sc", [P, 2, D], F32)
        sq = sb(f"{pfx}_sq", [P, 2, D], F32)
        xb = sb(f"{pfx}_xb", [P, 2, D], BF16)
        scol = sb(f"{pfx}_scol", [P, 16], F32)
        mcol = sb(f"{pfx}_mcol", [P, 16], F32)
        vcol = sb(f"{pfx}_vcol", [P, 16], F32)
        sqv = sb(f"{pfx}_sqv", [P, 16], F32)
        rstd = sb(f"{pfx}_rstd", [P, 16], F32)
        nc._lncol = (scol, mcol, vcol, rstd)
        for tn, kk in ((scol, "scol"), (mcol, "mcol"), (vcol, "vcol"), (sqv, "sqv"), (rstd, "rstd")):
            s.step("dve", w=[(kk, tt) for tt in range(16)] + [(kk,)],
                   emit=lambda q, tn=tn: q.memset(tn[:, :], 0.0))
        hx = sb(f"{pfx}_hx", [P, 1, D], F32) if halo_src is not None else None
        ntile = 8 + (4 if halo_src is not None else 0)

        def load_tile(tt, tag):
            if tt < 8:
                if load:
                    s.step("sp", w=[("xres", tt)], dma=True,
                           emit=lambda q, tt=tt: q.dma_start(
                               out=xres[:, tt, :], in_=src[tt * 128:(tt + 1) * 128, :]))
                return xres[:, tt, :], ("xres", tt)
            th = tt - 8
            slot = 0
            s.step("sp", w=[(tag, slot)], dma=True,
                   emit=lambda q, th=th, slot=slot: q.dma_start(
                       out=hx[:, slot, :], in_=halo_src[th * 128:(th + 1) * 128, :]))
            return hx[:, slot, :], (tag, slot)

        for tt in range(ntile):
            xin, key = load_tile(tt, "hx")
            slot = tt % 2
            s.step("dve", r=[key], w=[("scol", tt)],
                   emit=lambda q, xin=xin, tt=tt: q.reduce_sum(scol[:, tt:tt + 1], xin, axis=AX))
            s.step("dve", r=[("scol", tt)], w=[("mcol", tt)],
                   emit=lambda q, tt=tt: q.tensor_scalar_mul(mcol[:, tt:tt + 1], scol[:, tt:tt + 1], 1.0 / D))
            s.step("dve", r=[key, ("mcol", tt)], w=[("sc", slot)],
                   emit=lambda q, xin=xin, slot=slot, tt=tt: q.tensor_single_scalar(
                       sc[:, slot, :], xin, mcol[:, tt:tt + 1], ALU.subtract))
            s.step("act", r=[("sc", slot)], w=[("sq", slot), ("vcol", tt)],
                   emit=lambda q, slot=slot, tt=tt: q.activation(
                       out=sq[:, slot, :], in_=sc[:, slot, :], func=AF.Square,
                       accum_out=vcol[:, tt:tt + 1]))
        s.step("act", r=[("vcol", tt) for tt in range(ntile)] + [("cst",)], w=[("sqv",)],
               emit=lambda q: q.activation(out=sqv[:, 0:ntile], in_=vcol[:, 0:ntile],
                                           func=AF.Sqrt, scale=1.0 / D, bias=nc._cst[:, 0:1]))
        s.step("dve", r=[("sqv",)], w=[("rstd",)],
               emit=lambda q: q.reciprocal(rstd[:, 0:ntile], sqv[:, 0:ntile]))
        for tt in range(ntile):
            if tt < 8:
                xin, key = xres[:, tt, :], ("xres", tt)
            else:
                xin, key = load_tile(tt, "hx")
            slot = tt % 2
            s.step("dve", r=[key, ("mcol", tt), ("rstd",)], w=[("xb", slot)],
                   emit=lambda q, xin=xin, slot=slot, tt=tt: q.tensor_scalar(
                       xb[:, slot, :], xin, mcol[:, tt:tt + 1], rstd[:, tt:tt + 1],
                       ALU.subtract, ALU.mult))
            for fc in range(FC):
                bank = 6 + (fc % 2)
                dstT = xnT if tt < 8 else haloT
                dcol = (tt * 128) if tt < 8 else ((tt - 8) * 128)
                dkey = ("xnT", fc) if tt < 8 else ("haloT", fc)
                s.step("pe", r=[("xb", slot), ("id",)], w=[("ps", bank)],
                       emit=lambda q, slot=slot, fc=fc, bank=bank: q.transpose(
                           tp[bank - 6][:, 0:128], xb[:, slot, fc * 128:(fc + 1) * 128], id_s[:, :]))
                if fc % 2 == 0:
                    s.step("dve", r=[("ps", bank)], w=[dkey],
                           emit=lambda q, bank=bank, dstT=dstT, fc=fc, dcol=dcol: q.tensor_copy(
                               dstT[:, fc, dcol:dcol + 128], tp[bank - 6][:, 0:128]))
                else:
                    s.step("act", r=[("ps", bank)], w=[dkey],
                           emit=lambda q, bank=bank, dstT=dstT, fc=fc, dcol=dcol: q.copy(
                               dstT[:, fc, dcol:dcol + 128], tp[bank - 6][:, 0:128]))
        s.barrier()


XNKEYS = [("xnT", fc) for fc in range(FC)]
HALOKEYS = [("haloT", fc) for fc in range(FC)]


def emit_l0_attn(nc, s, xres, xnT, haloT, id_s, pb, attn_st, dram, bg=None):
    with ExitStack() as ph:
        sb = lambda n, sh, dt: ph.enter_context(nc.sbuf_tensor(n, sh, dt))
        tp = [ph.enter_context(nc.psum_tensor(f"atp{i}", [128, 1024], BF16)) for i in range(2)]
        masks = sb("a_masks", [P, 8, 640], BF16)
        wv_s = sb("a_wv", [P, FC, 512], BF16)
        wq_s = sb("a_wq", [P, 2, FC, 128], BF16)
        wk_s = sb("a_wk", [P, 2, FC, 128], BF16)
        vg = sb("a_vg", [P, 12, 512], BF16)
        qh = sb("a_qh", [P, 2, T], BF16)
        kh = sb("a_kh", [P, 2, 1536], BF16)
        sm = sb("a_sm", [P, 2, 640], F32)
        probs = sb("a_probs", [P, 2, 640], BF16)
        probsT = sb("a_probsT", [P, 2, 640], BF16)
        mx = sb("a_mx", [P, 8], F32)
        ath = sb("a_ath", [P, 2, T], BF16)

        s.step("sp", w=[("masks",)], dma=True,
               emit=lambda q: q.dma_start(out=masks[:, :, :],
                                          in_=dram["masks"].ap().rearrange("t p j -> p t j")))

        def load_wv(hg):
            s.step("pool", w=[("wvw",)], dma=True,
                   emit=lambda q, hg=hg: q.dma_start(out=wv_s[:, :, :], in_=dram["wv"].ap()[hg]))

        def load_wqk(h):
            hp = h % 2
            s.step("pool", w=[("wqw", hp)], dma=True,
                   emit=lambda q, h=h, hp=hp: q.dma_start(out=wq_s[:, hp, :, :], in_=dram["wq"].ap()[h]))
            s.step("pool", w=[("wkw", hp)], dma=True,
                   emit=lambda q, h=h, hp=hp: q.dma_start(out=wk_s[:, hp, :, :], in_=dram["wk"].ap()[h]))

        def proj_steps(h):
            hp = h % 2
            for th in range(2):
                bank = th % 2
                yield lambda th=th, hp=hp, bank=bank: (
                    s.step("pe", r=XNKEYS + [("wqw", hp)], w=[("ps", bank)],
                           emit=lambda q: [
                               q.matmul(pb[bank][:, :], wq_s[:, hp, hc, :],
                                        xnT[:, hc, th * 512:(th + 1) * 512],
                                        start=(hc == 0), stop=(hc == 15)) for hc in range(FC)]),
                    s.step("dve", r=[("ps", bank)], w=[("qh", hp)],
                           emit=lambda q: q.tensor_copy(
                               qh[:, hp, th * 512:(th + 1) * 512], pb[bank][:, :])))
            for kp in range(3):
                bank = kp % 2
                lsrc = haloT if kp == 0 else xnT
                lkeys = HALOKEYS if kp == 0 else XNKEYS
                tcol = 0 if kp == 0 else (kp - 1) * 512
                yield lambda kp=kp, hp=hp, bank=bank, lsrc=lsrc, lkeys=lkeys, tcol=tcol: (
                    s.step("pe", r=lkeys + [("wkw", hp)], w=[("ps", bank)],
                           emit=lambda q: [
                               q.matmul(pb[bank][:, :], wk_s[:, hp, hc, :],
                                        lsrc[:, hc, tcol:tcol + 512],
                                        start=(hc == 0), stop=(hc == 15)) for hc in range(FC)]),
                    s.step("dve", r=[("ps", bank)], w=[("kh", hp)],
                           emit=lambda q: q.tensor_copy(
                               kh[:, hp, kp * 512:(kp + 1) * 512], pb[bank][:, :])))

        def stA(h, tt):
            hp = h % 2
            bA, bC = 2 + tt % 2, 4 + tt % 2
            s.step("pe", r=[("qh", hp), ("kh", hp)], w=[("ps", bA)],
                   emit=lambda q: q.matmul(
                       pb[bA][:, :], qh[:, hp, tt * 128:(tt + 1) * 128],
                       kh[:, hp, tt * 128:tt * 128 + 512], start=True, stop=True))
            s.step("pe", r=[("qh", hp), ("kh", hp)], w=[("ps", bC)],
                   emit=lambda q: q.matmul(
                       pb[bC][:, 128:256], qh[:, hp, tt * 128:(tt + 1) * 128],
                       kh[:, hp, tt * 128 + 512:tt * 128 + 640], start=True, stop=True))

        def stB(h, tt):
            pp = tt % 2
            bA, bC = 2 + tt % 2, 4 + tt % 2
            s.step("dve", r=[("ps", bA), ("masks",)], w=[("sm", pp)],
                   emit=lambda q: q.scalar_tensor_tensor(
                       sm[:, pp, 0:512], pb[bA][:, :], SCALE0, masks[:, tt, 0:512],
                       ALU.mult, ALU.add))
            s.step("dve", r=[("ps", bC), ("masks",)], w=[("sm", pp)],
                   emit=lambda q: q.scalar_tensor_tensor(
                       sm[:, pp, 512:640], pb[bC][:, 128:256], SCALE0, masks[:, tt, 512:640],
                       ALU.mult, ALU.add))
            s.step("dve", r=[("sm", pp)], w=[("mx", pp)],
                   emit=lambda q: q.reduce_max(mx[:, 4 * pp:4 * pp + 1], sm[:, pp, :], axis=AX))
            s.step("dve", r=[("mx", pp)], w=[("mx", pp)],
                   emit=lambda q: q.tensor_scalar_mul(
                       mx[:, 4 * pp + 1:4 * pp + 2], mx[:, 4 * pp:4 * pp + 1], -1.0))

        def stC(h, tt):
            pp = tt % 2
            s.step("act", r=[("sm", pp), ("mx", pp)], w=[("probs", pp), ("mx", pp)],
                   emit=lambda q: q.activation(
                       out=probs[:, pp, :], in_=sm[:, pp, :], func=AF.Exp,
                       scale=1.0, bias=mx[:, 4 * pp + 1:4 * pp + 2],
                       accum_out=mx[:, 4 * pp + 2:4 * pp + 3]))

        def stD(h, tt):
            pp = tt % 2
            s.step("dve", r=[("mx", pp)], w=[("mx", pp)],
                   emit=lambda q: q.reciprocal(
                       mx[:, 4 * pp + 3:4 * pp + 4], mx[:, 4 * pp + 2:4 * pp + 3]))
            s.step("dve", r=[("probs", pp), ("mx", pp)], w=[("probs", pp)],
                   emit=lambda q: q.tensor_single_scalar(
                       probs[:, pp, :], probs[:, pp, :], mx[:, 4 * pp + 3:4 * pp + 4], ALU.mult))

        def stE(h, tt):
            pp = tt % 2
            for jj in range(5):
                s.step("pe", r=[("probs", pp), ("id",)], w=[("ps", 6 + pp)],
                       emit=lambda q, jj=jj: q.transpose(
                           tp[pp][:, jj * 128:(jj + 1) * 128],
                           probs[:, pp, jj * 128:(jj + 1) * 128], id_s[:, :]))

        def stF(h, tt):
            pp = tt % 2
            s.step("act", r=[("ps", 6 + pp)], w=[("probsT", pp)],
                   emit=lambda q: q.copy(probsT[:, pp, :], tp[pp][:, 0:640]))

        def stG(h, hs, tt):
            pp = tt % 2
            bC = 4 + tt % 2
            s.step("pe", r=[("probsT", pp)] + [("vg", tt + jj) for jj in range(5)],
                   w=[("ps", bC)],
                   emit=lambda q: [
                       q.matmul(pb[bC][:, 0:128], vg[:, tt + jj, hs * 128:(hs + 1) * 128],
                                probsT[:, pp, jj * 128:(jj + 1) * 128],
                                start=(jj == 0), stop=(jj == 4)) for jj in range(5)])

        def stH(h, tt):
            hp = h % 2
            bC = 4 + tt % 2
            s.step("dve", r=[("ps", bC)], w=[("ath", hp)],
                   emit=lambda q: q.tensor_copy(
                       ath[:, hp, tt * 128:(tt + 1) * 128], pb[bC][:, 0:128]))

        load_wv(0)
        load_wqk(0)
        pending = list(proj_steps(0))
        for st in pending:
            st()
        for hg in range(4):
            for tc in range(12):
                bank = tc % 2
                if tc < 4:
                    lsrc, lkeys, tcol = haloT, HALOKEYS, tc * 128
                else:
                    lsrc, lkeys, tcol = xnT, XNKEYS, (tc - 4) * 128
                s.step("pe", r=lkeys + [("wvw",)], w=[("ps", bank)],
                       emit=lambda q, lsrc=lsrc, tcol=tcol, bank=bank: [
                           q.matmul(pb[bank][:, :], lsrc[:, hc, tcol:tcol + 128],
                                    wv_s[:, hc, :], start=(hc == 0), stop=(hc == 15))
                           for hc in range(FC)])
                s.step("dve", r=[("ps", bank)], w=[("vg", tc)],
                       emit=lambda q, tc=tc, bank=bank: q.tensor_copy(vg[:, tc, :], pb[bank][:, :]))
                if bg and tc % 3 == 2:
                    bg()
            if hg < 3:
                load_wv(hg + 1)
            for hs in range(4):
                h = hg * 4 + hs
                hp = h % 2
                if h < 15:
                    load_wqk(h + 1)
                nxt = list(proj_steps(h + 1)) if h + 1 < 16 else []
                for i in range(10):
                    if i < 8:
                        stA(h, i)
                        stB(h, i)
                    if i - 1 >= 0 and i - 1 < 8:
                        stD(h, i - 1)
                    if i - 2 >= 0:
                        stE(h, i - 2)
                        stF(h, i - 2)
                        stG(h, hs, i - 2)
                        stH(h, i - 2)
                    if i < 8:
                        stC(h, i)
                    if i in (1, 2, 3, 4, 5) and nxt:
                        nxt.pop(0)()
                    if bg and i in (3, 7):
                        bg()
                while nxt:
                    nxt.pop(0)()
                s.step("sp", r=[("ath", hp)], w=[("ast", h)], dma=True,
                       emit=lambda q, h=h, hp=hp: q.dma_start(out=attn_st.ap()[h], in_=ath[:, hp, :]))
        s.barrier()
    with ExitStack() as ph:
        sb = lambda n, sh, dt: ph.enter_context(nc.sbuf_tensor(n, sh, dt))
        wo_s = sb("a_wo", [P, 2, FC, 512], BF16)
        atile = sb("a_atile", [P, 2, FC, 128], BF16)
        astk = [("ast", h) for h in range(NH)]
        for ps2 in range(2):
            for o2 in range(2):
                s.step("pool", w=[("wow", o2)], dma=True,
                       emit=lambda q, ps2=ps2, o2=o2: q.dma_start(
                           out=wo_s[:, o2, :, :], in_=dram["wo"].ap()[ps2 * 2 + o2]))
            for tt in range(8):
                s.step("sp", r=astk, w=[("atile", tt % 2)], dma=True,
                       emit=lambda q, tt=tt: q.dma_start(
                           out=atile[:, tt % 2, :, :],
                           in_=attn_st.ap()[:, :, tt * 128:(tt + 1) * 128].rearrange("h d t -> d h t")))
                for o2 in range(2):
                    bank = 2 * (tt % 2) + o2
                    s.step("pe", r=[("atile", tt % 2), ("wow", o2)], w=[("ps", bank)],
                           emit=lambda q, tt=tt, o2=o2, bank=bank: [
                               q.matmul(pb[bank][:, :], atile[:, tt % 2, fc, :],
                                        wo_s[:, o2, fc, :], start=(fc == 0), stop=(fc == 15))
                               for fc in range(FC)])
                    s.step("dve", r=[("ps", bank)], w=[("xres", tt)],
                           emit=lambda q, tt=tt, ps2=ps2, o2=o2, bank=bank: q.tensor_tensor(
                               xres[:, tt, (ps2 * 2 + o2) * 512:(ps2 * 2 + o2 + 1) * 512],
                               xres[:, tt, (ps2 * 2 + o2) * 512:(ps2 * 2 + o2 + 1) * 512],
                               pb[bank][:, :], ALU.add))
                if bg:
                    bg()
        s.barrier()


def emit_qffn(nc, s, xres, xnT, pb, h_st, nic, corrections, wcfg, pfx="q", bg=None, after_a=None):
    with ExitStack() as ph:
        sb = lambda n, sh, dt: ph.enter_context(nc.sbuf_tensor(n, sh, dt))
        pext = [ph.enter_context(nc.psum_tensor(f"{pfx}px{i}", [128, 512], F32)) for i in range(2)]
        pb8 = list(pb) + pext
        wg_s = sb(f"{pfx}_wg", wcfg["gu_shape"], BF16)
        wu_s = sb(f"{pfx}_wu", wcfg["gu_shape"], BF16)
        wd_s = sb(f"{pfx}_wd", [P, 3, 512], BF16)
        wdr = sb(f"{pfx}_wdr", [P, 3, 512], U8) if wcfg.get("wd_raw") else None
        s._wdr = wdr
        s._wg, s._wu, s._wd = wg_s, wu_s, wd_s
        h_s = sb(f"{pfx}_h", [P, 3, T], BF16)
        hl = sb(f"{pfx}_hl", [P, 3, T], BF16)
        sg = sb(f"{pfx}_sgb", [P, 2, 512], BF16)
        su = sb(f"{pfx}_sub", [P, 2, 512], BF16)
        ones_c = sb(f"{pfx}_ones", [P, 1], BF16)
        m128 = sb(f"{pfx}_m128", [1, 512], BF16)
        xnsum = sb(f"{pfx}_xnsum", [1, T], BF16)
        hsum = sb(f"{pfx}_hsum", [1, T], BF16)
        if corrections:
            ytmp = sb(f"{pfx}_ytmp", [P, 2, 512], F32)
            gs_c = sb(f"{pfx}_gs", [P, nic], F32)
            us_c = sb(f"{pfx}_us", [P, nic], F32)
            dsrep = sb(f"{pfx}_dsrep", [P, D], F32)
            s.step("sp", w=[("gsc",)], dma=True,
                   emit=lambda q: q.dma_start(out=gs_c[:, :], in_=wcfg["gs"].ap()[:, :]))
            s.step("sp", w=[("usc",)], dma=True,
                   emit=lambda q: q.dma_start(out=us_c[:, :], in_=wcfg["us"].ap()[:, :]))
            s.step("sp", w=[("dsr",)], dma=True,
                   emit=lambda q: q.dma_start(out=dsrep[:, :], in_=wcfg["dsrep"].ap()[:, :]))
        s.step("dve", w=[("ones",)], emit=lambda q: q.memset(ones_c[:, :], 1.0))
        s.step("dve", w=[("m128",)], emit=lambda q: q.memset(m128[:, :], -128.0))
        gsl = wcfg["gu_slice"]

        if corrections:
            for th in range(2):
                s.step("pe", r=XNKEYS + [("ones",)], w=[("ps", 4 + th)],
                       emit=lambda q, th=th: [
                           q.matmul(pb[4 + th][0:1, :], ones_c[:, :],
                                    xnT[:, hc, th * 512:(th + 1) * 512],
                                    start=(hc == 0), stop=(hc == 15)) for hc in range(FC)])
                s.step("dve", r=[("ps", 4 + th)], w=[("xnsum",)],
                       emit=lambda q, th=th: q.tensor_copy(
                           xnsum[0:1, th * 512:(th + 1) * 512], pb[4 + th][0:1, :]))

        def emit_hsum(ic):
            for th in range(2):
                s.step("pe", r=[("h", ic % 3), ("ones",)], w=[("ps", 4 + th)],
                       emit=lambda q, ic=ic, th=th: q.matmul(
                           pb[4 + th][0:1, :], ones_c[:, :],
                           h_s[:, ic % 3, th * 512:(th + 1) * 512],
                           start=(ic == 0), stop=(ic == nic - 1)))

        wcfg["wg_load"](s, 0)
        wcfg["wu_load"](s, 0)
        for ic in range(nic):
            for gu in range(2):
                wkey = ("wg" if gu == 0 else "wu", ic % 2)
                ws = wg_s if gu == 0 else wu_s
                for th in range(2):
                    bank = gu * 2 + th
                    def mmgu(q, ws=ws, ic=ic, th=th, bank=bank):
                        ins = [q.matmul(pb[bank][:, :], gsl(ws, ic % 2, hc),
                                        xnT[:, hc, th * 512:(th + 1) * 512],
                                        start=(hc == 0),
                                        stop=(hc == 15 and not corrections))
                               for hc in range(FC)]
                        if corrections:
                            ins.append(q.matmul(pb[bank][:, :], m128[0:1, 0:128],
                                                xnsum[0:1, th * 512:(th + 1) * 512],
                                                start=False, stop=True))
                        return ins
                    rdk = XNKEYS + [wkey]
                    if corrections:
                        rdk = rdk + [("xnsum",), ("m128",)]
                    s.step("pe", r=rdk, w=[("ps", bank)], emit=mmgu)
            if ic + 1 < nic:
                wcfg["wg_load"](s, ic + 1)
                wcfg["wu_load"](s, ic + 1)
            for th in range(2):
                if corrections:
                    s.step("act", r=[("ps", th), ("gsc",)], w=[("sg", th)],
                           emit=lambda q, ic=ic, th=th: q.activation(
                               out=sg[:, th, :], in_=pb[th][:, :], func=AF.Silu,
                               scale=gs_c[:, ic:ic + 1]))
                    s.step("act", r=[("ps", 2 + th), ("usc",)], w=[("su", th)],
                           emit=lambda q, ic=ic, th=th: q.activation(
                               out=su[:, th, :], in_=pb[2 + th][:, :], func=AF.Copy,
                               scale=us_c[:, ic:ic + 1]))
                else:
                    s.step("act", r=[("ps", th)], w=[("sg", th)],
                           emit=lambda q, th=th: q.activation(
                               out=sg[:, th, :], in_=pb[th][:, :], func=AF.Silu))
                    s.step("act", r=[("ps", 2 + th)], w=[("su", th)],
                           emit=lambda q, th=th: q.activation(
                               out=su[:, th, :], in_=pb[2 + th][:, :], func=AF.Copy))
                s.step("dve", r=[("sg", th), ("su", th)], w=[("h", ic % 3)],
                       emit=lambda q, ic=ic, th=th: q.tensor_tensor(
                           h_s[:, ic % 3, th * 512:(th + 1) * 512],
                           sg[:, th, :], su[:, th, :], ALU.mult))
            s.step("sp", r=[("h", ic % 3)], w=[("hst", ic)], dma=True,
                   emit=lambda q, ic=ic: q.dma_start(out=h_st.ap()[ic], in_=h_s[:, ic % 3, :]))
            if bg:
                bg()
            if corrections and ic >= 2:
                emit_hsum(ic - 2)
        if corrections:
            emit_hsum(nic - 2)
            emit_hsum(nic - 1)
            for th in range(2):
                s.step("dve", r=[("ps", 4 + th)], w=[("hsum",)],
                       emit=lambda q, th=th: q.tensor_copy(
                           hsum[0:1, th * 512:(th + 1) * 512], pb[4 + th][0:1, :]))
        if after_a is not None:
            while bg and (after_a.__self__.n < SHARD):
                bg()
            after_a()
        HLD = 3
        WDD = 3
        for oc in range(4):
            for ic in range(nic):
                s.step("sp", r=[("hst", ic)], w=[("hl", ic % HLD)], dma=True,
                       emit=lambda q, ic=ic: q.dma_start(out=hl[:, ic % HLD, :], in_=h_st.ap()[ic]))
                wcfg["wd_load"](s, ic, oc)
                last = (ic == nic - 1) and not corrections
                s.step("pe", r=[("hl", ic % HLD), ("wd", ic % WDD)],
                       w=[("ps", tt2) for tt2 in range(8)],
                       emit=lambda q, ic=ic, last=last: [
                           q.matmul(pb8[tt2][:, :], hl[:, ic % HLD, tt2 * 128:(tt2 + 1) * 128],
                                    wd_s[:, ic % WDD, :], start=(ic == 0), stop=last)
                           for tt2 in range(8)])
                if bg:
                    bg()
            if corrections:
                s.step("pe", r=[("hsum",), ("m128",)], w=[("ps", tt2) for tt2 in range(8)],
                       emit=lambda q: [
                           q.matmul(pb8[tt2][:, :], hsum[0:1, tt2 * 128:(tt2 + 1) * 128],
                                    m128[0:1, :], start=False, stop=True) for tt2 in range(8)])
            for tt in range(8):
                if corrections:
                    s.step("dve", r=[("ps", tt), ("dsr",)], w=[("ytmp", tt % 2)],
                           emit=lambda q, tt=tt, oc=oc: q.tensor_tensor(
                               ytmp[:, tt % 2, :], pb8[tt][:, :],
                               dsrep[:, oc * 512:(oc + 1) * 512], ALU.mult))
                    s.step("dve", r=[("ytmp", tt % 2)], w=[("xres", tt)],
                           emit=lambda q, tt=tt, oc=oc: q.tensor_tensor(
                               xres[:, tt, oc * 512:(oc + 1) * 512],
                               xres[:, tt, oc * 512:(oc + 1) * 512],
                               ytmp[:, tt % 2, :], ALU.add))
                else:
                    s.step("dve", r=[("ps", tt)], w=[("xres", tt)],
                           emit=lambda q, tt=tt, oc=oc: q.tensor_tensor(
                               xres[:, tt, oc * 512:(oc + 1) * 512],
                               xres[:, tt, oc * 512:(oc + 1) * 512], pb8[tt][:, :], ALU.add))
        s.barrier()


def _mk_common(nc):
    s = Sched(nc)
    pb = [nc.alloc_psum_tensor(f"pb{i}", [128, 512], F32) for i in range(6)]
    xres = nc.alloc_sbuf_tensor("xres", [P, 8, D], F32)
    xnT = nc.alloc_sbuf_tensor("xnT", [P, FC, T], BF16)
    id_d = nc.inline_tensor(np.eye(128, dtype=BF), name="ident")
    id_s = nc.alloc_sbuf_tensor("id_s", [P, 128], BF16)
    s.step("sp", w=[("id",)], dma=True,
           emit=lambda q: q.dma_start(out=id_s[:, :], in_=id_d.ap()[:, :]))
    cst = nc.alloc_sbuf_tensor("cst", [P, 2], F32)
    s.step("dve", w=[("cst",)], emit=lambda q: q.memset(cst[:, 0:1], 1e-5))
    s.step("dve", w=[("cst",)], emit=lambda q: q.memset(cst[:, 1:2], 1e-6))
    nc._cst = cst
    return s, pb, xres, xnT, id_s


def _store_x(nc, s, xres, dst):
    for tt in range(8):
        s.step("sp", r=[("xres", tt)], dma=True,
               emit=lambda q, tt=tt: q.dma_start(
                   out=dst.ap()[tt * 128:(tt + 1) * 128, :], in_=xres[:, tt, :]))
    s.barrier()


def build_l0(parts=("attn", "ffn"), debug=False):
    nc = bass.Bass("TRN2", target_bir_lowering=False)
    din = lambda n, sh, dt: nc.dram_tensor(n, sh, dt, kind="ExternalInput")
    x_own = din("x_own", [T, D], F32)
    x_halo = din("x_halo", [512, D], F32)
    dram = {
        "wq": din("wq_h", [NH, 128, FC, 128], F32),
        "wk": din("wk_h", [NH, 128, FC, 128], F32),
        "wv": din("wv_g", [4, 128, FC, 512], F32),
        "wo": din("wo_oc", [4, 128, FC, 512], F32),
        "masks": din("masks", [8, 128, 640], BF16),
    }
    gT_d = din("gT_h", [NIC0, 128, FC, 128], U8)
    uT_d = din("uT_h", [NIC0, 128, FC, 128], U8)
    dT_d = din("dT_h", [NIC0, 4, 128, 512], U8)
    gs_d = din("gs_c", [128, NIC0], F32)
    us_d = din("us_c", [128, NIC0], F32)
    dsrep_d = din("dsrep", [128, D], F32)
    x1 = nc.dram_tensor("x1", [T, D], F32, kind="ExternalOutput")
    kind_dbg = "ExternalOutput" if debug else "Internal"
    attn_st = nc.dram_tensor("attn_st", [NH, 128, T], BF16, kind=kind_dbg)
    h_st = nc.dram_tensor("h_st", [NIC0, 128, T], BF16, kind=kind_dbg)

    s, pb, xres, xnT, id_s = _mk_common(nc)
    haloT = nc.alloc_sbuf_tensor("haloT", [P, FC, 512], BF16)
    emit_stage0(nc, s, x_own.ap(), xres, xnT, id_s, x_halo.ap(), haloT)
    if debug:
        xnT_dbg = nc.dram_tensor("xnT_dbg", [128, FC, T], BF16, kind="ExternalOutput")
        s.step("sp", r=XNKEYS, dma=True,
               emit=lambda q: q.dma_start(out=xnT_dbg.ap()[:, :, :], in_=xnT[:, :, :]))
        col_dbg = nc.dram_tensor("col_dbg", [128, 64], F32, kind="ExternalOutput")
        for di, tn in enumerate(nc._lncol):
            s.step("sp", r=[("rstd",)], dma=True,
                   emit=lambda q, di=di, tn=tn: q.dma_start(
                       out=col_dbg.ap()[:, di * 16:(di + 1) * 16], in_=tn[:, :]))


    if "attn" in parts:
        emit_l0_attn(nc, s, xres, xnT, haloT, id_s, pb, attn_st, dram)

    if "ffn" in parts:
        def wg_load(s, ic):
            s.step("pool", w=[("wg", ic % 2)], dma=True,
                   emit=lambda q, ic=ic: q.dma_start(
                       out=s._wg[:, ic % 2, :, :], in_=gT_d.ap()[ic]))
        def wu_load(s, ic):
            s.step("pool", w=[("wu", ic % 2)], dma=True,
                   emit=lambda q, ic=ic: q.dma_start(
                       out=s._wu[:, ic % 2, :, :], in_=uT_d.ap()[ic]))
        def wd_load(s, ic, oc):
            s.step("pool", w=[("wd", ic % 3)], dma=True,
                   emit=lambda q, ic=ic, oc=oc: q.dma_start(
                       out=s._wd[:, ic % 3, :], in_=dT_d.ap()[ic, oc]))
        wcfg = {
            "gu_shape": [P, 2, FC, 128],
            "gu_slice": lambda ws, slot, hc: ws[:, slot, hc, :],
            "wg_load": wg_load, "wu_load": wu_load, "wd_load": wd_load,
            "gs": gs_d, "us": us_d, "dsrep": dsrep_d,
        }
        emit_qffn(nc, s, xres, xnT, pb, h_st, NIC0, True, wcfg)

    _store_x(nc, s, xres, x1)
    return nc


def build_deq():
    nc = bass.Bass("TRN2", target_bir_lowering=False)
    idx_d = nc.dram_tensor("l1idx_shard", [SHARD, 128, 512], I32, kind="ExternalInput")
    lut_d = nc.dram_tensor("lutrep", [128, 16], F32, kind="ExternalInput")
    out_d = nc.dram_tensor("wshard", [SHARD, 128, 512], BF16, kind="ExternalOutput")
    s = Sched(nc)
    lut_s = nc.alloc_sbuf_tensor("lut_s", [P, 16], F32)
    ib = nc.alloc_sbuf_tensor("ib", [P, 4, 512], I32)
    wb = nc.alloc_sbuf_tensor("wb", [P, 4, 512], BF16)
    tb = nc.alloc_sbuf_tensor("tb", [P, 4, 512], BF16)
    s.step("sp", w=[("lut",)], dma=True,
           emit=lambda q: q.dma_start(out=lut_s[:, :], in_=lut_d.ap()[:, :]))
    dvn = [0]
    pon = [0]
    for n in range(SHARD):
        eng = "dve" if (n % 3) != 2 else "pool"
        if eng == "dve":
            slot = dvn[0] % 2
            dvn[0] += 1
        else:
            slot = 2 + pon[0] % 2
            pon[0] += 1
        s.step("sp", w=[("ib", slot)], dma=True,
               emit=lambda q, n=n, slot=slot: q.dma_start(out=ib[:, slot, :], in_=idx_d.ap()[n]))
        def chain(q, slot=slot):
            ins = [q.tensor_scalar(wb[:, slot, :], ib[:, slot, :], 0.0, lut_s[:, 0:1],
                                   ALU.is_equal, ALU.mult)]
            for j in range(1, 16):
                ins.append(q.tensor_scalar(tb[:, slot, :], ib[:, slot, :], float(j),
                                           lut_s[:, j:j + 1], ALU.is_equal, ALU.mult))
                ins.append(q.tensor_tensor(wb[:, slot, :], wb[:, slot, :], tb[:, slot, :], ALU.add))
            return ins
        s.step(eng, r=[("ib", slot), ("lut",)], w=[("wb", slot), ("tb", slot)], emit=chain)
        s.step("sp", r=[("wb", slot)], dma=True,
               emit=lambda q, n=n, slot=slot: q.dma_start(out=out_d.ap()[n], in_=wb[:, slot, :]))
    s.barrier()
    return nc


LL = [-(2.0 ** (-8.0 * (h + 1) / NH)) for h in range(NH)]
LAMC = [float(np.exp(np.float32(128.0) * np.float32(LL[h]))) for h in range(NH)]


def _mk_rope_bufs(nc, sb, pfx, pb, tag=0, shared=None):
    R = {"pb": pb, "tg": tag}
    if shared is None:
        R["cos2"] = sb(f"{pfx}_cos", [P, T], BF16)
        R["sin2"] = sb(f"{pfx}_sin", [P, T], BF16)
    else:
        R["cos2"] = shared["cos2"]
        R["sin2"] = shared["sin2"]
    R["ksil"] = sb(f"{pfx}_ksil", [P, T], BF16)
    R["t1"] = sb(f"{pfx}_t1", [P, T], BF16)
    R["t2"] = sb(f"{pfx}_t2", [P, T], BF16)
    R["t1sh"] = sb(f"{pfx}_t1sh", [64, T], BF16)
    R["t2sh"] = sb(f"{pfx}_t2sh", [64, T], BF16)
    R["obuf"] = sb(f"{pfx}_obuf", [64, T], BF16)
    return R


def _rope1_silu(nc, s, R, bank, th):
    g = R["tg"]
    s.step("act", r=[("ps", bank)], w=[("ksil", g, th)],
           emit=lambda q, th=th, b=bank: q.activation(
               out=R["ksil"][:, th * 512:(th + 1) * 512], in_=R["pb"][b][:, :], func=AF.Silu))


def _rope1_tail(nc, s, R):
    g = R["tg"]
    s.step("dve", r=[("ksil", g, 0), ("ksil", g, 1), ("cos2",)], w=[("t1", g)],
           emit=lambda q: q.tensor_tensor(R["t1"][:, :], R["ksil"][:, :], R["cos2"][:, :], ALU.mult))
    s.step("dve", r=[("ksil", g, 0), ("ksil", g, 1), ("sin2",)], w=[("t2", g)],
           emit=lambda q: q.tensor_tensor(R["t2"][:, :], R["ksil"][:, :], R["sin2"][:, :], ALU.mult))
    s.step("sp", r=[("t1", g)], w=[("t1sh", g)], dma=True,
           emit=lambda q: q.dma_start(out=R["t1sh"][:, :], in_=R["t1"][64:128, :]))
    s.step("sp", r=[("t2", g)], w=[("t2sh", g)], dma=True,
           emit=lambda q: q.dma_start(out=R["t2sh"][:, :], in_=R["t2"][64:128, :]))


def _rope1(nc, s, R, psrc0):
    for th in range(2):
        _rope1_silu(nc, s, R, psrc0 + th, th)
    _rope1_tail(nc, s, R)


def _rope2(nc, s, R, dst, dst_key):
    g = R["tg"]
    s.step("dve", r=[("t1", g), ("t2sh", g)], w=[dst_key],
           emit=lambda q: q.tensor_tensor(dst[0:64, :], R["t1"][0:64, :], R["t2sh"][:, :], ALU.subtract))
    s.step("dve", r=[("t2", g), ("t1sh", g)], w=[("obuf", g)],
           emit=lambda q: q.tensor_tensor(R["obuf"][:, :], R["t2"][0:64, :], R["t1sh"][:, :], ALU.add))
    s.step("sp", r=[("obuf", g)], w=[dst_key], dma=True,
           emit=lambda q, dst=dst: q.dma_start(out=dst[64:128, :], in_=R["obuf"][:, :]))


def _rope(nc, s, R, psrc0, dst, dst_key):
    _rope1(nc, s, R, psrc0)
    _rope2(nc, s, R, dst, dst_key)


def emit_l1pre(nc, s, pb, xnT, id_s, hd, fused=False, bg=None):
    wk_d, wv_d, cos_d, sin_d, dk_d = hd["wk"], hd["wv"], hd["cos"], hd["sin"], hd["dk"]
    krot_d, vst_d, pst_d = hd["krot_st"], hd["vst"], hd["pst"]
    with ExitStack() as ph:
        sb = lambda n, sh, dt: ph.enter_context(nc.sbuf_tensor(n, sh, dt))
        tp = [ph.enter_context(nc.psum_tensor(f"ptp{i}", [128, 1024], BF16)) for i in range(2)]
        R = _mk_rope_bufs(nc, sb, "pr", pb)
        wv_s = sb("p_wv", [P, FC, 512], BF16)
        wk_s = sb("p_wk", [P, 2, FC, 128], BF16)
        vg = sb("p_vg", [P, 8, 512], BF16)
        krot2 = sb("p_krot", [P, 2, T], BF16)
        Ssb2 = sb("p_S", [P, 2, 128], F32)
        pcb = sb("p_pcb", [P, 2, 128], BF16)
        ktdb = sb("p_ktdb", [P, 2, 128], BF16)
        dk_s = sb("p_dk", [P, NH], F32)
        if fused:
            fl2 = sb("p_fl2", [P, 1], F32)
            cbuf = sb("p_cbuf", [P, 2, 128], F32)
            s.step("sp", w=[("fl2",)], dma=True,
                   emit=lambda q: q.dma_start(out=fl2[:, :], in_=hd["flag2"].ap()[:, :]))
        s.step("sp", w=[("cos2",)], dma=True, emit=lambda q: q.dma_start(out=R["cos2"][:, :], in_=cos_d.ap()[:, :]))
        s.step("sp", w=[("sin2",)], dma=True, emit=lambda q: q.dma_start(out=R["sin2"][:, :], in_=sin_d.ap()[:, :]))
        s.step("sp", w=[("dk",)], dma=True, emit=lambda q: q.dma_start(out=dk_s[:, :], in_=dk_d.ap()[:, :]))

        def load_wv(hg):
            s.step("pool", w=[("wvw",)], dma=True,
                   emit=lambda q, hg=hg: q.dma_start(out=wv_s[:, :, :], in_=wv_d.ap()[hg]))
        def load_wk(h):
            s.step("pool", w=[("wkw", h % 2)], dma=True,
                   emit=lambda q, h=h: q.dma_start(out=wk_s[:, h % 2, :, :], in_=wk_d.ap()[h]))

        def kprep_steps(h):
            hp = h % 2
            for th in range(2):
                yield lambda th=th, hp=hp: (
                    s.step("pe", r=XNKEYS + [("wkw", hp)], w=[("ps", 4)],
                           emit=lambda q: [
                               q.matmul(pb[4][:, :], wk_s[:, hp, hc, :],
                                        xnT[:, hc, th * 512:(th + 1) * 512],
                                        start=(hc == 0), stop=(hc == 15)) for hc in range(FC)]),
                    _rope1_silu(nc, s, R, 4, th))
            yield lambda hp=hp: (_rope1_tail(nc, s, R),
                                 load_wk(h + 1) if h + 1 < 16 else None)
            def r2(hp=hp, h=h):
                _rope2(nc, s, R, krot2[:, hp, :], ("krot", hp))
                s.step("sp", r=[("krot", hp)], dma=True,
                       emit=lambda q: q.dma_start(out=krot_d.ap()[h], in_=krot2[:, hp, :]))
            yield r2
        load_wv(0)
        load_wk(0)
        for hg in range(4):
            for tc in range(8):
                bank = 4
                s.step("pe", r=XNKEYS + [("wvw",)], w=[("ps", bank)],
                       emit=lambda q, tc=tc, bank=bank: [
                           q.matmul(pb[bank][:, :], xnT[:, hc, tc * 128:(tc + 1) * 128],
                                    wv_s[:, hc, :], start=(hc == 0), stop=(hc == 15))
                           for hc in range(FC)])
                s.step("dve", r=[("ps", bank)], w=[("vg", tc)],
                       emit=lambda q, tc=tc, bank=bank: q.tensor_copy(vg[:, tc, :], pb[bank][:, :]))
                if bg:
                    bg()
                    bg()
            if hg < 3:
                load_wv(hg + 1)
            s.step("sp", r=[("vg", tc) for tc in range(8)], dma=True,
                   emit=lambda q, hg=hg: q.dma_start(
                       out=vst_d.ap()[hg].rearrange("tc p j -> p tc j"), in_=vg[:, :, :]))
            for hs in range(4):
                h = hg * 4 + hs
                hp = h % 2
                krot = krot2[:, hp, :]
                Ssb = Ssb2[:, hp, :]
                if h == 0:
                    for st in kprep_steps(0):
                        st()
                nxt = list(kprep_steps(h + 1)) if h + 1 < 16 else []
                s.step("dve", w=[("S", hp)], emit=lambda q, Ssb=Ssb: q.memset(Ssb[:, :], 0.0))
                for c in range(8):
                    if c in (1, 3, 5, 7) and nxt:
                        nxt.pop(0)()
                    cp = c % 2
                    s.step("dve", r=[("S", hp)], w=[("pcb", cp)],
                           emit=lambda q, cp=cp, Ssb=Ssb: q.tensor_copy(pcb[:, cp, :], Ssb[:, :]))
                    s.step("sp", r=[("pcb", cp)], dma=True,
                           emit=lambda q, h=h, c=c, cp=cp: q.dma_start(
                               out=pst_d.ap()[h, c], in_=pcb[:, cp, :]))
                    s.step("pe", r=[("krot", hp), ("id",)], w=[("ps", 6 + cp)],
                           emit=lambda q, c=c, cp=cp, krot=krot: q.transpose(
                               tp[cp][:, 0:128], krot[:, c * 128:(c + 1) * 128], id_s[:, :]))
                    s.step("dve", r=[("ps", 6 + cp), ("dk",)], w=[("ktdb", cp)],
                           emit=lambda q, h=h, cp=cp: q.tensor_single_scalar(
                               ktdb[:, cp, :], tp[cp][:, 0:128], dk_s[:, h:h + 1], ALU.mult))
                    s.step("pe", r=[("ktdb", cp), ("vg", c)], w=[("ps", 5)],
                           emit=lambda q, c=c, hs=hs, cp=cp: q.matmul(
                               pb[5][:, 0:128], ktdb[:, cp, :],
                               vg[:, c, hs * 128:(hs + 1) * 128], start=True, stop=True))
                    s.step("dve", r=[("ps", 5)], w=[("S", hp)],
                           emit=lambda q, h=h, cp=cp, Ssb=Ssb: q.scalar_tensor_tensor(
                               Ssb[:, :], Ssb[:, :], LAMC[h], pb[5][:, 0:128],
                               ALU.mult, ALU.add))
                    if bg:
                        bg()
                while nxt:
                    nxt.pop(0)()
                if fused:
                    s.step("dve", r=[("S", hp), ("fl2",)], w=[("cb", h % 2)],
                           emit=lambda q, h=h, Ssb=Ssb: q.tensor_single_scalar(
                               cbuf[:, h % 2, :], Ssb[:, :], fl2[:, 0:1], ALU.mult))
                    s.step("sp", r=[("cb", h % 2)], w=[("ctb", h)], dma=True,
                           emit=lambda q, h=h: q.dma_start(
                               out=hd["contrib"].ap()[h], in_=cbuf[:, h % 2, :]))
                else:
                    s.step("sp", r=[("S", hp)], dma=True,
                           emit=lambda q, h=h, Ssb=Ssb: q.dma_start(out=hd["sfin"].ap()[h], in_=Ssb[:, :]))
        if fused:
            s.step("pool", r=[("ctb", h) for h in range(NH)], w=[("rstD",)], cc=True,
                   emit=lambda q: q.collective_compute(
                       "AllReduce", ALU.add, [[0, 1], [2, 3], [4, 5], [6, 7]],
                       [hd["contrib"].ap().opt()], [hd["rst"].ap().opt()]))
        s.barrier()


def build_l1pre():
    nc = bass.Bass("TRN2", target_bir_lowering=False)
    din = lambda n, sh, dt: nc.dram_tensor(n, sh, dt, kind="ExternalInput")
    x1_d = din("x1", [T, D], F32)
    hd = {
        "wk": din("l1wk", [NH, 128, FC, 128], F32),
        "wv": din("l1wv", [4, 128, FC, 512], F32),
        "cos": din("cos2", [128, T], BF16),
        "sin": din("sin2", [128, T], BF16),
        "dk": din("dkcol", [128, NH], F32),
        "sfin": nc.dram_tensor("S_fin", [NH, 128, 128], F32, kind="ExternalOutput"),
        "krot_st": nc.dram_tensor("krot", [NH, 128, T], BF16, kind="ExternalOutput"),
        "vst": nc.dram_tensor("vst", [4, 8, 128, 512], BF16, kind="ExternalOutput"),
        "pst": nc.dram_tensor("pst", [NH, 8, 128, 128], BF16, kind="ExternalOutput"),
    }
    s, pb, xres, xnT, id_s = _mk_common(nc)
    emit_stage0(nc, s, x1_d.ap(), xres, xnT, id_s)
    emit_l1pre(nc, s, pb, xnT, id_s, hd)
    return nc


def emit_l1g1(nc, s, pb, xnT, hd):
    g1 = nc.alloc_sbuf_tensor("m_g1", [P, T], BF16)
    wg2_s = nc.alloc_sbuf_tensor("m_wg2", [P, D], BF16)
    s.step("pool", w=[("wg2",)], dma=True,
           emit=lambda q: q.dma_start(out=wg2_s[:, :], in_=hd["wg2"].ap()[:, :]))
    with ExitStack() as ph:
        wg1_s = ph.enter_context(nc.sbuf_tensor("m_wg1", [P, FC, 128], BF16))
        s.step("pool", w=[("wg1",)], dma=True,
               emit=lambda q: q.dma_start(out=wg1_s[:, :, :], in_=hd["wg1"].ap()[:, :, :]))
        for th in range(2):
            s.step("pe", r=XNKEYS + [("wg1",)], w=[("ps", th)],
                   emit=lambda q, th=th: [
                       q.matmul(pb[th][:, :], wg1_s[:, hc, :], xnT[:, hc, th * 512:(th + 1) * 512],
                                start=(hc == 0), stop=(hc == 15)) for hc in range(FC)])
            s.step("dve", r=[("ps", th)], w=[("g1",)],
                   emit=lambda q, th=th: q.tensor_copy(g1[:, th * 512:(th + 1) * 512], pb[th][:, :]))
        s.barrier()
    return g1, wg2_s


def emit_l1ffn(nc, s, xres, xnT, pb, h_st, wgath_d, wgath_key=None, bg=None):
    rk = [wgath_key] if wgath_key else []
    def wg_load(s2, ic):
        s2.step("sp", r=rk, w=[("wg", ic % 2)], dma=True,
                emit=lambda q, ic=ic: q.dma_start(
                    out=s2._wg[:, ic % 2, :, :],
                    in_=wgath_d.ap()[4 * ic:4 * ic + 4].rearrange("b p j -> p b j")))
    def wu_load(s2, ic):
        s2.step("sp", r=rk, w=[("wu", ic % 2)], dma=True,
                emit=lambda q, ic=ic: q.dma_start(
                    out=s2._wu[:, ic % 2, :, :],
                    in_=wgath_d.ap()[172 + 4 * ic:172 + 4 * ic + 4].rearrange("b p j -> p b j")))
    def wd_load(s2, ic, oc):
        s2.step("act", r=rk, w=[("wd", ic % 3)], dma=True,
                emit=lambda q, ic=ic, oc=oc: q.dma_start(
                    out=s2._wd[:, ic % 3, :], in_=wgath_d.ap()[344 + 4 * ic + oc]))
    wcfg = {
        "gu_shape": [P, 2, 4, 512],
        "gu_slice": lambda ws, slot, hc: ws[:, slot, hc // 4, (hc % 4) * 128:((hc % 4) + 1) * 128],
        "wg_load": wg_load, "wu_load": wu_load, "wd_load": wd_load,
    }
    emit_qffn(nc, s, xres, xnT, pb, h_st, NIC1, False, wcfg, pfx="q1", bg=bg)


def l1ffnA_units(nc, s, ph, xnT, pb, h_st, wgath_d):
    sb = lambda n, sh, dt: ph.enter_context(nc.sbuf_tensor(n, sh, dt))
    wg_s = sb("q1_wg", [P, 2, 4, 512], BF16)
    wu_s = sb("q1_wu", [P, 2, 4, 512], BF16)
    h_s = sb("q1_h", [P, 3, T], BF16)
    sg = sb("q1_sgb", [P, 2, 512], BF16)
    su = sb("q1_sub", [P, 2, 512], BF16)
    gsl = lambda ws, slot, hc: ws[:, slot, hc // 4, (hc % 4) * 128:((hc % 4) + 1) * 128]
    def wg_load(ic):
        s.step("pool", r=[("wgath",)], w=[("wg", ic % 2)], dma=True,
               emit=lambda q, ic=ic: q.dma_start(
                   out=wg_s[:, ic % 2, :, :],
                   in_=wgath_d.ap()[4 * ic:4 * ic + 4].rearrange("b p j -> p b j")))
    def wu_load(ic):
        s.step("pool", r=[("wgath",)], w=[("wu", ic % 2)], dma=True,
               emit=lambda q, ic=ic: q.dma_start(
                   out=wu_s[:, ic % 2, :, :],
                   in_=wgath_d.ap()[172 + 4 * ic:172 + 4 * ic + 4].rearrange("b p j -> p b j")))
    wg_load(0)
    wu_load(0)
    def mmgu(q, ws, ic, th, bank):
        return [q.matmul(pb[bank][:, :], gsl(ws, ic % 2, hc),
                         xnT[:, hc, th * 512:(th + 1) * 512],
                         start=(hc == 0), stop=(hc == 15)) for hc in range(FC)]
    def u_ep(th, ic):
        s.step("act", r=[("ps", th)], w=[("sg", th)],
               emit=lambda q, th=th: q.activation(
                   out=sg[:, th, :], in_=pb[th][:, :], func=AF.Silu))
        s.step("act", r=[("ps", 2 + th)], w=[("su", th)],
               emit=lambda q, th=th: q.activation(
                   out=su[:, th, :], in_=pb[2 + th][:, :], func=AF.Copy))
        s.step("dve", r=[("sg", th), ("su", th)], w=[("h", ic % 3)],
               emit=lambda q, ic=ic, th=th: q.tensor_tensor(
                   h_s[:, ic % 3, th * 512:(th + 1) * 512],
                   sg[:, th, :], su[:, th, :], ALU.mult))
        if th == 1:
            s.step("sp", r=[("h", ic % 3)], w=[("hst", ic)], dma=True,
                   emit=lambda q, ic=ic: q.dma_start(out=h_st.ap()[ic], in_=h_s[:, ic % 3, :]))
    def gen():
        for ic in range(NIC1):
            def u_g(ic=ic):
                for th in range(2):
                    s.step("pe", r=XNKEYS + [("wg", ic % 2)], w=[("ps", th)],
                           emit=lambda q, th=th, ic=ic: mmgu(q, wg_s, ic, th, th))
                if ic + 1 < NIC1:
                    wg_load(ic + 1)
            yield u_g
            def u_u(ic=ic):
                for th in range(2):
                    s.step("pe", r=XNKEYS + [("wu", ic % 2)], w=[("ps", 2 + th)],
                           emit=lambda q, th=th, ic=ic: mmgu(q, wu_s, ic, th, 2 + th))
                if ic + 1 < NIC1:
                    wu_load(ic + 1)
            yield u_u
            yield lambda ic=ic: u_ep(0, ic)
            yield lambda ic=ic: u_ep(1, ic)
    return gen()


def emit_l1ffnB(nc, s, xres, pb, h_st, wgath_d):
    with ExitStack() as ph:
        sb = lambda n, sh, dt: ph.enter_context(nc.sbuf_tensor(n, sh, dt))
        pext = [ph.enter_context(nc.psum_tensor(f"q1px{i}", [128, 512], F32)) for i in range(2)]
        pb8 = list(pb) + pext
        wd_s = sb("q1_wd", [P, 3, 512], BF16)
        hl = sb("q1_hl", [P, 3, T], BF16)
        for oc in range(4):
            for ic in range(NIC1):
                s.step("sp", r=[("hst", ic)], w=[("hl", ic % 3)], dma=True,
                       emit=lambda q, ic=ic: q.dma_start(out=hl[:, ic % 3, :], in_=h_st.ap()[ic]))
                s.step("act", r=[("wgath",)], w=[("wd", ic % 3)], dma=True,
                       emit=lambda q, ic=ic, oc=oc: q.dma_start(
                           out=wd_s[:, ic % 3, :], in_=wgath_d.ap()[344 + 4 * ic + oc]))
                s.step("pe", r=[("hl", ic % 3), ("wd", ic % 3)],
                       w=[("ps", tt2) for tt2 in range(8)],
                       emit=lambda q, ic=ic: [
                           q.matmul(pb8[tt2][:, :], hl[:, ic % 3, tt2 * 128:(tt2 + 1) * 128],
                                    wd_s[:, ic % 3, :], start=(ic == 0), stop=(ic == NIC1 - 1))
                           for tt2 in range(8)])
            for tt in range(8):
                s.step("dve", r=[("ps", tt)], w=[("xres", tt)],
                       emit=lambda q, tt=tt, oc=oc: q.tensor_tensor(
                           xres[:, tt, oc * 512:(oc + 1) * 512],
                           xres[:, tt, oc * 512:(oc + 1) * 512], pb8[tt][:, :], ALU.add))
        s.barrier()


def emit_l1attn(nc, s, pb, xres, xnT, id_s, g1, wg2_s, hd, fused=False):
    krot_d, vst_d, pst_d = hd["krot_st"], hd["vst"], hd["pst"]
    wq_d, wo_d, nw_d = hd["wq"], hd["wo"], hd["nw"]
    cos_d, sin_d, dq_d, D_d, lf_d = hd["cos"], hd["sin"], hd["dq"], hd["Dm"], hd["lf"]
    xg_st, oft_st = hd["xg_st"], hd["oft_st"]
    mscols = nc.alloc_sbuf_tensor("m_ms", [P, 128], F32)
    rstd = nc.alloc_sbuf_tensor("m_rstd", [P, 128], F32)
    with ExitStack() as ph:
        sb = lambda n, sh, dt: ph.enter_context(nc.sbuf_tensor(n, sh, dt))
        tp = [ph.enter_context(nc.psum_tensor(f"mtp{i}", [128, 1024], BF16)) for i in range(2)]
        R0 = _mk_rope_bufs(nc, sb, "mr0", pb, tag=0)
        R1 = _mk_rope_bufs(nc, sb, "mr1", pb, tag=1, shared=R0)
        R = R0
        wq_s = sb("m_wq", [P, 2, FC, 128], BF16)
        qrot2 = sb("m_qrot", [P, 2, T], BF16)
        krot2 = sb("m_krot", [P, 4, T], BF16)
        vg = sb("m_vg", [P, 2, 8, 512], BF16)
        pstb2 = sb("m_pst", [P, 4, 8, 128], BF16)
        Rp = sb("m_Rp", [P, NH, 128], BF16)
        D_sb = sb("m_D", [P, NH, 128], BF16)
        dq_sb = sb("m_dq", [P, NH, 128], BF16)
        lf_s = sb("m_lf", [P, 128], F32)
        CPD = 4
        qt = sb("m_qt", [P, CPD, 128], BF16)
        suseb = sb("m_suseb", [P, CPD, 128], BF16)
        scd = sb("m_scd", [P, CPD, 128], BF16)
        scdT = sb("m_scdT", [P, CPD, 128], BF16)
        sgf = sb("m_sgf", [P, T], F32)
        sgb = sb("m_sgb", [P, 2, T], BF16)
        sqt = sb("m_sqt", [P, CPD, 128], F32)
        xgh = sb("m_xgh", [P, 2, 8, 128], BF16)

        s.step("sp", w=[("cos2",)], dma=True, emit=lambda q: q.dma_start(out=R["cos2"][:, :], in_=cos_d.ap()[:, :]))
        s.step("sp", w=[("sin2",)], dma=True, emit=lambda q: q.dma_start(out=R["sin2"][:, :], in_=sin_d.ap()[:, :]))
        rsrc = hd["rst"] if fused else hd["R"]
        s.step("pool", r=([("rstD",)] if fused else []), w=[("Rp",)], dma=True,
               emit=lambda q: q.dma_start(out=Rp[:, :, :], in_=rsrc.ap().rearrange("h p j -> p h j")))
        s.step("sp", w=[("Ds",)], dma=True,
               emit=lambda q: q.dma_start(out=D_sb[:, :, :], in_=D_d.ap().rearrange("h p j -> p h j")))
        s.step("sp", w=[("dq",)], dma=True,
               emit=lambda q: q.dma_start(out=dq_sb[:, :, :], in_=dq_d.ap().rearrange("h p j -> p h j")))
        s.step("sp", w=[("lf",)], dma=True, emit=lambda q: q.dma_start(out=lf_s[:, :], in_=lf_d.ap()[:, :]))

        def load_wq(h):
            s.step("pool", w=[("wqw", h % 2)], dma=True,
                   emit=lambda q, h=h: q.dma_start(out=wq_s[:, h % 2, :, :], in_=wq_d.ap()[h]))

        def head_prep_steps(h):
            hp = h % 2
            for th in range(2):
                yield lambda th=th, hp=hp: s.step(
                    "pe", r=XNKEYS + [("wqw", hp)], w=[("ps", th)],
                    emit=lambda q: [
                        q.matmul(pb[th][:, :], wq_s[:, hp, hc, :],
                                 xnT[:, hc, th * 512:(th + 1) * 512],
                                 start=(hc == 0), stop=(hc == 15)) for hc in range(FC)])
            Rh = R0 if h % 2 == 0 else R1
            yield lambda hp=hp, Rh=Rh: (_rope1(nc, s, Rh, 0),
                                        load_wq(h + 1) if h + 1 < 16 else None)
            yield lambda hp=hp, Rh=Rh: _rope2(nc, s, Rh, qrot2[:, hp, :], ("qrot", hp))
            for b in range(2):
                yield lambda b=b, h=h: s.step(
                    "pe", r=[("g1",), ("wg2",)], w=[("ps", b)],
                    emit=lambda q: [
                        q.matmul(pb[b][:, i * 128:(i + 1) * 128],
                                 g1[:, (b * 4 + i) * 128:(b * 4 + i + 1) * 128],
                                 wg2_s[:, h * 128:(h + 1) * 128],
                                 start=(i == 0), stop=(i == 3)) for i in range(4)])
            def tanhts(hp=hp):
                for b in range(2):
                    s.step("act", r=[("ps", b)], w=[("sgf",)],
                           emit=lambda q, b=b: q.activation(
                               out=sgf[:, b * 512:(b + 1) * 512], in_=pb[b][:, :],
                               func=AF.Tanh, scale=0.5))
                s.step("dve", r=[("sgf",)], w=[("sgb", hp)],
                       emit=lambda q: q.tensor_scalar(
                           sgb[:, hp, :], sgf[:, :], 0.5, 0.5, ALU.mult, ALU.add))
            yield tanhts
        load_wq(0)
        def make_stages(h, hs):
            hp = h % 2
            h4 = h % 4
            g2 = (h // 4) % 2
            qrot = qrot2[:, hp, :]
            krot = krot2[:, h4, :]
            sb_b = 2 + hp
            ob = 4 + hp
            tpb = hp
            st = {}
            def sQ(c):
                cp = hp * 2 + c % 2
                s.step("dve", r=[("qrot", hp), ("dq",)], w=[("qt", cp)],
                       emit=lambda q: q.tensor_tensor(
                           qt[:, cp, :], qrot[:, c * 128:(c + 1) * 128], dq_sb[:, h, :], ALU.mult))
                s.step("dve", r=[("Rp",), ("lf",), ("pstb", h4)], w=[("suseb", cp)],
                       emit=lambda q: q.scalar_tensor_tensor(
                           suseb[:, cp, :], Rp[:, h, :], lf_s[:, h * 8 + c:h * 8 + c + 1],
                           pstb2[:, h4, c, :], ALU.mult, ALU.add))
            def sS(c):
                s.step("pe", r=[("qrot", hp), ("krot", h4)], w=[("ps", sb_b)],
                       emit=lambda q: q.matmul(
                           pb[sb_b][:, 0:128], qrot[:, c * 128:(c + 1) * 128],
                           krot[:, c * 128:(c + 1) * 128], start=True, stop=True))
            def sD(c):
                cp = hp * 2 + c % 2
                s.step("dve", r=[("ps", sb_b), ("Ds",)], w=[("scd", cp)],
                       emit=lambda q: q.tensor_tensor(
                           scd[:, cp, :], pb[sb_b][:, 0:128], D_sb[:, h, :], ALU.mult))
            def sT(c):
                cp = hp * 2 + c % 2
                s.step("pe", r=[("scd", cp), ("id",)], w=[("ps", 6 + tpb)],
                       emit=lambda q: q.transpose(tp[tpb][:, 0:128], scd[:, cp, :], id_s[:, :]))
                s.step("act", r=[("ps", 6 + tpb)], w=[("scdT", cp)],
                       emit=lambda q: q.copy(scdT[:, cp, :], tp[tpb][:, 0:128]))
            def sAV(c):
                cp = hp * 2 + c % 2
                s.step("pe", r=[("scdT", cp), ("vg", g2), ("qt", cp), ("suseb", cp)],
                       w=[("ps", ob)],
                       emit=lambda q: [
                           q.matmul(pb[ob][:, 0:128], scdT[:, cp, :],
                                    vg[:, g2, c, hs * 128:(hs + 1) * 128], start=True, stop=False),
                           q.matmul(pb[ob][:, 0:128], qt[:, cp, :], suseb[:, cp, :],
                                    start=False, stop=True)])
            def sX(c):
                cp = hp * 2 + c % 2
                s.step("dve", r=[("ps", ob), ("sgb", hp)], w=[("xgh", hp, c)],
                       emit=lambda q: q.tensor_tensor(
                           xgh[:, hp, c, :], pb[ob][:, 0:128],
                           sgb[:, hp, c * 128:(c + 1) * 128], ALU.mult))
                s.step("act", r=[("xgh", hp, c)], w=[("sqt", cp), ("ms",)],
                       emit=lambda q: q.activation(
                           out=sqt[:, cp, :], in_=xgh[:, hp, c, :], func=AF.Square,
                           accum_out=mscols[:, h * 8 + c:h * 8 + c + 1]))
            st.update(sQ=sQ, sS=sS, sD=sD, sT=sT, sAV=sAV, sX=sX)
            return st

        def load_vg(hg):
            s.step("sp", w=[("vg", hg % 2)], dma=True,
                   emit=lambda q, hg=hg: q.dma_start(
                       out=vg[:, hg % 2, :, :], in_=vst_d.ap()[hg].rearrange("tc p j -> p tc j")))
        def load_kp(h):
            h4 = h % 4
            s.step("sp", w=[("krot", h4)], dma=True,
                   emit=lambda q, h=h, h4=h4: q.dma_start(out=krot2[:, h4, :], in_=krot_d.ap()[h]))
            s.step("sp", w=[("pstb", h4)], dma=True,
                   emit=lambda q, h=h, h4=h4: q.dma_start(
                       out=pstb2[:, h4, :, :], in_=pst_d.ap()[h].rearrange("c p j -> p c j")))
        load_vg(0)
        load_kp(0)
        load_kp(1)
        for hg in range(4):
            for pr in range(2):
                h0 = hg * 4 + 2 * pr
                h1 = h0 + 1
                hs0, hs1 = 2 * pr, 2 * pr + 1
                if h0 == 0:
                    for st in head_prep_steps(0):
                        st()
                    for st in head_prep_steps(1):
                        st()
                nxt = []
                if h0 + 2 < 16:
                    u2 = list(head_prep_steps(h0 + 2))
                    u3 = list(head_prep_steps(h0 + 3)) if h0 + 3 < 16 else []
                    while u2 or u3:
                        if u2:
                            nxt.append(u2.pop(0))
                        if u3:
                            nxt.append(u3.pop(0))
                st0 = make_stages(h0, hs0)
                st1 = make_stages(h1, hs1)
                for i in range(10):
                    for st in (st0, st1):
                        if i - 2 >= 0:
                            st["sX"](i - 2)
                        if i < 8:
                            st["sQ"](i)
                            st["sS"](i)
                        if 1 <= i <= 8:
                            st["sD"](i - 1)
                            st["sT"](i - 1)
                            st["sAV"](i - 1)
                    if i == 4 and h0 + 2 < 16:
                        load_kp(h0 + 2)
                        load_kp(h0 + 3)
                        if pr == 1 and hg < 3:
                            load_vg(hg + 1)
                    if 1 <= i <= 7 and nxt:
                        nxt.pop(0)()
                        if nxt:
                            nxt.pop(0)()
                while nxt:
                    nxt.pop(0)()
                for h in (h0, h1):
                    hp = h % 2
                    s.step("sp", r=[("xgh", hp, c) for c in range(8)], w=[("xgst", h)], dma=True,
                           emit=lambda q, h=h, hp=hp: q.dma_start(
                               out=xg_st.ap()[h].rearrange("c p j -> p c j"), in_=xgh[:, hp, :, :]))
        s.barrier()
    s.step("act", r=[("ms",), ("cst",)], w=[("rstd",)],
           emit=lambda q: q.activation(out=rstd[:, :], in_=mscols[:, :], func=AF.Sqrt,
                                       scale=1.0 / 128.0, bias=nc._cst[:, 1:2]))
    s.step("dve", r=[("rstd",)], w=[("rstd",)],
           emit=lambda q: q.reciprocal(rstd[:, :], rstd[:, :]))
    with ExitStack() as ph:
        sb = lambda n, sh, dt: ph.enter_context(nc.sbuf_tensor(n, sh, dt))
        tp = [ph.enter_context(nc.psum_tensor(f"p2tp{i}", [128, 1024], BF16)) for i in range(2)]
        nw_s = sb("m_nw", [P, D], BF16)
        ofc = sb("m_ofc", [P, 4, 128], BF16)
        s.step("pool", w=[("nw",)], dma=True,
               emit=lambda q: q.dma_start(out=nw_s[:, :], in_=nw_d.ap()[:, :]))
        xgh2b = ph.enter_context(nc.sbuf_tensor("m_xgh2b", [P, 2, 8, 128], BF16))
        ofthb = ph.enter_context(nc.sbuf_tensor("m_ofthb", [P, 2, T], BF16))
        def p2_stt(h, c):
            hp, cp = h % 2, c % 4
            s.step("dve", r=[("xgh2", hp), ("rstd",), ("nw",)], w=[("ofc", cp)],
                   emit=lambda q: q.scalar_tensor_tensor(
                       ofc[:, cp, :], xgh2b[:, hp, c, :], rstd[:, h * 8 + c:h * 8 + c + 1],
                       nw_s[:, h * 128:(h + 1) * 128], ALU.mult, ALU.mult))
        def p2_tr(h, c):
            hp, cp = h % 2, c % 4
            s.step("pe", r=[("ofc", cp), ("id",)], w=[("ps", 6 + c % 2)],
                   emit=lambda q: q.transpose(tp[c % 2][:, 0:128], ofc[:, cp, :], id_s[:, :]))
            s.step("dve", r=[("ps", 6 + c % 2)], w=[("ofth", hp)],
                   emit=lambda q: q.tensor_copy(
                       ofthb[:, hp, c * 128:(c + 1) * 128], tp[c % 2][:, 0:128]))
        for h in range(NH):
            hp = h % 2
            s.step("sp", r=[("xgst", h)], w=[("xgh2", hp)], dma=True,
                   emit=lambda q, h=h, hp=hp: q.dma_start(
                       out=xgh2b[:, hp, :, :], in_=xg_st.ap()[h].rearrange("c p j -> p c j")))
            for i in range(9):
                if i < 8:
                    p2_stt(h, i)
                if i >= 1:
                    p2_tr(h, i - 1)
            s.step("sp", r=[("ofth", hp)], w=[("oft", h)], dma=True,
                   emit=lambda q, h=h, hp=hp: q.dma_start(out=oft_st.ap()[h], in_=ofthb[:, hp, :]))
        s.barrier()
    with ExitStack() as ph:
        sb = lambda n, sh, dt: ph.enter_context(nc.sbuf_tensor(n, sh, dt))
        wo_s = sb("m_wo", [P, 2, FC, 512], BF16)
        otile = sb("m_otile", [P, 2, FC, 128], BF16)
        oftk = [("oft", h) for h in range(NH)]
        for ps2 in range(2):
            for o2 in range(2):
                s.step("pool", w=[("wow", o2)], dma=True,
                       emit=lambda q, ps2=ps2, o2=o2: q.dma_start(
                           out=wo_s[:, o2, :, :], in_=wo_d.ap()[ps2 * 2 + o2]))
            for tt in range(8):
                s.step("sp", r=oftk, w=[("otile", tt % 2)], dma=True,
                       emit=lambda q, tt=tt: q.dma_start(
                           out=otile[:, tt % 2, :, :],
                           in_=oft_st.ap()[:, :, tt * 128:(tt + 1) * 128].rearrange("h d t -> d h t")))
                for o2 in range(2):
                    bank = 2 * (tt % 2) + o2
                    s.step("pe", r=[("otile", tt % 2), ("wow", o2)], w=[("ps", bank)],
                           emit=lambda q, tt=tt, o2=o2, bank=bank: [
                               q.matmul(pb[bank][:, :], otile[:, tt % 2, fc, :],
                                        wo_s[:, o2, fc, :], start=(fc == 0), stop=(fc == 15))
                               for fc in range(FC)])
                    s.step("dve", r=[("ps", bank)], w=[("xres", tt)],
                           emit=lambda q, tt=tt, ps2=ps2, o2=o2, bank=bank: q.tensor_tensor(
                               xres[:, tt, (ps2 * 2 + o2) * 512:(ps2 * 2 + o2 + 1) * 512],
                               xres[:, tt, (ps2 * 2 + o2) * 512:(ps2 * 2 + o2 + 1) * 512],
                               pb[bank][:, :], ALU.add))
        s.barrier()


def build_l1main():
    nc = bass.Bass("TRN2", target_bir_lowering=False)
    din = lambda n, sh, dt: nc.dram_tensor(n, sh, dt, kind="ExternalInput")
    x1_d = din("x1", [T, D], F32)
    hd = {
        "krot_st": din("krot", [NH, 128, T], BF16),
        "vst": din("vst", [4, 8, 128, 512], BF16),
        "pst": din("pst", [NH, 8, 128, 128], BF16),
        "R": din("Rstate", [NH, 128, 128], F32),
        "wq": din("l1wq", [NH, 128, FC, 128], F32),
        "wo": din("l1wo", [4, 128, FC, 512], F32),
        "wg1": din("wg1_s", [128, FC, 128], F32),
        "wg2": din("wg2_s", [128, D], F32),
        "nw": din("nwrep", [128, D], F32),
        "cos": din("cos2", [128, T], BF16),
        "sin": din("sin2", [128, T], BF16),
        "dq": din("dqrep", [NH, 128, 128], BF16),
        "Dm": din("D_s", [NH, 128, 128], BF16),
        "lf": din("lamflag", [128, 128], F32),
    }
    wgath_d = din("w_gath", [8 * SHARD, 128, 512], BF16)
    xout = nc.dram_tensor("xout", [T, D], F32, kind="ExternalOutput")
    h_st = nc.dram_tensor("h_st", [NIC1, 128, T], BF16)
    hd["xg_st"] = nc.dram_tensor("xg_st", [NH, 8, 128, 128], BF16)
    hd["oft_st"] = nc.dram_tensor("oft_st", [NH, 128, T], BF16)

    s, pb, xres, xnT, id_s = _mk_common(nc)
    emit_stage0(nc, s, x1_d.ap(), xres, xnT, id_s)
    g1, wg2_s = emit_l1g1(nc, s, pb, xnT, hd)
    emit_l1ffn(nc, s, xres, xnT, pb, h_st, wgath_d)
    emit_l1attn(nc, s, pb, xres, xnT, id_s, g1, wg2_s, hd)
    _store_x(nc, s, xres, xout)
    return nc


class Deq:
    def __init__(self, nc, s, idx_d, lut_d, wsh_d, wgath_d, sb=None):
        self.nc, self.s = nc, s
        self.idx_d, self.wsh_d, self.wgath_d = idx_d, wsh_d, wgath_d
        if sb is None:
            sb = lambda n, sh, dt: nc.alloc_sbuf_tensor(n, sh, dt)
        self.lut_s = sb("dq_lut", [P, 16], F32)
        self.ib = sb("dq_ib", [P, 2, 512], I32)
        self.wb = sb("dq_wb", [P, 512], BF16)
        self.tb = sb("dq_tb", [P, 512], BF16)
        self.n = 0
        self.piece = 0
        self.triggered = False
        s.step("sp", w=[("dqlut",)], dma=True,
               emit=lambda q: q.dma_start(out=self.lut_s[:, :], in_=lut_d.ap()[:, :]))
        s.step("sp", w=[("dqib", 0)], dma=True,
               emit=lambda q: q.dma_start(out=self.ib[:, 0, :], in_=idx_d.ap()[0]))

    def bg(self):
        if self.n >= SHARD:
            return
        n = self.n
        self.n += 1
        s, ib, wb, tb, lut_s = self.s, self.ib, self.wb, self.tb, self.lut_s
        slot = n % 2
        def chain(q, slot=slot):
            ins = [q.tensor_scalar(wb[:, :], ib[:, slot, :], 0.0, lut_s[:, 0:1],
                                   ALU.is_equal, ALU.mult)]
            for j in range(1, 16):
                ins.append(q.tensor_scalar(tb[:, :], ib[:, slot, :], float(j),
                                           lut_s[:, j:j + 1], ALU.is_equal, ALU.mult))
                ins.append(q.tensor_tensor(wb[:, :], wb[:, :], tb[:, :], ALU.add))
            return ins
        s.step("dve", r=[("dqib", slot), ("dqlut",)], w=[("dqwb",), ("dqtb",)], emit=chain)
        if n + 1 < SHARD:
            s.step("sp", w=[("dqib", (n + 1) % 2)], dma=True,
                   emit=lambda q, n=n: q.dma_start(out=self.ib[:, (n + 1) % 2, :],
                                                   in_=self.idx_d.ap()[n + 1]))
        s.step("sp", r=[("dqwb",)], w=[("dqst", n)], dma=True,
               emit=lambda q, n=n: q.dma_start(out=self.wsh_d.ap()[n], in_=wb[:, :]))
        if self.n >= SHARD:
            self.trigger()

    def trigger(self):
        if self.triggered:
            return
        self.triggered = True
        self.s.step("pool", r=[("dqst", n) for n in range(SHARD)], w=[("wgath",)], cc=True,
                    emit=lambda q: q.collective_compute(
                        "AllGather", ALU.bypass, [[0, 1, 2, 3, 4, 5, 6, 7]],
                        [self.wsh_d.ap().opt()], [self.wgath_d.ap().opt()]))

    def finish(self):
        while self.n < SHARD:
            self.bg()
        self.trigger()


def build_fused():
    nc = bass.Bass("TRN2", target_bir_lowering=False)
    din = lambda n, sh, dt: nc.dram_tensor(n, sh, dt, kind="ExternalInput")
    x_own = din("x_own", [T, D], F32)
    x_halo = din("x_halo", [512, D], F32)
    dram0 = {
        "wq": din("wq_h", [NH, 128, FC, 128], F32),
        "wk": din("wk_h", [NH, 128, FC, 128], F32),
        "wv": din("wv_g", [4, 128, FC, 512], F32),
        "wo": din("wo_oc", [4, 128, FC, 512], F32),
        "masks": din("masks", [8, 128, 640], BF16),
    }
    gT_d = din("gT_h", [NIC0, 128, FC, 128], U8)
    uT_d = din("uT_h", [NIC0, 128, FC, 128], U8)
    dT_d = din("dT_h", [NIC0, 4, 128, 512], U8)
    gs_d = din("gs_c", [128, NIC0], F32)
    us_d = din("us_c", [128, NIC0], F32)
    dsrep_d = din("dsrep", [128, D], F32)
    idx_d = din("l1idx_shard", [SHARD, 128, 512], I32)
    lut_d = din("lutrep", [128, 16], F32)
    hd = {
        "wk": din("l1wk", [NH, 128, FC, 128], F32),
        "wv": din("l1wv", [4, 128, FC, 512], F32),
        "wq": din("l1wq", [NH, 128, FC, 128], F32),
        "wo": din("l1wo", [4, 128, FC, 512], F32),
        "wg1": din("wg1_s", [128, FC, 128], F32),
        "wg2": din("wg2_s", [128, D], F32),
        "nw": din("nwrep", [128, D], F32),
        "cos": din("cos2", [128, T], BF16),
        "sin": din("sin2", [128, T], BF16),
        "dk": din("dkcol", [128, NH], F32),
        "dq": din("dqrep", [NH, 128, 128], BF16),
        "Dm": din("D_s", [NH, 128, 128], BF16),
        "lf": din("lamflag", [128, 128], F32),
        "flag2": din("flag2c", [128, 1], F32),
    }
    xout = nc.dram_tensor("xout", [T, D], F32, kind="ExternalOutput")
    attn_st = nc.dram_tensor("attn_st", [NH, 128, T], BF16)
    h_st = nc.dram_tensor("h_st", [NIC0, 128, T], BF16)
    wsh_d = nc.dram_tensor("wshard", [SHARD, 128, 512], BF16)
    wgath_d = nc.dram_tensor("w_gath", [8 * SHARD, 128, 512], BF16, addr_space="Shared")
    hd["krot_st"] = nc.dram_tensor("krot", [NH, 128, T], BF16)
    hd["vst"] = nc.dram_tensor("vst", [4, 8, 128, 512], BF16)
    hd["pst"] = nc.dram_tensor("pst", [NH, 8, 128, 128], BF16)
    hd["contrib"] = nc.dram_tensor("contrib", [NH, 128, 128], F32)
    hd["rst"] = nc.dram_tensor("rst", [NH, 128, 128], F32)
    hd["xg_st"] = nc.dram_tensor("xg_st", [NH, 8, 128, 128], BF16)
    hd["oft_st"] = nc.dram_tensor("oft_st", [NH, 128, T], BF16)

    s, pb, xres, xnT, id_s = _mk_common(nc)
    phD = ExitStack()
    sbD = lambda n, sh, dt: phD.enter_context(nc.sbuf_tensor(n, sh, dt))
    deq = Deq(nc, s, idx_d, lut_d, wsh_d, wgath_d, sbD)
    with ExitStack() as ph0:
        haloT = ph0.enter_context(nc.sbuf_tensor("haloT", [P, FC, 512], BF16))
        emit_stage0(nc, s, x_own.ap(), xres, xnT, id_s, x_halo.ap(), haloT)
        emit_l0_attn(nc, s, xres, xnT, haloT, id_s, pb, attn_st, dram0)
    def wg_load(s2, ic):
        s2.step("pool", w=[("wg", ic % 2)], dma=True,
                emit=lambda q, ic=ic: q.dma_start(out=s2._wg[:, ic % 2, :, :], in_=gT_d.ap()[ic]))
    def wu_load(s2, ic):
        s2.step("pool", w=[("wu", ic % 2)], dma=True,
                emit=lambda q, ic=ic: q.dma_start(out=s2._wu[:, ic % 2, :, :], in_=uT_d.ap()[ic]))
    def wd_load(s2, ic, oc):
        s2.step("act", w=[("wdr", ic % 3)], dma=True,
                emit=lambda q, ic=ic, oc=oc: q.dma_start(out=s2._wdr[:, ic % 3, :], in_=dT_d.ap()[ic, oc]))
        s2.step("dve", r=[("wdr", ic % 3)], w=[("wd", ic % 3)],
                emit=lambda q, ic=ic: q.tensor_copy(s2._wd[:, ic % 3, :], s2._wdr[:, ic % 3, :]))
    wcfg = {
        "gu_shape": [P, 2, FC, 128],
        "gu_slice": lambda ws, slot, hc: ws[:, slot, hc, :],
        "wg_load": wg_load, "wu_load": wu_load, "wd_load": wd_load,
        "gs": gs_d, "us": us_d, "dsrep": dsrep_d, "wd_raw": True,
    }
    emit_qffn(nc, s, xres, xnT, pb, h_st, NIC0, True, wcfg, pfx="q0", bg=deq.bg,
              after_a=deq.finish)
    phD.close()
    emit_stage0(nc, s, None, xres, xnT, id_s, pfx="l2", load=False)
    g1, wg2_s = emit_l1g1(nc, s, pb, xnT, hd)
    with ExitStack() as phA:
        gen = l1ffnA_units(nc, s, phA, xnT, pb, h_st, wgath_d)
        def pump():
            try:
                next(gen)()
            except StopIteration:
                pass
        emit_l1pre(nc, s, pb, xnT, id_s, hd, fused=True, bg=pump)
        for u in gen:
            u()
    emit_l1ffnB(nc, s, xres, pb, h_st, wgath_d)
    emit_l1attn(nc, s, pb, xres, xnT, id_s, g1, wg2_s, hd, fused=True)
    _store_x(nc, s, xres, xout)
    return nc


FUSED = True


def host_prepare(inp):
    H = {}
    f32 = np.float32
    cc = np.ascontiguousarray
    wqkv = np.asarray(inp["L0_attn_w_qkv"], f32).reshape(3, NH, 128, FC, 128)
    H["wq_h"] = cc(wqkv[0].transpose(0, 3, 2, 1))
    H["wk_h"] = cc(wqkv[1].transpose(0, 3, 2, 1))
    H["wv_g"] = cc(wqkv[2].reshape(4, 4, 128, FC, 128).transpose(0, 4, 3, 1, 2)
                   .reshape(4, 128, FC, 512))
    H["wo_oc"] = cc(np.asarray(inp["L0_attn_w_out"], f32).reshape(4, 512, FC, 128)
                    .transpose(0, 3, 2, 1))
    H["gT_h"] = cc(np.asarray(inp["L0_ffn_gate_w"], np.int32).astype(np.uint8)
                   .reshape(NIC0, 128, FC, 128).transpose(0, 3, 2, 1))
    H["uT_h"] = cc(np.asarray(inp["L0_ffn_up_w"], np.int32).astype(np.uint8)
                   .reshape(NIC0, 128, FC, 128).transpose(0, 3, 2, 1))
    H["dT_h"] = cc(np.asarray(inp["L0_ffn_down_w"], np.int32).astype(np.uint8)
                   .reshape(4, 512, NIC0, 128).transpose(2, 0, 3, 1))
    H["gs_c"] = cc(np.asarray(inp["L0_ffn_gate_scale"], f32).reshape(NIC0, 128).T)
    H["us_c"] = cc(np.asarray(inp["L0_ffn_up_scale"], f32).reshape(NIC0, 128).T)
    H["dsrep"] = cc(np.broadcast_to(np.asarray(inp["L0_ffn_down_scale"], f32)[None, :], (128, D)))
    H["l1wq"] = cc(np.asarray(inp["L1_attn_Wq"], f32).reshape(FC, 128, NH, 128).transpose(2, 1, 0, 3))
    H["l1wk"] = cc(np.asarray(inp["L1_attn_Wk"], f32).reshape(FC, 128, NH, 128).transpose(2, 1, 0, 3))
    H["l1wv"] = cc(np.asarray(inp["L1_attn_Wv"], f32).reshape(FC, 128, 4, 512).transpose(2, 1, 0, 3))
    H["l1wo"] = cc(np.asarray(inp["L1_attn_Wo"], f32).reshape(FC, 128, 4, 512).transpose(2, 1, 0, 3))
    H["wg1_s"] = cc(np.asarray(inp["L1_attn_Wg1"], f32).reshape(FC, 128, 128).transpose(1, 0, 2))
    H["wg2_s"] = cc(np.asarray(inp["L1_attn_Wg2"], f32))
    H["nwrep"] = cc(np.broadcast_to(np.asarray(inp["L1_attn_norm_weight"], f32)[None, :], (128, D)))
    H["lutrep"] = cc(np.broadcast_to(np.asarray(inp["L1_ffn_lut"], f32)[None, :], (128, 16)))

    ll = -(2.0 ** (-8.0 * (np.arange(NH, dtype=np.float64) + 1.0) / NH))
    i = np.arange(128, dtype=np.float64)
    H["dkcol"] = np.exp(ll[None, :] * (127.0 - i[:, None])).astype(f32)
    dq = np.exp(ll[:, None] * (i[None, :] + 1.0)).astype(f32)
    H["dqrep"] = cc(np.broadcast_to(dq[:, None, :], (NH, 128, 128))).astype(BF)
    diff = i[:, None] - i[None, :]
    Dm = np.where(diff[None, :, :] >= 0, np.exp(ll[:, None, None] * diff[None, :, :]), 0.0)
    H["D_s"] = Dm.astype(f32).astype(BF)

    gidxT = np.asarray(inp["L1_ffn_gate_idx"], np.int32).T
    uidxT = np.asarray(inp["L1_ffn_up_idx"], np.int32).T
    didxT = np.asarray(inp["L1_ffn_down_idx"], np.int32).T
    def tile_gu(m):
        return m.reshape(FC, 128, NIC1, 128).transpose(2, 1, 0, 3).reshape(NIC1, 128, D)
    gt, ut = tile_gu(gidxT), tile_gu(uidxT)
    dt = didxT.reshape(NIC1, 128, 4, 512).transpose(0, 2, 1, 3)
    blocks = np.zeros((8 * SHARD, 128, 512), np.int32)
    b = 0
    for src in (gt, ut):
        for ic in range(NIC1):
            for jb in range(4):
                blocks[b] = src[ic, :, jb * 512:(jb + 1) * 512]
                b += 1
    for ic in range(NIC1):
        for oc in range(4):
            blocks[b] = dt[ic, oc]
            b += 1
    H["blocks"] = blocks

    x = np.asarray(inp["L0_attn_x"], f32)
    percore = []
    ii = np.arange(128)[:, None]
    jj = np.arange(640)[None, :]
    th = 10000.0 ** (-2.0 * np.arange(64, dtype=np.float64) / 128.0)
    lam_c = np.exp(np.float32(128.0) * ll.astype(f32)).astype(np.float64)
    for c in range(8):
        bq, half = c // 2, c % 2
        pos0 = half * 1024
        pc = {}
        pc["x_own"] = cc(x[bq, pos0:pos0 + 1024, :])
        hlo = pos0 - 512 if half == 1 else 0
        pc["x_halo"] = cc(x[bq, hlo:hlo + 512, :])
        m = np.zeros((8, 128, 640), np.float32)
        for tt in range(8):
            g = pos0 - 512 + tt * 128 + jj
            valid = (jj >= ii) & (jj <= ii + 512) & (g >= 0)
            m[tt] = np.where(valid, 0.0, NEG)
        pc["masks"] = m.astype(BF)
        tpos = (pos0 + np.arange(1024, dtype=np.float64))[None, :]
        ang = np.concatenate([th, th])[:, None] * tpos
        pc["cos2"] = np.cos(ang).astype(f32).astype(BF)
        pc["sin2"] = np.sin(ang).astype(f32).astype(BF)
        lf = np.zeros((128,), np.float64)
        for h in range(NH):
            for cch in range(8):
                lf[h * 8 + cch] = (lam_c[h] ** cch) * float(half)
        pc["lamflag"] = cc(np.broadcast_to(lf.astype(f32)[None, :], (128, 128)))
        pc["l1idx_shard"] = cc(blocks[c * SHARD:(c + 1) * SHARD])
        percore.append(pc)
    H["percore"] = percore
    return H


def kernel(**inputs):
    H = host_prepare(inputs)
    cores = list(range(8))
    if FUSED:
        ncF = build_fused()
        shared = {k: H[k] for k in ("gT_h", "uT_h", "dT_h", "gs_c", "us_c", "dsrep",
                                    "wq_h", "wk_h", "wv_g", "wo_oc", "l1wq", "l1wk",
                                    "l1wv", "l1wo", "wg1_s", "wg2_s", "nwrep",
                                    "lutrep", "dkcol", "dqrep", "D_s")}
        imF = []
        for c in cores:
            pc = H["percore"][c]
            m = dict(shared)
            m["x_own"] = pc["x_own"]
            m["x_halo"] = pc["x_halo"]
            m["masks"] = pc["masks"]
            m["cos2"] = pc["cos2"]
            m["sin2"] = pc["sin2"]
            m["lamflag"] = pc["lamflag"]
            m["l1idx_shard"] = pc["l1idx_shard"]
            m["flag2c"] = np.full((128, 1), 1.0 - (c % 2), np.float32)
            imF.append(m)
        res = run_bass_kernel_spmd(ncF, imF, core_ids=cores)
        out = np.concatenate([res.results[c]["xout"] for c in cores], axis=0)
        return out.reshape(4, 2048, 2048)
    ncA = build_deq()
    imA = [{"l1idx_shard": H["percore"][c]["l1idx_shard"], "lutrep": H["lutrep"]} for c in cores]
    resA = run_bass_kernel_spmd(ncA, imA, core_ids=cores)
    w_gath = np.concatenate([resA.results[c]["wshard"] for c in cores], axis=0)

    ncB = build_l0()
    shared = {k: H[k] for k in ("gT_h", "uT_h", "dT_h", "gs_c", "us_c", "dsrep")}
    shared["wq_h"] = H["wq_h"]; shared["wk_h"] = H["wk_h"]
    shared["wv_g"] = H["wv_g"]; shared["wo_oc"] = H["wo_oc"]
    imB = []
    for c in cores:
        m = dict(shared)
        m["x_own"] = H["percore"][c]["x_own"]
        m["x_halo"] = H["percore"][c]["x_halo"]
        m["masks"] = H["percore"][c]["masks"]
        imB.append(m)
    resB = run_bass_kernel_spmd(ncB, imB, core_ids=cores)
    x1 = [resB.results[c]["x1"] for c in cores]

    ncC = build_l1pre()
    imC = [{"x1": x1[c], "l1wk": H["l1wk"], "l1wv": H["l1wv"],
            "cos2": H["percore"][c]["cos2"], "sin2": H["percore"][c]["sin2"],
            "dkcol": H["dkcol"]} for c in cores]
    resC = run_bass_kernel_spmd(ncC, imC, core_ids=cores)

    ncD = build_l1main()
    imD = [{"x1": x1[c],
            "krot": resC.results[c]["krot"], "vst": resC.results[c]["vst"],
            "pst": resC.results[c]["pst"], "Rstate": resC.results[c ^ 1]["S_fin"],
            "l1wq": H["l1wq"], "l1wo": H["l1wo"], "wg1_s": H["wg1_s"],
            "wg2_s": H["wg2_s"], "nwrep": H["nwrep"],
            "cos2": H["percore"][c]["cos2"], "sin2": H["percore"][c]["sin2"],
            "dqrep": H["dqrep"], "D_s": H["D_s"], "lamflag": H["percore"][c]["lamflag"],
            "w_gath": w_gath} for c in cores]
    resD = run_bass_kernel_spmd(ncD, imD, core_ids=cores)
    out = np.concatenate([resD.results[c]["xout"] for c in cores], axis=0)
    return out.reshape(4, 2048, 2048)


# version 19
# speedup vs baseline: 1.0149x; 1.0123x over previous
import numpy as np
import ml_dtypes
from contextlib import ExitStack

import concourse.bass as bass
import concourse.mybir as mybir
from concourse.bass_utils import run_bass_kernel_spmd

F32 = mybir.dt.float32
BF16 = mybir.dt.bfloat16
I32 = mybir.dt.int32
U8 = mybir.dt.uint8
AF = mybir.ActivationFunctionType
ALU = mybir.AluOpType
AX = mybir.AxisListType.X
BF = ml_dtypes.bfloat16

P = 128
T = 1024
D = 2048
NH = 16
HD = 128
FC = D // P
I0, NIC0 = 8192, 64
I1, NIC1 = 5504, 43
NEG = -1.0e38
SCALE0 = 1.0 / float(np.sqrt(128.0))
SHARD = 65

class Buf:
    __slots__ = ("w", "r")
    def __init__(self):
        self.w = None
        self.r = {}

NLANE = 8

class Sched:
    def __init__(self, nc):
        self.nc = nc
        self.q = {"pe": nc.tensor, "dve": nc.vector, "act": nc.scalar,
                  "pool": nc.gpsimd, "sp": nc.sync}
        self.sem = {}
        self.cnt = {}
        for n in ("pe", "dve", "act", "pool", "sp", "cc"):
            self.sem[n] = nc.alloc_semaphore(f"s_{n}")
            self.cnt[n] = 0
        self.rr = {}
        for qn in ("sp", "pool", "act"):
            for l in range(NLANE):
                n = f"D{qn}{l}"
                self.sem[n] = nc.alloc_semaphore(f"s_{n}")
                self.cnt[n] = 0
            self.rr[qn] = 0
        self.lastw = {}
        self.bufs = {}

    def buf(self, key):
        b = self.bufs.get(key)
        if b is None:
            b = self.bufs[key] = Buf()
        return b

    def _wait(self, eng, ctr, val):
        if val <= 0:
            return
        k = (eng, ctr)
        if self.lastw.get(k, 0) >= val:
            return
        self.q[eng].wait_ge(self.sem[ctr], val)
        self.lastw[k] = val

    def step(self, eng, r=(), w=(), emit=None, dma=False, cc=False):
        need = {}
        for key in r:
            b = self.buf(key)
            if b.w and need.get(b.w[0], 0) < b.w[1]:
                need[b.w[0]] = b.w[1]
        for key in w:
            b = self.buf(key)
            if b.w and need.get(b.w[0], 0) < b.w[1]:
                need[b.w[0]] = b.w[1]
            for c, v in b.r.items():
                if need.get(c, 0) < v:
                    need[c] = v
        if dma:
            lane = self.rr[eng] % NLANE
            self.rr[eng] += 1
            ctr = f"D{eng}{lane}"
            if self.cnt[ctr] > 0:
                self._wait(eng, ctr, self.cnt[ctr])
        for c in sorted(need):
            self._wait(eng, c, need[c])
        insts = emit(self.q[eng])
        if not isinstance(insts, (list, tuple)):
            insts = [insts]
        if cc:
            ctr = "cc"
            for inst in insts:
                self.cnt[ctr] += 1
                inst.then_inc(self.sem[ctr])
        elif dma:
            assert len(insts) == 1, "one dma_start per dma step"
            self.cnt[ctr] += 16
            insts[0].then_inc(self.sem[ctr], 16)
        else:
            ctr = eng
            self.cnt[ctr] += 1
            insts[-1].then_inc(self.sem[ctr], 1)
        v = self.cnt[ctr]
        for key in r:
            self.buf(key).r[ctr] = v
        for key in w:
            b = self.buf(key)
            b.w = (ctr, v)
            b.r = {}
        return v

    def barrier(self):
        for e in self.q:
            for c, n in self.cnt.items():
                if c == "cc":
                    continue
                self._wait(e, c, n)


def emit_stage0(nc, s, src, xres, xnT, id_s, halo_src=None, haloT=None, pfx="ln", load=True):
    assert load or halo_src is None
    with ExitStack() as ph:
        sb = lambda n, sh, dt: ph.enter_context(nc.sbuf_tensor(n, sh, dt))
        tp = [ph.enter_context(nc.psum_tensor(f"{pfx}tp{i}", [128, 1024], BF16)) for i in range(2)]
        sc = sb(f"{pfx}_sc", [P, 2, D], F32)
        sq = sb(f"{pfx}_sq", [P, 2, D], F32)
        xb = sb(f"{pfx}_xb", [P, 2, D], BF16)
        scol = sb(f"{pfx}_scol", [P, 16], F32)
        mcol = sb(f"{pfx}_mcol", [P, 16], F32)
        vcol = sb(f"{pfx}_vcol", [P, 16], F32)
        sqv = sb(f"{pfx}_sqv", [P, 16], F32)
        rstd = sb(f"{pfx}_rstd", [P, 16], F32)
        nc._lncol = (scol, mcol, vcol, rstd)
        for tn, kk in ((scol, "scol"), (mcol, "mcol"), (vcol, "vcol"), (sqv, "sqv"), (rstd, "rstd")):
            s.step("dve", w=[(kk, tt) for tt in range(16)] + [(kk,)],
                   emit=lambda q, tn=tn: q.memset(tn[:, :], 0.0))
        hx = sb(f"{pfx}_hx", [P, 1, D], F32) if halo_src is not None else None
        ntile = 8 + (4 if halo_src is not None else 0)

        def load_tile(tt, tag):
            if tt < 8:
                if load:
                    s.step("sp", w=[("xres", tt)], dma=True,
                           emit=lambda q, tt=tt: q.dma_start(
                               out=xres[:, tt, :], in_=src[tt * 128:(tt + 1) * 128, :]))
                return xres[:, tt, :], ("xres", tt)
            th = tt - 8
            slot = 0
            s.step("sp", w=[(tag, slot)], dma=True,
                   emit=lambda q, th=th, slot=slot: q.dma_start(
                       out=hx[:, slot, :], in_=halo_src[th * 128:(th + 1) * 128, :]))
            return hx[:, slot, :], (tag, slot)

        for tt in range(ntile):
            xin, key = load_tile(tt, "hx")
            slot = tt % 2
            s.step("dve", r=[key], w=[("scol", tt)],
                   emit=lambda q, xin=xin, tt=tt: q.reduce_sum(scol[:, tt:tt + 1], xin, axis=AX))
            s.step("dve", r=[("scol", tt)], w=[("mcol", tt)],
                   emit=lambda q, tt=tt: q.tensor_scalar_mul(mcol[:, tt:tt + 1], scol[:, tt:tt + 1], 1.0 / D))
            s.step("dve", r=[key, ("mcol", tt)], w=[("sc", slot)],
                   emit=lambda q, xin=xin, slot=slot, tt=tt: q.tensor_single_scalar(
                       sc[:, slot, :], xin, mcol[:, tt:tt + 1], ALU.subtract))
            s.step("act", r=[("sc", slot)], w=[("sq", slot), ("vcol", tt)],
                   emit=lambda q, slot=slot, tt=tt: q.activation(
                       out=sq[:, slot, :], in_=sc[:, slot, :], func=AF.Square,
                       accum_out=vcol[:, tt:tt + 1]))
        s.step("act", r=[("vcol", tt) for tt in range(ntile)] + [("cst",)], w=[("sqv",)],
               emit=lambda q: q.activation(out=sqv[:, 0:ntile], in_=vcol[:, 0:ntile],
                                           func=AF.Sqrt, scale=1.0 / D, bias=nc._cst[:, 0:1]))
        s.step("dve", r=[("sqv",)], w=[("rstd",)],
               emit=lambda q: q.reciprocal(rstd[:, 0:ntile], sqv[:, 0:ntile]))
        for tt in range(ntile):
            if tt < 8:
                xin, key = xres[:, tt, :], ("xres", tt)
            else:
                xin, key = load_tile(tt, "hx")
            slot = tt % 2
            s.step("dve", r=[key, ("mcol", tt), ("rstd",)], w=[("xb", slot)],
                   emit=lambda q, xin=xin, slot=slot, tt=tt: q.tensor_scalar(
                       xb[:, slot, :], xin, mcol[:, tt:tt + 1], rstd[:, tt:tt + 1],
                       ALU.subtract, ALU.mult))
            for fc in range(FC):
                bank = 6 + (fc % 2)
                dstT = xnT if tt < 8 else haloT
                dcol = (tt * 128) if tt < 8 else ((tt - 8) * 128)
                dkey = ("xnT", fc) if tt < 8 else ("haloT", fc)
                s.step("pe", r=[("xb", slot), ("id",)], w=[("ps", bank)],
                       emit=lambda q, slot=slot, fc=fc, bank=bank: q.transpose(
                           tp[bank - 6][:, 0:128], xb[:, slot, fc * 128:(fc + 1) * 128], id_s[:, :]))
                if fc % 2 == 0:
                    s.step("dve", r=[("ps", bank)], w=[dkey],
                           emit=lambda q, bank=bank, dstT=dstT, fc=fc, dcol=dcol: q.tensor_copy(
                               dstT[:, fc, dcol:dcol + 128], tp[bank - 6][:, 0:128]))
                else:
                    s.step("act", r=[("ps", bank)], w=[dkey],
                           emit=lambda q, bank=bank, dstT=dstT, fc=fc, dcol=dcol: q.copy(
                               dstT[:, fc, dcol:dcol + 128], tp[bank - 6][:, 0:128]))
        s.barrier()


XNKEYS = [("xnT", fc) for fc in range(FC)]
HALOKEYS = [("haloT", fc) for fc in range(FC)]


def emit_l0_attn(nc, s, xres, xnT, haloT, id_s, pb, attn_st, dram, bg=None):
    with ExitStack() as ph:
        sb = lambda n, sh, dt: ph.enter_context(nc.sbuf_tensor(n, sh, dt))
        tp = [ph.enter_context(nc.psum_tensor(f"atp{i}", [128, 1024], BF16)) for i in range(2)]
        masks = sb("a_masks", [P, 8, 640], BF16)
        wv_s = sb("a_wv", [P, FC, 512], BF16)
        wq_s = sb("a_wq", [P, 2, FC, 128], BF16)
        wk_s = sb("a_wk", [P, 2, FC, 128], BF16)
        vg = sb("a_vg", [P, 12, 512], BF16)
        qh = sb("a_qh", [P, 2, T], BF16)
        kh = sb("a_kh", [P, 2, 1536], BF16)
        sm = sb("a_sm", [P, 2, 640], F32)
        probs = sb("a_probs", [P, 2, 640], BF16)
        probsT = sb("a_probsT", [P, 2, 640], BF16)
        mx = sb("a_mx", [P, 8], F32)
        ath = sb("a_ath", [P, 2, T], BF16)

        s.step("sp", w=[("masks",)], dma=True,
               emit=lambda q: q.dma_start(out=masks[:, :, :],
                                          in_=dram["masks"].ap().rearrange("t p j -> p t j")))

        def load_wv(hg):
            s.step("pool", w=[("wvw",)], dma=True,
                   emit=lambda q, hg=hg: q.dma_start(out=wv_s[:, :, :], in_=dram["wv"].ap()[hg]))

        def load_wqk(h):
            hp = h % 2
            s.step("pool", w=[("wqw", hp)], dma=True,
                   emit=lambda q, h=h, hp=hp: q.dma_start(out=wq_s[:, hp, :, :], in_=dram["wq"].ap()[h]))
            s.step("pool", w=[("wkw", hp)], dma=True,
                   emit=lambda q, h=h, hp=hp: q.dma_start(out=wk_s[:, hp, :, :], in_=dram["wk"].ap()[h]))

        def proj_steps(h):
            hp = h % 2
            for th in range(2):
                bank = th % 2
                yield lambda th=th, hp=hp, bank=bank: (
                    s.step("pe", r=XNKEYS + [("wqw", hp)], w=[("ps", bank)],
                           emit=lambda q: [
                               q.matmul(pb[bank][:, :], wq_s[:, hp, hc, :],
                                        xnT[:, hc, th * 512:(th + 1) * 512],
                                        start=(hc == 0), stop=(hc == 15)) for hc in range(FC)]),
                    s.step("dve", r=[("ps", bank)], w=[("qh", hp)],
                           emit=lambda q: q.tensor_copy(
                               qh[:, hp, th * 512:(th + 1) * 512], pb[bank][:, :])))
            for kp in range(3):
                bank = kp % 2
                lsrc = haloT if kp == 0 else xnT
                lkeys = HALOKEYS if kp == 0 else XNKEYS
                tcol = 0 if kp == 0 else (kp - 1) * 512
                yield lambda kp=kp, hp=hp, bank=bank, lsrc=lsrc, lkeys=lkeys, tcol=tcol: (
                    s.step("pe", r=lkeys + [("wkw", hp)], w=[("ps", bank)],
                           emit=lambda q: [
                               q.matmul(pb[bank][:, :], wk_s[:, hp, hc, :],
                                        lsrc[:, hc, tcol:tcol + 512],
                                        start=(hc == 0), stop=(hc == 15)) for hc in range(FC)]),
                    s.step("dve", r=[("ps", bank)], w=[("kh", hp)],
                           emit=lambda q: q.tensor_copy(
                               kh[:, hp, kp * 512:(kp + 1) * 512], pb[bank][:, :])))

        def stA(h, tt):
            hp = h % 2
            bA, bC = 2 + tt % 2, 4 + tt % 2
            s.step("pe", r=[("qh", hp), ("kh", hp)], w=[("ps", bA)],
                   emit=lambda q: q.matmul(
                       pb[bA][:, :], qh[:, hp, tt * 128:(tt + 1) * 128],
                       kh[:, hp, tt * 128:tt * 128 + 512], start=True, stop=True))
            s.step("pe", r=[("qh", hp), ("kh", hp)], w=[("ps", bC)],
                   emit=lambda q: q.matmul(
                       pb[bC][:, 128:256], qh[:, hp, tt * 128:(tt + 1) * 128],
                       kh[:, hp, tt * 128 + 512:tt * 128 + 640], start=True, stop=True))

        def stB(h, tt):
            pp = tt % 2
            bA, bC = 2 + tt % 2, 4 + tt % 2
            s.step("dve", r=[("ps", bA), ("masks",)], w=[("sm", pp)],
                   emit=lambda q: q.scalar_tensor_tensor(
                       sm[:, pp, 0:512], pb[bA][:, :], SCALE0, masks[:, tt, 0:512],
                       ALU.mult, ALU.add))
            s.step("dve", r=[("ps", bC), ("masks",)], w=[("sm", pp)],
                   emit=lambda q: q.scalar_tensor_tensor(
                       sm[:, pp, 512:640], pb[bC][:, 128:256], SCALE0, masks[:, tt, 512:640],
                       ALU.mult, ALU.add))
            s.step("dve", r=[("sm", pp)], w=[("mx", pp)],
                   emit=lambda q: q.reduce_max(mx[:, 4 * pp:4 * pp + 1], sm[:, pp, :], axis=AX))
            s.step("dve", r=[("mx", pp)], w=[("mx", pp)],
                   emit=lambda q: q.tensor_scalar_mul(
                       mx[:, 4 * pp + 1:4 * pp + 2], mx[:, 4 * pp:4 * pp + 1], -1.0))

        def stC(h, tt):
            pp = tt % 2
            s.step("act", r=[("sm", pp), ("mx", pp)], w=[("probs", pp), ("mx", pp)],
                   emit=lambda q: q.activation(
                       out=probs[:, pp, :], in_=sm[:, pp, :], func=AF.Exp,
                       scale=1.0, bias=mx[:, 4 * pp + 1:4 * pp + 2],
                       accum_out=mx[:, 4 * pp + 2:4 * pp + 3]))

        def stD(h, tt):
            pp = tt % 2
            s.step("dve", r=[("mx", pp)], w=[("mx", pp)],
                   emit=lambda q: q.reciprocal(
                       mx[:, 4 * pp + 3:4 * pp + 4], mx[:, 4 * pp + 2:4 * pp + 3]))
            s.step("dve", r=[("probs", pp), ("mx", pp)], w=[("probs", pp)],
                   emit=lambda q: q.tensor_single_scalar(
                       probs[:, pp, :], probs[:, pp, :], mx[:, 4 * pp + 3:4 * pp + 4], ALU.mult))

        def stE(h, tt):
            pp = tt % 2
            for jj in range(5):
                s.step("pe", r=[("probs", pp), ("id",)], w=[("ps", 6 + pp)],
                       emit=lambda q, jj=jj: q.transpose(
                           tp[pp][:, jj * 128:(jj + 1) * 128],
                           probs[:, pp, jj * 128:(jj + 1) * 128], id_s[:, :]))

        def stF(h, tt):
            pp = tt % 2
            s.step("act", r=[("ps", 6 + pp)], w=[("probsT", pp)],
                   emit=lambda q: q.copy(probsT[:, pp, :], tp[pp][:, 0:640]))

        def stG(h, hs, tt):
            pp = tt % 2
            bC = 4 + tt % 2
            s.step("pe", r=[("probsT", pp)] + [("vg", tt + jj) for jj in range(5)],
                   w=[("ps", bC)],
                   emit=lambda q: [
                       q.matmul(pb[bC][:, 0:128], vg[:, tt + jj, hs * 128:(hs + 1) * 128],
                                probsT[:, pp, jj * 128:(jj + 1) * 128],
                                start=(jj == 0), stop=(jj == 4)) for jj in range(5)])

        def stH(h, tt):
            hp = h % 2
            bC = 4 + tt % 2
            s.step("dve", r=[("ps", bC)], w=[("ath", hp)],
                   emit=lambda q: q.tensor_copy(
                       ath[:, hp, tt * 128:(tt + 1) * 128], pb[bC][:, 0:128]))

        load_wv(0)
        load_wqk(0)
        pending = list(proj_steps(0))
        for st in pending:
            st()
        for hg in range(4):
            for tc in range(12):
                bank = tc % 2
                if tc < 4:
                    lsrc, lkeys, tcol = haloT, HALOKEYS, tc * 128
                else:
                    lsrc, lkeys, tcol = xnT, XNKEYS, (tc - 4) * 128
                s.step("pe", r=lkeys + [("wvw",)], w=[("ps", bank)],
                       emit=lambda q, lsrc=lsrc, tcol=tcol, bank=bank: [
                           q.matmul(pb[bank][:, :], lsrc[:, hc, tcol:tcol + 128],
                                    wv_s[:, hc, :], start=(hc == 0), stop=(hc == 15))
                           for hc in range(FC)])
                s.step("dve", r=[("ps", bank)], w=[("vg", tc)],
                       emit=lambda q, tc=tc, bank=bank: q.tensor_copy(vg[:, tc, :], pb[bank][:, :]))
                if bg and tc % 3 == 2:
                    bg()
            if hg < 3:
                load_wv(hg + 1)
            for hs in range(4):
                h = hg * 4 + hs
                hp = h % 2
                if h < 15:
                    load_wqk(h + 1)
                nxt = list(proj_steps(h + 1)) if h + 1 < 16 else []
                for i in range(10):
                    if i < 8:
                        stA(h, i)
                        stB(h, i)
                    if i - 1 >= 0 and i - 1 < 8:
                        stD(h, i - 1)
                    if i - 2 >= 0:
                        stE(h, i - 2)
                        stF(h, i - 2)
                        stG(h, hs, i - 2)
                        stH(h, i - 2)
                    if i < 8:
                        stC(h, i)
                    if i in (1, 2, 3, 4, 5) and nxt:
                        nxt.pop(0)()
                    if bg and i in (3, 7):
                        bg()
                while nxt:
                    nxt.pop(0)()
                s.step("sp", r=[("ath", hp)], w=[("ast", h)], dma=True,
                       emit=lambda q, h=h, hp=hp: q.dma_start(out=attn_st.ap()[h], in_=ath[:, hp, :]))
        s.barrier()
    with ExitStack() as ph:
        sb = lambda n, sh, dt: ph.enter_context(nc.sbuf_tensor(n, sh, dt))
        wo_s = sb("a_wo", [P, 2, FC, 512], BF16)
        atile = sb("a_atile", [P, 2, FC, 128], BF16)
        astk = [("ast", h) for h in range(NH)]
        for ps2 in range(2):
            for o2 in range(2):
                s.step("pool", w=[("wow", o2)], dma=True,
                       emit=lambda q, ps2=ps2, o2=o2: q.dma_start(
                           out=wo_s[:, o2, :, :], in_=dram["wo"].ap()[ps2 * 2 + o2]))
            for tt in range(8):
                s.step("sp", r=astk, w=[("atile", tt % 2)], dma=True,
                       emit=lambda q, tt=tt: q.dma_start(
                           out=atile[:, tt % 2, :, :],
                           in_=attn_st.ap()[:, :, tt * 128:(tt + 1) * 128].rearrange("h d t -> d h t")))
                for o2 in range(2):
                    bank = 2 * (tt % 2) + o2
                    s.step("pe", r=[("atile", tt % 2), ("wow", o2)], w=[("ps", bank)],
                           emit=lambda q, tt=tt, o2=o2, bank=bank: [
                               q.matmul(pb[bank][:, :], atile[:, tt % 2, fc, :],
                                        wo_s[:, o2, fc, :], start=(fc == 0), stop=(fc == 15))
                               for fc in range(FC)])
                    s.step("dve", r=[("ps", bank)], w=[("xres", tt)],
                           emit=lambda q, tt=tt, ps2=ps2, o2=o2, bank=bank: q.tensor_tensor(
                               xres[:, tt, (ps2 * 2 + o2) * 512:(ps2 * 2 + o2 + 1) * 512],
                               xres[:, tt, (ps2 * 2 + o2) * 512:(ps2 * 2 + o2 + 1) * 512],
                               pb[bank][:, :], ALU.add))
                if bg:
                    bg()
        s.barrier()


def emit_qffn(nc, s, xres, xnT, pb, h_st, nic, corrections, wcfg, pfx="q", bg=None, after_a=None):
    with ExitStack() as ph:
        sb = lambda n, sh, dt: ph.enter_context(nc.sbuf_tensor(n, sh, dt))
        pext = [ph.enter_context(nc.psum_tensor(f"{pfx}px{i}", [128, 512], F32)) for i in range(2)]
        pb8 = list(pb) + pext
        wg_s = sb(f"{pfx}_wg", wcfg["gu_shape"], BF16)
        wu_s = sb(f"{pfx}_wu", wcfg["gu_shape"], BF16)
        wd_s = sb(f"{pfx}_wd", [P, 3, 512], BF16)
        wdr = sb(f"{pfx}_wdr", [P, 3, 512], U8) if wcfg.get("wd_raw") else None
        s._wdr = wdr
        s._wg, s._wu, s._wd = wg_s, wu_s, wd_s
        h_s = sb(f"{pfx}_h", [P, 3, T], BF16)
        hl = sb(f"{pfx}_hl", [P, 3, T], BF16)
        sg = sb(f"{pfx}_sgb", [P, 2, 512], BF16)
        su = sb(f"{pfx}_sub", [P, 2, 512], BF16)
        ones_c = sb(f"{pfx}_ones", [P, 1], BF16)
        m128 = sb(f"{pfx}_m128", [1, 512], BF16)
        xnsum = sb(f"{pfx}_xnsum", [1, T], BF16)
        hsum = sb(f"{pfx}_hsum", [1, T], BF16)
        if corrections:
            ytmp = sb(f"{pfx}_ytmp", [P, 2, 512], F32)
            gs_c = sb(f"{pfx}_gs", [P, nic], F32)
            us_c = sb(f"{pfx}_us", [P, nic], F32)
            dsrep = sb(f"{pfx}_dsrep", [P, D], F32)
            s.step("sp", w=[("gsc",)], dma=True,
                   emit=lambda q: q.dma_start(out=gs_c[:, :], in_=wcfg["gs"].ap()[:, :]))
            s.step("sp", w=[("usc",)], dma=True,
                   emit=lambda q: q.dma_start(out=us_c[:, :], in_=wcfg["us"].ap()[:, :]))
            s.step("sp", w=[("dsr",)], dma=True,
                   emit=lambda q: q.dma_start(out=dsrep[:, :], in_=wcfg["dsrep"].ap()[:, :]))
        s.step("dve", w=[("ones",)], emit=lambda q: q.memset(ones_c[:, :], 1.0))
        s.step("dve", w=[("m128",)], emit=lambda q: q.memset(m128[:, :], -128.0))
        gsl = wcfg["gu_slice"]

        if corrections:
            for th in range(2):
                s.step("pe", r=XNKEYS + [("ones",)], w=[("ps", 4 + th)],
                       emit=lambda q, th=th: [
                           q.matmul(pb[4 + th][0:1, :], ones_c[:, :],
                                    xnT[:, hc, th * 512:(th + 1) * 512],
                                    start=(hc == 0), stop=(hc == 15)) for hc in range(FC)])
                s.step("dve", r=[("ps", 4 + th)], w=[("xnsum",)],
                       emit=lambda q, th=th: q.tensor_copy(
                           xnsum[0:1, th * 512:(th + 1) * 512], pb[4 + th][0:1, :]))

        def emit_hsum(ic):
            for th in range(2):
                s.step("pe", r=[("h", ic % 3), ("ones",)], w=[("ps", 4 + th)],
                       emit=lambda q, ic=ic, th=th: q.matmul(
                           pb[4 + th][0:1, :], ones_c[:, :],
                           h_s[:, ic % 3, th * 512:(th + 1) * 512],
                           start=(ic == 0), stop=(ic == nic - 1)))

        wcfg["wg_load"](s, 0)
        wcfg["wu_load"](s, 0)
        for ic in range(nic):
            for gu in range(2):
                wkey = ("wg" if gu == 0 else "wu", ic % 2)
                ws = wg_s if gu == 0 else wu_s
                for th in range(2):
                    bank = gu * 2 + th
                    def mmgu(q, ws=ws, ic=ic, th=th, bank=bank):
                        ins = [q.matmul(pb[bank][:, :], gsl(ws, ic % 2, hc),
                                        xnT[:, hc, th * 512:(th + 1) * 512],
                                        start=(hc == 0),
                                        stop=(hc == 15 and not corrections))
                               for hc in range(FC)]
                        if corrections:
                            ins.append(q.matmul(pb[bank][:, :], m128[0:1, 0:128],
                                                xnsum[0:1, th * 512:(th + 1) * 512],
                                                start=False, stop=True))
                        return ins
                    rdk = XNKEYS + [wkey]
                    if corrections:
                        rdk = rdk + [("xnsum",), ("m128",)]
                    s.step("pe", r=rdk, w=[("ps", bank)], emit=mmgu)
            if ic + 1 < nic:
                wcfg["wg_load"](s, ic + 1)
                wcfg["wu_load"](s, ic + 1)
            for th in range(2):
                if corrections:
                    s.step("act", r=[("ps", th), ("gsc",)], w=[("sg", th)],
                           emit=lambda q, ic=ic, th=th: q.activation(
                               out=sg[:, th, :], in_=pb[th][:, :], func=AF.Silu,
                               scale=gs_c[:, ic:ic + 1]))
                    s.step("act", r=[("ps", 2 + th), ("usc",)], w=[("su", th)],
                           emit=lambda q, ic=ic, th=th: q.activation(
                               out=su[:, th, :], in_=pb[2 + th][:, :], func=AF.Copy,
                               scale=us_c[:, ic:ic + 1]))
                else:
                    s.step("act", r=[("ps", th)], w=[("sg", th)],
                           emit=lambda q, th=th: q.activation(
                               out=sg[:, th, :], in_=pb[th][:, :], func=AF.Silu))
                    s.step("act", r=[("ps", 2 + th)], w=[("su", th)],
                           emit=lambda q, th=th: q.activation(
                               out=su[:, th, :], in_=pb[2 + th][:, :], func=AF.Copy))
                s.step("dve", r=[("sg", th), ("su", th)], w=[("h", ic % 3)],
                       emit=lambda q, ic=ic, th=th: q.tensor_tensor(
                           h_s[:, ic % 3, th * 512:(th + 1) * 512],
                           sg[:, th, :], su[:, th, :], ALU.mult))
            s.step("sp", r=[("h", ic % 3)], w=[("hst", ic)], dma=True,
                   emit=lambda q, ic=ic: q.dma_start(out=h_st.ap()[ic], in_=h_s[:, ic % 3, :]))
            if bg:
                bg()
            if corrections and ic >= 2:
                emit_hsum(ic - 2)
        if corrections:
            emit_hsum(nic - 2)
            emit_hsum(nic - 1)
            for th in range(2):
                s.step("dve", r=[("ps", 4 + th)], w=[("hsum",)],
                       emit=lambda q, th=th: q.tensor_copy(
                           hsum[0:1, th * 512:(th + 1) * 512], pb[4 + th][0:1, :]))
        if after_a is not None:
            while bg and (after_a.__self__.n < SHARD):
                bg()
            after_a()
        HLD = 3
        WDD = 3
        for oc in range(4):
            for ic in range(nic):
                s.step("sp", r=[("hst", ic)], w=[("hl", ic % HLD)], dma=True,
                       emit=lambda q, ic=ic: q.dma_start(out=hl[:, ic % HLD, :], in_=h_st.ap()[ic]))
                wcfg["wd_load"](s, ic, oc)
                last = (ic == nic - 1) and not corrections
                s.step("pe", r=[("hl", ic % HLD), ("wd", ic % WDD)],
                       w=[("ps", tt2) for tt2 in range(8)],
                       emit=lambda q, ic=ic, last=last: [
                           q.matmul(pb8[tt2][:, :], hl[:, ic % HLD, tt2 * 128:(tt2 + 1) * 128],
                                    wd_s[:, ic % WDD, :], start=(ic == 0), stop=last)
                           for tt2 in range(8)])
                if bg:
                    bg()
            if corrections:
                s.step("pe", r=[("hsum",), ("m128",)], w=[("ps", tt2) for tt2 in range(8)],
                       emit=lambda q: [
                           q.matmul(pb8[tt2][:, :], hsum[0:1, tt2 * 128:(tt2 + 1) * 128],
                                    m128[0:1, :], start=False, stop=True) for tt2 in range(8)])
            for tt in range(8):
                if corrections:
                    s.step("dve", r=[("ps", tt), ("dsr",)], w=[("ytmp", tt % 2)],
                           emit=lambda q, tt=tt, oc=oc: q.tensor_tensor(
                               ytmp[:, tt % 2, :], pb8[tt][:, :],
                               dsrep[:, oc * 512:(oc + 1) * 512], ALU.mult))
                    s.step("dve", r=[("ytmp", tt % 2)], w=[("xres", tt)],
                           emit=lambda q, tt=tt, oc=oc: q.tensor_tensor(
                               xres[:, tt, oc * 512:(oc + 1) * 512],
                               xres[:, tt, oc * 512:(oc + 1) * 512],
                               ytmp[:, tt % 2, :], ALU.add))
                else:
                    s.step("dve", r=[("ps", tt)], w=[("xres", tt)],
                           emit=lambda q, tt=tt, oc=oc: q.tensor_tensor(
                               xres[:, tt, oc * 512:(oc + 1) * 512],
                               xres[:, tt, oc * 512:(oc + 1) * 512], pb8[tt][:, :], ALU.add))
        s.barrier()


def _mk_common(nc):
    s = Sched(nc)
    pb = [nc.alloc_psum_tensor(f"pb{i}", [128, 512], F32) for i in range(6)]
    xres = nc.alloc_sbuf_tensor("xres", [P, 8, D], F32)
    xnT = nc.alloc_sbuf_tensor("xnT", [P, FC, T], BF16)
    id_d = nc.inline_tensor(np.eye(128, dtype=BF), name="ident")
    id_s = nc.alloc_sbuf_tensor("id_s", [P, 128], BF16)
    s.step("sp", w=[("id",)], dma=True,
           emit=lambda q: q.dma_start(out=id_s[:, :], in_=id_d.ap()[:, :]))
    cst = nc.alloc_sbuf_tensor("cst", [P, 2], F32)
    s.step("dve", w=[("cst",)], emit=lambda q: q.memset(cst[:, 0:1], 1e-5))
    s.step("dve", w=[("cst",)], emit=lambda q: q.memset(cst[:, 1:2], 1e-6))
    nc._cst = cst
    return s, pb, xres, xnT, id_s


def _store_x(nc, s, xres, dst):
    for tt in range(8):
        s.step("sp", r=[("xres", tt)], dma=True,
               emit=lambda q, tt=tt: q.dma_start(
                   out=dst.ap()[tt * 128:(tt + 1) * 128, :], in_=xres[:, tt, :]))
    s.barrier()


def build_l0(parts=("attn", "ffn"), debug=False):
    nc = bass.Bass("TRN2", target_bir_lowering=False)
    din = lambda n, sh, dt: nc.dram_tensor(n, sh, dt, kind="ExternalInput")
    x_own = din("x_own", [T, D], F32)
    x_halo = din("x_halo", [512, D], F32)
    dram = {
        "wq": din("wq_h", [NH, 128, FC, 128], F32),
        "wk": din("wk_h", [NH, 128, FC, 128], F32),
        "wv": din("wv_g", [4, 128, FC, 512], F32),
        "wo": din("wo_oc", [4, 128, FC, 512], F32),
        "masks": din("masks", [8, 128, 640], BF16),
    }
    gT_d = din("gT_h", [NIC0, 128, FC, 128], U8)
    uT_d = din("uT_h", [NIC0, 128, FC, 128], U8)
    dT_d = din("dT_h", [NIC0, 4, 128, 512], U8)
    gs_d = din("gs_c", [128, NIC0], F32)
    us_d = din("us_c", [128, NIC0], F32)
    dsrep_d = din("dsrep", [128, D], F32)
    x1 = nc.dram_tensor("x1", [T, D], F32, kind="ExternalOutput")
    kind_dbg = "ExternalOutput" if debug else "Internal"
    attn_st = nc.dram_tensor("attn_st", [NH, 128, T], BF16, kind=kind_dbg)
    h_st = nc.dram_tensor("h_st", [NIC0, 128, T], BF16, kind=kind_dbg)

    s, pb, xres, xnT, id_s = _mk_common(nc)
    haloT = nc.alloc_sbuf_tensor("haloT", [P, FC, 512], BF16)
    emit_stage0(nc, s, x_own.ap(), xres, xnT, id_s, x_halo.ap(), haloT)
    if debug:
        xnT_dbg = nc.dram_tensor("xnT_dbg", [128, FC, T], BF16, kind="ExternalOutput")
        s.step("sp", r=XNKEYS, dma=True,
               emit=lambda q: q.dma_start(out=xnT_dbg.ap()[:, :, :], in_=xnT[:, :, :]))
        col_dbg = nc.dram_tensor("col_dbg", [128, 64], F32, kind="ExternalOutput")
        for di, tn in enumerate(nc._lncol):
            s.step("sp", r=[("rstd",)], dma=True,
                   emit=lambda q, di=di, tn=tn: q.dma_start(
                       out=col_dbg.ap()[:, di * 16:(di + 1) * 16], in_=tn[:, :]))


    if "attn" in parts:
        emit_l0_attn(nc, s, xres, xnT, haloT, id_s, pb, attn_st, dram)

    if "ffn" in parts:
        def wg_load(s, ic):
            s.step("pool", w=[("wg", ic % 2)], dma=True,
                   emit=lambda q, ic=ic: q.dma_start(
                       out=s._wg[:, ic % 2, :, :], in_=gT_d.ap()[ic]))
        def wu_load(s, ic):
            s.step("pool", w=[("wu", ic % 2)], dma=True,
                   emit=lambda q, ic=ic: q.dma_start(
                       out=s._wu[:, ic % 2, :, :], in_=uT_d.ap()[ic]))
        def wd_load(s, ic, oc):
            s.step("pool", w=[("wd", ic % 3)], dma=True,
                   emit=lambda q, ic=ic, oc=oc: q.dma_start(
                       out=s._wd[:, ic % 3, :], in_=dT_d.ap()[ic, oc]))
        wcfg = {
            "gu_shape": [P, 2, FC, 128],
            "gu_slice": lambda ws, slot, hc: ws[:, slot, hc, :],
            "wg_load": wg_load, "wu_load": wu_load, "wd_load": wd_load,
            "gs": gs_d, "us": us_d, "dsrep": dsrep_d,
        }
        emit_qffn(nc, s, xres, xnT, pb, h_st, NIC0, True, wcfg)

    _store_x(nc, s, xres, x1)
    return nc


def build_deq():
    nc = bass.Bass("TRN2", target_bir_lowering=False)
    idx_d = nc.dram_tensor("l1idx_shard", [SHARD, 128, 512], I32, kind="ExternalInput")
    lut_d = nc.dram_tensor("lutrep", [128, 16], F32, kind="ExternalInput")
    out_d = nc.dram_tensor("wshard", [SHARD, 128, 512], BF16, kind="ExternalOutput")
    s = Sched(nc)
    lut_s = nc.alloc_sbuf_tensor("lut_s", [P, 16], F32)
    ib = nc.alloc_sbuf_tensor("ib", [P, 4, 512], I32)
    wb = nc.alloc_sbuf_tensor("wb", [P, 4, 512], BF16)
    tb = nc.alloc_sbuf_tensor("tb", [P, 4, 512], BF16)
    s.step("sp", w=[("lut",)], dma=True,
           emit=lambda q: q.dma_start(out=lut_s[:, :], in_=lut_d.ap()[:, :]))
    dvn = [0]
    pon = [0]
    for n in range(SHARD):
        eng = "dve" if (n % 3) != 2 else "pool"
        if eng == "dve":
            slot = dvn[0] % 2
            dvn[0] += 1
        else:
            slot = 2 + pon[0] % 2
            pon[0] += 1
        s.step("sp", w=[("ib", slot)], dma=True,
               emit=lambda q, n=n, slot=slot: q.dma_start(out=ib[:, slot, :], in_=idx_d.ap()[n]))
        def chain(q, slot=slot):
            ins = [q.tensor_scalar(wb[:, slot, :], ib[:, slot, :], 0.0, lut_s[:, 0:1],
                                   ALU.is_equal, ALU.mult)]
            for j in range(1, 16):
                ins.append(q.tensor_scalar(tb[:, slot, :], ib[:, slot, :], float(j),
                                           lut_s[:, j:j + 1], ALU.is_equal, ALU.mult))
                ins.append(q.tensor_tensor(wb[:, slot, :], wb[:, slot, :], tb[:, slot, :], ALU.add))
            return ins
        s.step(eng, r=[("ib", slot), ("lut",)], w=[("wb", slot), ("tb", slot)], emit=chain)
        s.step("sp", r=[("wb", slot)], dma=True,
               emit=lambda q, n=n, slot=slot: q.dma_start(out=out_d.ap()[n], in_=wb[:, slot, :]))
    s.barrier()
    return nc


LL = [-(2.0 ** (-8.0 * (h + 1) / NH)) for h in range(NH)]
LAMC = [float(np.exp(np.float32(128.0) * np.float32(LL[h]))) for h in range(NH)]


def _mk_rope_bufs(nc, sb, pfx, pb, tag=0, shared=None):
    R = {"pb": pb, "tg": tag}
    if shared is None:
        R["cos2"] = sb(f"{pfx}_cos", [P, T], BF16)
        R["sin2"] = sb(f"{pfx}_sin", [P, T], BF16)
    else:
        R["cos2"] = shared["cos2"]
        R["sin2"] = shared["sin2"]
    R["ksil"] = sb(f"{pfx}_ksil", [P, T], BF16)
    R["t1"] = sb(f"{pfx}_t1", [P, T], BF16)
    R["t2"] = sb(f"{pfx}_t2", [P, T], BF16)
    R["t1sh"] = sb(f"{pfx}_t1sh", [64, T], BF16)
    R["t2sh"] = sb(f"{pfx}_t2sh", [64, T], BF16)
    R["obuf"] = sb(f"{pfx}_obuf", [64, T], BF16)
    return R


def _rope1_silu(nc, s, R, bank, th):
    g = R["tg"]
    s.step("act", r=[("ps", bank)], w=[("ksil", g, th)],
           emit=lambda q, th=th, b=bank: q.activation(
               out=R["ksil"][:, th * 512:(th + 1) * 512], in_=R["pb"][b][:, :], func=AF.Silu))


def _rope1_tail(nc, s, R):
    g = R["tg"]
    s.step("dve", r=[("ksil", g, 0), ("ksil", g, 1), ("cos2",)], w=[("t1", g)],
           emit=lambda q: q.tensor_tensor(R["t1"][:, :], R["ksil"][:, :], R["cos2"][:, :], ALU.mult))
    s.step("dve", r=[("ksil", g, 0), ("ksil", g, 1), ("sin2",)], w=[("t2", g)],
           emit=lambda q: q.tensor_tensor(R["t2"][:, :], R["ksil"][:, :], R["sin2"][:, :], ALU.mult))
    s.step("sp", r=[("t1", g)], w=[("t1sh", g)], dma=True,
           emit=lambda q: q.dma_start(out=R["t1sh"][:, :], in_=R["t1"][64:128, :]))
    s.step("sp", r=[("t2", g)], w=[("t2sh", g)], dma=True,
           emit=lambda q: q.dma_start(out=R["t2sh"][:, :], in_=R["t2"][64:128, :]))


def _rope1(nc, s, R, psrc0):
    for th in range(2):
        _rope1_silu(nc, s, R, psrc0 + th, th)
    _rope1_tail(nc, s, R)


def _rope2(nc, s, R, dst, dst_key):
    g = R["tg"]
    s.step("dve", r=[("t1", g), ("t2sh", g)], w=[dst_key],
           emit=lambda q: q.tensor_tensor(dst[0:64, :], R["t1"][0:64, :], R["t2sh"][:, :], ALU.subtract))
    s.step("dve", r=[("t2", g), ("t1sh", g)], w=[("obuf", g)],
           emit=lambda q: q.tensor_tensor(R["obuf"][:, :], R["t2"][0:64, :], R["t1sh"][:, :], ALU.add))
    s.step("sp", r=[("obuf", g)], w=[dst_key], dma=True,
           emit=lambda q, dst=dst: q.dma_start(out=dst[64:128, :], in_=R["obuf"][:, :]))


def _rope(nc, s, R, psrc0, dst, dst_key):
    _rope1(nc, s, R, psrc0)
    _rope2(nc, s, R, dst, dst_key)


def emit_l1pre(nc, s, pb, xnT, id_s, hd, fused=False, bg=None):
    wk_d, wv_d, cos_d, sin_d, dk_d = hd["wk"], hd["wv"], hd["cos"], hd["sin"], hd["dk"]
    krot_d, vst_d, pst_d = hd["krot_st"], hd["vst"], hd["pst"]
    with ExitStack() as ph:
        sb = lambda n, sh, dt: ph.enter_context(nc.sbuf_tensor(n, sh, dt))
        tp = [ph.enter_context(nc.psum_tensor(f"ptp{i}", [128, 1024], BF16)) for i in range(2)]
        R = _mk_rope_bufs(nc, sb, "pr", pb)
        wv_s = sb("p_wv", [P, FC, 512], BF16)
        wk_s = sb("p_wk", [P, 2, FC, 128], BF16)
        vg = sb("p_vg", [P, 8, 512], BF16)
        krot2 = sb("p_krot", [P, 2, T], BF16)
        Ssb2 = sb("p_S", [P, 2, 128], F32)
        pcb = sb("p_pcb", [P, 2, 128], BF16)
        ktdb = sb("p_ktdb", [P, 2, 128], BF16)
        dk_s = sb("p_dk", [P, NH], F32)
        if fused:
            fl2 = sb("p_fl2", [P, 1], F32)
            cbuf = sb("p_cbuf", [P, 2, 128], F32)
            s.step("sp", w=[("fl2",)], dma=True,
                   emit=lambda q: q.dma_start(out=fl2[:, :], in_=hd["flag2"].ap()[:, :]))
        s.step("sp", w=[("cos2",)], dma=True, emit=lambda q: q.dma_start(out=R["cos2"][:, :], in_=cos_d.ap()[:, :]))
        s.step("sp", w=[("sin2",)], dma=True, emit=lambda q: q.dma_start(out=R["sin2"][:, :], in_=sin_d.ap()[:, :]))
        s.step("sp", w=[("dk",)], dma=True, emit=lambda q: q.dma_start(out=dk_s[:, :], in_=dk_d.ap()[:, :]))

        def load_wv(hg):
            s.step("pool", w=[("wvw",)], dma=True,
                   emit=lambda q, hg=hg: q.dma_start(out=wv_s[:, :, :], in_=wv_d.ap()[hg]))
        def load_wk(h):
            s.step("pool", w=[("wkw", h % 2)], dma=True,
                   emit=lambda q, h=h: q.dma_start(out=wk_s[:, h % 2, :, :], in_=wk_d.ap()[h]))

        def kprep_steps(h):
            hp = h % 2
            for th in range(2):
                yield lambda th=th, hp=hp: (
                    s.step("pe", r=XNKEYS + [("wkw", hp)], w=[("ps", 4)],
                           emit=lambda q: [
                               q.matmul(pb[4][:, :], wk_s[:, hp, hc, :],
                                        xnT[:, hc, th * 512:(th + 1) * 512],
                                        start=(hc == 0), stop=(hc == 15)) for hc in range(FC)]),
                    _rope1_silu(nc, s, R, 4, th))
            yield lambda hp=hp: (_rope1_tail(nc, s, R),
                                 load_wk(h + 1) if h + 1 < 16 else None)
            def r2(hp=hp, h=h):
                _rope2(nc, s, R, krot2[:, hp, :], ("krot", hp))
                s.step("sp", r=[("krot", hp)], dma=True,
                       emit=lambda q: q.dma_start(out=krot_d.ap()[h], in_=krot2[:, hp, :]))
            yield r2
        load_wv(0)
        load_wk(0)
        for hg in range(4):
            for tc in range(8):
                bank = 4
                s.step("pe", r=XNKEYS + [("wvw",)], w=[("ps", bank)],
                       emit=lambda q, tc=tc, bank=bank: [
                           q.matmul(pb[bank][:, :], xnT[:, hc, tc * 128:(tc + 1) * 128],
                                    wv_s[:, hc, :], start=(hc == 0), stop=(hc == 15))
                           for hc in range(FC)])
                s.step("dve", r=[("ps", bank)], w=[("vg", tc)],
                       emit=lambda q, tc=tc, bank=bank: q.tensor_copy(vg[:, tc, :], pb[bank][:, :]))
                if bg:
                    bg()
                    bg()
            if hg < 3:
                load_wv(hg + 1)
            s.step("sp", r=[("vg", tc) for tc in range(8)], dma=True,
                   emit=lambda q, hg=hg: q.dma_start(
                       out=vst_d.ap()[hg].rearrange("tc p j -> p tc j"), in_=vg[:, :, :]))
            for hs in range(4):
                h = hg * 4 + hs
                hp = h % 2
                krot = krot2[:, hp, :]
                Ssb = Ssb2[:, hp, :]
                if h == 0:
                    for st in kprep_steps(0):
                        st()
                nxt = list(kprep_steps(h + 1)) if h + 1 < 16 else []
                s.step("dve", w=[("S", hp)], emit=lambda q, Ssb=Ssb: q.memset(Ssb[:, :], 0.0))
                for c in range(8):
                    if c in (1, 3, 5, 7) and nxt:
                        nxt.pop(0)()
                    cp = c % 2
                    s.step("dve", r=[("S", hp)], w=[("pcb", cp)],
                           emit=lambda q, cp=cp, Ssb=Ssb: q.tensor_copy(pcb[:, cp, :], Ssb[:, :]))
                    s.step("sp", r=[("pcb", cp)], dma=True,
                           emit=lambda q, h=h, c=c, cp=cp: q.dma_start(
                               out=pst_d.ap()[h, c], in_=pcb[:, cp, :]))
                    s.step("pe", r=[("krot", hp), ("id",)], w=[("ps", 6 + cp)],
                           emit=lambda q, c=c, cp=cp, krot=krot: q.transpose(
                               tp[cp][:, 0:128], krot[:, c * 128:(c + 1) * 128], id_s[:, :]))
                    s.step("dve", r=[("ps", 6 + cp), ("dk",)], w=[("ktdb", cp)],
                           emit=lambda q, h=h, cp=cp: q.tensor_single_scalar(
                               ktdb[:, cp, :], tp[cp][:, 0:128], dk_s[:, h:h + 1], ALU.mult))
                    s.step("pe", r=[("ktdb", cp), ("vg", c)], w=[("ps", 5)],
                           emit=lambda q, c=c, hs=hs, cp=cp: q.matmul(
                               pb[5][:, 0:128], ktdb[:, cp, :],
                               vg[:, c, hs * 128:(hs + 1) * 128], start=True, stop=True))
                    s.step("dve", r=[("ps", 5)], w=[("S", hp)],
                           emit=lambda q, h=h, cp=cp, Ssb=Ssb: q.scalar_tensor_tensor(
                               Ssb[:, :], Ssb[:, :], LAMC[h], pb[5][:, 0:128],
                               ALU.mult, ALU.add))
                    if bg:
                        bg()
                while nxt:
                    nxt.pop(0)()
                if fused:
                    s.step("dve", r=[("S", hp), ("fl2",)], w=[("cb", h % 2)],
                           emit=lambda q, h=h, Ssb=Ssb: q.tensor_single_scalar(
                               cbuf[:, h % 2, :], Ssb[:, :], fl2[:, 0:1], ALU.mult))
                    s.step("sp", r=[("cb", h % 2)], w=[("ctb", h)], dma=True,
                           emit=lambda q, h=h: q.dma_start(
                               out=hd["contrib"].ap()[h], in_=cbuf[:, h % 2, :]))
                else:
                    s.step("sp", r=[("S", hp)], dma=True,
                           emit=lambda q, h=h, Ssb=Ssb: q.dma_start(out=hd["sfin"].ap()[h], in_=Ssb[:, :]))
        if fused:
            s.step("pool", r=[("ctb", h) for h in range(NH)], w=[("rstD",)], cc=True,
                   emit=lambda q: q.collective_compute(
                       "AllReduce", ALU.add, [[0, 1], [2, 3], [4, 5], [6, 7]],
                       [hd["contrib"].ap().opt()], [hd["rst"].ap().opt()]))
        s.barrier()


def build_l1pre():
    nc = bass.Bass("TRN2", target_bir_lowering=False)
    din = lambda n, sh, dt: nc.dram_tensor(n, sh, dt, kind="ExternalInput")
    x1_d = din("x1", [T, D], F32)
    hd = {
        "wk": din("l1wk", [NH, 128, FC, 128], F32),
        "wv": din("l1wv", [4, 128, FC, 512], F32),
        "cos": din("cos2", [128, T], BF16),
        "sin": din("sin2", [128, T], BF16),
        "dk": din("dkcol", [128, NH], F32),
        "sfin": nc.dram_tensor("S_fin", [NH, 128, 128], F32, kind="ExternalOutput"),
        "krot_st": nc.dram_tensor("krot", [NH, 128, T], BF16, kind="ExternalOutput"),
        "vst": nc.dram_tensor("vst", [4, 8, 128, 512], BF16, kind="ExternalOutput"),
        "pst": nc.dram_tensor("pst", [NH, 8, 128, 128], BF16, kind="ExternalOutput"),
    }
    s, pb, xres, xnT, id_s = _mk_common(nc)
    emit_stage0(nc, s, x1_d.ap(), xres, xnT, id_s)
    emit_l1pre(nc, s, pb, xnT, id_s, hd)
    return nc


def emit_l1g1(nc, s, pb, xnT, hd):
    g1 = nc.alloc_sbuf_tensor("m_g1", [P, T], BF16)
    wg2_s = nc.alloc_sbuf_tensor("m_wg2", [P, D], BF16)
    s.step("pool", w=[("wg2",)], dma=True,
           emit=lambda q: q.dma_start(out=wg2_s[:, :], in_=hd["wg2"].ap()[:, :]))
    with ExitStack() as ph:
        wg1_s = ph.enter_context(nc.sbuf_tensor("m_wg1", [P, FC, 128], BF16))
        s.step("pool", w=[("wg1",)], dma=True,
               emit=lambda q: q.dma_start(out=wg1_s[:, :, :], in_=hd["wg1"].ap()[:, :, :]))
        for th in range(2):
            s.step("pe", r=XNKEYS + [("wg1",)], w=[("ps", th)],
                   emit=lambda q, th=th: [
                       q.matmul(pb[th][:, :], wg1_s[:, hc, :], xnT[:, hc, th * 512:(th + 1) * 512],
                                start=(hc == 0), stop=(hc == 15)) for hc in range(FC)])
            s.step("dve", r=[("ps", th)], w=[("g1",)],
                   emit=lambda q, th=th: q.tensor_copy(g1[:, th * 512:(th + 1) * 512], pb[th][:, :]))
        s.barrier()
    return g1, wg2_s


def emit_l1ffn(nc, s, xres, xnT, pb, h_st, wgath_d, wgath_key=None, bg=None):
    rk = [wgath_key] if wgath_key else []
    def wg_load(s2, ic):
        s2.step("sp", r=rk, w=[("wg", ic % 2)], dma=True,
                emit=lambda q, ic=ic: q.dma_start(
                    out=s2._wg[:, ic % 2, :, :],
                    in_=wgath_d.ap()[4 * ic:4 * ic + 4].rearrange("b p j -> p b j")))
    def wu_load(s2, ic):
        s2.step("sp", r=rk, w=[("wu", ic % 2)], dma=True,
                emit=lambda q, ic=ic: q.dma_start(
                    out=s2._wu[:, ic % 2, :, :],
                    in_=wgath_d.ap()[172 + 4 * ic:172 + 4 * ic + 4].rearrange("b p j -> p b j")))
    def wd_load(s2, ic, oc):
        s2.step("act", r=rk, w=[("wd", ic % 3)], dma=True,
                emit=lambda q, ic=ic, oc=oc: q.dma_start(
                    out=s2._wd[:, ic % 3, :], in_=wgath_d.ap()[344 + 4 * ic + oc]))
    wcfg = {
        "gu_shape": [P, 2, 4, 512],
        "gu_slice": lambda ws, slot, hc: ws[:, slot, hc // 4, (hc % 4) * 128:((hc % 4) + 1) * 128],
        "wg_load": wg_load, "wu_load": wu_load, "wd_load": wd_load,
    }
    emit_qffn(nc, s, xres, xnT, pb, h_st, NIC1, False, wcfg, pfx="q1", bg=bg)


def l1ffnA_units(nc, s, ph, xnT, pb, h_st, wgath_d):
    sb = lambda n, sh, dt: ph.enter_context(nc.sbuf_tensor(n, sh, dt))
    wg_s = sb("q1_wg", [P, 2, 4, 512], BF16)
    wu_s = sb("q1_wu", [P, 2, 4, 512], BF16)
    h_s = sb("q1_h", [P, 3, T], BF16)
    sg = sb("q1_sgb", [P, 2, 512], BF16)
    su = sb("q1_sub", [P, 2, 512], BF16)
    gsl = lambda ws, slot, hc: ws[:, slot, hc // 4, (hc % 4) * 128:((hc % 4) + 1) * 128]
    def wg_load(ic):
        s.step("pool", r=[("wgath",)], w=[("wg", ic % 2)], dma=True,
               emit=lambda q, ic=ic: q.dma_start(
                   out=wg_s[:, ic % 2, :, :],
                   in_=wgath_d.ap()[4 * ic:4 * ic + 4].rearrange("b p j -> p b j")))
    def wu_load(ic):
        s.step("pool", r=[("wgath",)], w=[("wu", ic % 2)], dma=True,
               emit=lambda q, ic=ic: q.dma_start(
                   out=wu_s[:, ic % 2, :, :],
                   in_=wgath_d.ap()[172 + 4 * ic:172 + 4 * ic + 4].rearrange("b p j -> p b j")))
    wg_load(0)
    wu_load(0)
    def mmgu(q, ws, ic, th, bank):
        return [q.matmul(pb[bank][:, :], gsl(ws, ic % 2, hc),
                         xnT[:, hc, th * 512:(th + 1) * 512],
                         start=(hc == 0), stop=(hc == 15)) for hc in range(FC)]
    def u_ep(th, ic):
        s.step("act", r=[("ps", th)], w=[("sg", th)],
               emit=lambda q, th=th: q.activation(
                   out=sg[:, th, :], in_=pb[th][:, :], func=AF.Silu))
        s.step("act", r=[("ps", 2 + th)], w=[("su", th)],
               emit=lambda q, th=th: q.activation(
                   out=su[:, th, :], in_=pb[2 + th][:, :], func=AF.Copy))
        s.step("dve", r=[("sg", th), ("su", th)], w=[("h", ic % 3)],
               emit=lambda q, ic=ic, th=th: q.tensor_tensor(
                   h_s[:, ic % 3, th * 512:(th + 1) * 512],
                   sg[:, th, :], su[:, th, :], ALU.mult))
        if th == 1:
            s.step("sp", r=[("h", ic % 3)], w=[("hst", ic)], dma=True,
                   emit=lambda q, ic=ic: q.dma_start(out=h_st.ap()[ic], in_=h_s[:, ic % 3, :]))
    def gen():
        for ic in range(NIC1):
            def u_g(ic=ic):
                for th in range(2):
                    s.step("pe", r=XNKEYS + [("wg", ic % 2)], w=[("ps", th)],
                           emit=lambda q, th=th, ic=ic: mmgu(q, wg_s, ic, th, th))
                if ic + 1 < NIC1:
                    wg_load(ic + 1)
            yield u_g
            def u_u(ic=ic):
                for th in range(2):
                    s.step("pe", r=XNKEYS + [("wu", ic % 2)], w=[("ps", 2 + th)],
                           emit=lambda q, th=th, ic=ic: mmgu(q, wu_s, ic, th, 2 + th))
                if ic + 1 < NIC1:
                    wu_load(ic + 1)
            yield u_u
            yield lambda ic=ic: u_ep(0, ic)
            yield lambda ic=ic: u_ep(1, ic)
    return gen()


def emit_l1ffnB(nc, s, xres, pb, h_st, wgath_d):
    with ExitStack() as ph:
        sb = lambda n, sh, dt: ph.enter_context(nc.sbuf_tensor(n, sh, dt))
        pext = [ph.enter_context(nc.psum_tensor(f"q1px{i}", [128, 512], F32)) for i in range(2)]
        pb8 = list(pb) + pext
        wd_s = sb("q1_wd", [P, 3, 512], BF16)
        hl = sb("q1_hl", [P, 3, T], BF16)
        for oc in range(4):
            for ic in range(NIC1):
                s.step("sp", r=[("hst", ic)], w=[("hl", ic % 3)], dma=True,
                       emit=lambda q, ic=ic: q.dma_start(out=hl[:, ic % 3, :], in_=h_st.ap()[ic]))
                s.step("act", r=[("wgath",)], w=[("wd", ic % 3)], dma=True,
                       emit=lambda q, ic=ic, oc=oc: q.dma_start(
                           out=wd_s[:, ic % 3, :], in_=wgath_d.ap()[344 + 4 * ic + oc]))
                s.step("pe", r=[("hl", ic % 3), ("wd", ic % 3)],
                       w=[("ps", tt2) for tt2 in range(8)],
                       emit=lambda q, ic=ic: [
                           q.matmul(pb8[tt2][:, :], hl[:, ic % 3, tt2 * 128:(tt2 + 1) * 128],
                                    wd_s[:, ic % 3, :], start=(ic == 0), stop=(ic == NIC1 - 1))
                           for tt2 in range(8)])
            for tt in range(8):
                s.step("dve", r=[("ps", tt)], w=[("xres", tt)],
                       emit=lambda q, tt=tt, oc=oc: q.tensor_tensor(
                           xres[:, tt, oc * 512:(oc + 1) * 512],
                           xres[:, tt, oc * 512:(oc + 1) * 512], pb8[tt][:, :], ALU.add))
        s.barrier()


def emit_l1attn(nc, s, pb, xres, xnT, id_s, g1, wg2_s, hd, fused=False):
    krot_d, vst_d, pst_d = hd["krot_st"], hd["vst"], hd["pst"]
    wq_d, wo_d, nw_d = hd["wq"], hd["wo"], hd["nw"]
    cos_d, sin_d, dq_d, D_d, lf_d = hd["cos"], hd["sin"], hd["dq"], hd["Dm"], hd["lf"]
    xg_st, oft_st = hd["xg_st"], hd["oft_st"]
    mscols = nc.alloc_sbuf_tensor("m_ms", [P, 128], F32)
    rstd = nc.alloc_sbuf_tensor("m_rstd", [P, 128], F32)
    with ExitStack() as ph:
        sb = lambda n, sh, dt: ph.enter_context(nc.sbuf_tensor(n, sh, dt))
        tp = [ph.enter_context(nc.psum_tensor(f"mtp{i}", [128, 1024], BF16)) for i in range(2)]
        R0 = _mk_rope_bufs(nc, sb, "mr0", pb, tag=0)
        R1 = _mk_rope_bufs(nc, sb, "mr1", pb, tag=1, shared=R0)
        R = R0
        wq_s = sb("m_wq", [P, 2, FC, 128], BF16)
        qrot2 = sb("m_qrot", [P, 2, T], BF16)
        krot2 = sb("m_krot", [P, 4, T], BF16)
        vg = sb("m_vg", [P, 2, 8, 512], BF16)
        pstb2 = sb("m_pst", [P, 4, 8, 128], BF16)
        Rp = sb("m_Rp", [P, NH, 128], BF16)
        D_sb = sb("m_D", [P, NH, 128], BF16)
        dq_sb = sb("m_dq", [P, NH, 128], BF16)
        lf_s = sb("m_lf", [P, 128], F32)
        CPD = 4
        qt = sb("m_qt", [P, CPD, 128], BF16)
        suseb = sb("m_suseb", [P, CPD, 128], BF16)
        scd = sb("m_scd", [P, CPD, 128], BF16)
        scdT = sb("m_scdT", [P, CPD, 128], BF16)
        sgf = sb("m_sgf", [P, T], F32)
        sgb = sb("m_sgb", [P, 2, T], BF16)
        sqt = sb("m_sqt", [P, CPD, 128], F32)
        xgh = sb("m_xgh", [P, 2, 8, 128], BF16)

        s.step("sp", w=[("cos2",)], dma=True, emit=lambda q: q.dma_start(out=R["cos2"][:, :], in_=cos_d.ap()[:, :]))
        s.step("sp", w=[("sin2",)], dma=True, emit=lambda q: q.dma_start(out=R["sin2"][:, :], in_=sin_d.ap()[:, :]))
        rsrc = hd["rst"] if fused else hd["R"]
        s.step("pool", r=([("rstD",)] if fused else []), w=[("Rp",)], dma=True,
               emit=lambda q: q.dma_start(out=Rp[:, :, :], in_=rsrc.ap().rearrange("h p j -> p h j")))
        s.step("sp", w=[("Ds",)], dma=True,
               emit=lambda q: q.dma_start(out=D_sb[:, :, :], in_=D_d.ap().rearrange("h p j -> p h j")))
        s.step("sp", w=[("dq",)], dma=True,
               emit=lambda q: q.dma_start(out=dq_sb[:, :, :], in_=dq_d.ap().rearrange("h p j -> p h j")))
        s.step("sp", w=[("lf",)], dma=True, emit=lambda q: q.dma_start(out=lf_s[:, :], in_=lf_d.ap()[:, :]))

        def load_wq(h):
            s.step("pool", w=[("wqw", h % 2)], dma=True,
                   emit=lambda q, h=h: q.dma_start(out=wq_s[:, h % 2, :, :], in_=wq_d.ap()[h]))

        def head_prep_steps(h):
            hp = h % 2
            for th in range(2):
                yield lambda th=th, hp=hp: s.step(
                    "pe", r=XNKEYS + [("wqw", hp)], w=[("ps", th)],
                    emit=lambda q: [
                        q.matmul(pb[th][:, :], wq_s[:, hp, hc, :],
                                 xnT[:, hc, th * 512:(th + 1) * 512],
                                 start=(hc == 0), stop=(hc == 15)) for hc in range(FC)])
            Rh = R0 if h % 2 == 0 else R1
            yield lambda hp=hp, Rh=Rh: (_rope1(nc, s, Rh, 0),
                                        load_wq(h + 1) if h + 1 < 16 else None)
            yield lambda hp=hp, Rh=Rh: _rope2(nc, s, Rh, qrot2[:, hp, :], ("qrot", hp))
            for b in range(2):
                yield lambda b=b, h=h: s.step(
                    "pe", r=[("g1",), ("wg2",)], w=[("ps", b)],
                    emit=lambda q: [
                        q.matmul(pb[b][:, i * 128:(i + 1) * 128],
                                 g1[:, (b * 4 + i) * 128:(b * 4 + i + 1) * 128],
                                 wg2_s[:, h * 128:(h + 1) * 128],
                                 start=(i == 0), stop=(i == 3)) for i in range(4)])
            def tanhts(hp=hp):
                for b in range(2):
                    s.step("act", r=[("ps", b)], w=[("sgf",)],
                           emit=lambda q, b=b: q.activation(
                               out=sgf[:, b * 512:(b + 1) * 512], in_=pb[b][:, :],
                               func=AF.Tanh, scale=0.5))
                s.step("dve", r=[("sgf",)], w=[("sgb", hp)],
                       emit=lambda q: q.tensor_scalar(
                           sgb[:, hp, :], sgf[:, :], 0.5, 0.5, ALU.mult, ALU.add))
            yield tanhts
        load_wq(0)
        def make_stages(h, hs):
            hp = h % 2
            h4 = h % 4
            g2 = (h // 4) % 2
            qrot = qrot2[:, hp, :]
            krot = krot2[:, h4, :]
            sb_b = 2 + hp
            ob = 4 + hp
            tpb = hp
            st = {}
            def sQ(c):
                cp = hp * 2 + c % 2
                s.step("dve", r=[("qrot", hp), ("dq",)], w=[("qt", cp)],
                       emit=lambda q: q.tensor_tensor(
                           qt[:, cp, :], qrot[:, c * 128:(c + 1) * 128], dq_sb[:, h, :], ALU.mult))
                s.step("dve", r=[("Rp",), ("lf",), ("pstb", h4)], w=[("suseb", cp)],
                       emit=lambda q: q.scalar_tensor_tensor(
                           suseb[:, cp, :], Rp[:, h, :], lf_s[:, h * 8 + c:h * 8 + c + 1],
                           pstb2[:, h4, c, :], ALU.mult, ALU.add))
            def sS(c):
                s.step("pe", r=[("qrot", hp), ("krot", h4)], w=[("ps", sb_b)],
                       emit=lambda q: q.matmul(
                           pb[sb_b][:, 0:128], qrot[:, c * 128:(c + 1) * 128],
                           krot[:, c * 128:(c + 1) * 128], start=True, stop=True))
            def sD(c):
                cp = hp * 2 + c % 2
                s.step("dve", r=[("ps", sb_b), ("Ds",)], w=[("scd", cp)],
                       emit=lambda q: q.tensor_tensor(
                           scd[:, cp, :], pb[sb_b][:, 0:128], D_sb[:, h, :], ALU.mult))
            def sT(c):
                cp = hp * 2 + c % 2
                s.step("pe", r=[("scd", cp), ("id",)], w=[("ps", 6 + tpb)],
                       emit=lambda q: q.transpose(tp[tpb][:, 0:128], scd[:, cp, :], id_s[:, :]))
                s.step("act", r=[("ps", 6 + tpb)], w=[("scdT", cp)],
                       emit=lambda q: q.copy(scdT[:, cp, :], tp[tpb][:, 0:128]))
            def sAV(c):
                cp = hp * 2 + c % 2
                s.step("pe", r=[("scdT", cp), ("vg", g2), ("qt", cp), ("suseb", cp)],
                       w=[("ps", ob)],
                       emit=lambda q: [
                           q.matmul(pb[ob][:, 0:128], scdT[:, cp, :],
                                    vg[:, g2, c, hs * 128:(hs + 1) * 128], start=True, stop=False),
                           q.matmul(pb[ob][:, 0:128], qt[:, cp, :], suseb[:, cp, :],
                                    start=False, stop=True)])
            def sX(c):
                cp = hp * 2 + c % 2
                s.step("dve", r=[("ps", ob), ("sgb", hp)], w=[("xgh", hp, c)],
                       emit=lambda q: q.tensor_tensor(
                           xgh[:, hp, c, :], pb[ob][:, 0:128],
                           sgb[:, hp, c * 128:(c + 1) * 128], ALU.mult))
                s.step("act", r=[("xgh", hp, c)], w=[("sqt", cp), ("ms",)],
                       emit=lambda q: q.activation(
                           out=sqt[:, cp, :], in_=xgh[:, hp, c, :], func=AF.Square,
                           accum_out=mscols[:, h * 8 + c:h * 8 + c + 1]))
            st.update(sQ=sQ, sS=sS, sD=sD, sT=sT, sAV=sAV, sX=sX)
            return st

        def load_vg(hg):
            s.step("sp", w=[("vg", hg % 2)], dma=True,
                   emit=lambda q, hg=hg: q.dma_start(
                       out=vg[:, hg % 2, :, :], in_=vst_d.ap()[hg].rearrange("tc p j -> p tc j")))
        def load_kp(h):
            h4 = h % 4
            s.step("sp", w=[("krot", h4)], dma=True,
                   emit=lambda q, h=h, h4=h4: q.dma_start(out=krot2[:, h4, :], in_=krot_d.ap()[h]))
            s.step("sp", w=[("pstb", h4)], dma=True,
                   emit=lambda q, h=h, h4=h4: q.dma_start(
                       out=pstb2[:, h4, :, :], in_=pst_d.ap()[h].rearrange("c p j -> p c j")))
        for hg in range(4):
            load_vg(hg)
            for pr in range(2):
                h0 = hg * 4 + 2 * pr
                h1 = h0 + 1
                hs0, hs1 = 2 * pr, 2 * pr + 1
                load_kp(h0)
                load_kp(h1)
                if h0 == 0:
                    for st in head_prep_steps(0):
                        st()
                    for st in head_prep_steps(1):
                        st()
                nxt = []
                if h0 + 2 < 16:
                    u2 = list(head_prep_steps(h0 + 2))
                    u3 = list(head_prep_steps(h0 + 3)) if h0 + 3 < 16 else []
                    while u2 or u3:
                        if u2:
                            nxt.append(u2.pop(0))
                        if u3:
                            nxt.append(u3.pop(0))
                st0 = make_stages(h0, hs0)
                st1 = make_stages(h1, hs1)
                for i in range(10):
                    for st in (st0, st1):
                        if i - 2 >= 0:
                            st["sX"](i - 2)
                        if i < 8:
                            st["sQ"](i)
                            st["sS"](i)
                        if 1 <= i <= 8:
                            st["sD"](i - 1)
                            st["sT"](i - 1)
                            st["sAV"](i - 1)
                    if 1 <= i <= 7 and nxt:
                        nxt.pop(0)()
                        if nxt:
                            nxt.pop(0)()
                while nxt:
                    nxt.pop(0)()
                for h in (h0, h1):
                    hp = h % 2
                    s.step("sp", r=[("xgh", hp, c) for c in range(8)], w=[("xgst", h)], dma=True,
                           emit=lambda q, h=h, hp=hp: q.dma_start(
                               out=xg_st.ap()[h].rearrange("c p j -> p c j"), in_=xgh[:, hp, :, :]))
        s.barrier()
    s.step("act", r=[("ms",), ("cst",)], w=[("rstd",)],
           emit=lambda q: q.activation(out=rstd[:, :], in_=mscols[:, :], func=AF.Sqrt,
                                       scale=1.0 / 128.0, bias=nc._cst[:, 1:2]))
    s.step("dve", r=[("rstd",)], w=[("rstd",)],
           emit=lambda q: q.reciprocal(rstd[:, :], rstd[:, :]))
    with ExitStack() as ph:
        sb = lambda n, sh, dt: ph.enter_context(nc.sbuf_tensor(n, sh, dt))
        tp = [ph.enter_context(nc.psum_tensor(f"p2tp{i}", [128, 1024], BF16)) for i in range(2)]
        nw_s = sb("m_nw", [P, D], BF16)
        ofc = sb("m_ofc", [P, 4, 128], BF16)
        s.step("pool", w=[("nw",)], dma=True,
               emit=lambda q: q.dma_start(out=nw_s[:, :], in_=nw_d.ap()[:, :]))
        xgh2b = ph.enter_context(nc.sbuf_tensor("m_xgh2b", [P, 2, 8, 128], BF16))
        ofthb = ph.enter_context(nc.sbuf_tensor("m_ofthb", [P, 2, T], BF16))
        def p2_stt(h, c):
            hp, cp = h % 2, c % 4
            s.step("dve", r=[("xgh2", hp), ("rstd",), ("nw",)], w=[("ofc", cp)],
                   emit=lambda q: q.scalar_tensor_tensor(
                       ofc[:, cp, :], xgh2b[:, hp, c, :], rstd[:, h * 8 + c:h * 8 + c + 1],
                       nw_s[:, h * 128:(h + 1) * 128], ALU.mult, ALU.mult))
        def p2_tr(h, c):
            hp, cp = h % 2, c % 4
            s.step("pe", r=[("ofc", cp), ("id",)], w=[("ps", 6 + c % 2)],
                   emit=lambda q: q.transpose(tp[c % 2][:, 0:128], ofc[:, cp, :], id_s[:, :]))
            s.step("dve", r=[("ps", 6 + c % 2)], w=[("ofth", hp)],
                   emit=lambda q: q.tensor_copy(
                       ofthb[:, hp, c * 128:(c + 1) * 128], tp[c % 2][:, 0:128]))
        for h in range(NH):
            hp = h % 2
            s.step("sp", r=[("xgst", h)], w=[("xgh2", hp)], dma=True,
                   emit=lambda q, h=h, hp=hp: q.dma_start(
                       out=xgh2b[:, hp, :, :], in_=xg_st.ap()[h].rearrange("c p j -> p c j")))
            for i in range(9):
                if i < 8:
                    p2_stt(h, i)
                if i >= 1:
                    p2_tr(h, i - 1)
            s.step("sp", r=[("ofth", hp)], w=[("oft", h)], dma=True,
                   emit=lambda q, h=h, hp=hp: q.dma_start(out=oft_st.ap()[h], in_=ofthb[:, hp, :]))
        s.barrier()
    with ExitStack() as ph:
        sb = lambda n, sh, dt: ph.enter_context(nc.sbuf_tensor(n, sh, dt))
        wo_s = sb("m_wo", [P, 2, FC, 512], BF16)
        otile = sb("m_otile", [P, 2, FC, 128], BF16)
        oftk = [("oft", h) for h in range(NH)]
        for ps2 in range(2):
            for o2 in range(2):
                s.step("pool", w=[("wow", o2)], dma=True,
                       emit=lambda q, ps2=ps2, o2=o2: q.dma_start(
                           out=wo_s[:, o2, :, :], in_=wo_d.ap()[ps2 * 2 + o2]))
            for tt in range(8):
                s.step("sp", r=oftk, w=[("otile", tt % 2)], dma=True,
                       emit=lambda q, tt=tt: q.dma_start(
                           out=otile[:, tt % 2, :, :],
                           in_=oft_st.ap()[:, :, tt * 128:(tt + 1) * 128].rearrange("h d t -> d h t")))
                for o2 in range(2):
                    bank = 2 * (tt % 2) + o2
                    s.step("pe", r=[("otile", tt % 2), ("wow", o2)], w=[("ps", bank)],
                           emit=lambda q, tt=tt, o2=o2, bank=bank: [
                               q.matmul(pb[bank][:, :], otile[:, tt % 2, fc, :],
                                        wo_s[:, o2, fc, :], start=(fc == 0), stop=(fc == 15))
                               for fc in range(FC)])
                    s.step("dve", r=[("ps", bank)], w=[("xres", tt)],
                           emit=lambda q, tt=tt, ps2=ps2, o2=o2, bank=bank: q.tensor_tensor(
                               xres[:, tt, (ps2 * 2 + o2) * 512:(ps2 * 2 + o2 + 1) * 512],
                               xres[:, tt, (ps2 * 2 + o2) * 512:(ps2 * 2 + o2 + 1) * 512],
                               pb[bank][:, :], ALU.add))
        s.barrier()


def build_l1main():
    nc = bass.Bass("TRN2", target_bir_lowering=False)
    din = lambda n, sh, dt: nc.dram_tensor(n, sh, dt, kind="ExternalInput")
    x1_d = din("x1", [T, D], F32)
    hd = {
        "krot_st": din("krot", [NH, 128, T], BF16),
        "vst": din("vst", [4, 8, 128, 512], BF16),
        "pst": din("pst", [NH, 8, 128, 128], BF16),
        "R": din("Rstate", [NH, 128, 128], F32),
        "wq": din("l1wq", [NH, 128, FC, 128], F32),
        "wo": din("l1wo", [4, 128, FC, 512], F32),
        "wg1": din("wg1_s", [128, FC, 128], F32),
        "wg2": din("wg2_s", [128, D], F32),
        "nw": din("nwrep", [128, D], F32),
        "cos": din("cos2", [128, T], BF16),
        "sin": din("sin2", [128, T], BF16),
        "dq": din("dqrep", [NH, 128, 128], BF16),
        "Dm": din("D_s", [NH, 128, 128], BF16),
        "lf": din("lamflag", [128, 128], F32),
    }
    wgath_d = din("w_gath", [8 * SHARD, 128, 512], BF16)
    xout = nc.dram_tensor("xout", [T, D], F32, kind="ExternalOutput")
    h_st = nc.dram_tensor("h_st", [NIC1, 128, T], BF16)
    hd["xg_st"] = nc.dram_tensor("xg_st", [NH, 8, 128, 128], BF16)
    hd["oft_st"] = nc.dram_tensor("oft_st", [NH, 128, T], BF16)

    s, pb, xres, xnT, id_s = _mk_common(nc)
    emit_stage0(nc, s, x1_d.ap(), xres, xnT, id_s)
    g1, wg2_s = emit_l1g1(nc, s, pb, xnT, hd)
    emit_l1ffn(nc, s, xres, xnT, pb, h_st, wgath_d)
    emit_l1attn(nc, s, pb, xres, xnT, id_s, g1, wg2_s, hd)
    _store_x(nc, s, xres, xout)
    return nc


class Deq:
    def __init__(self, nc, s, idx_d, lut_d, wsh_d, wgath_d, sb=None):
        self.nc, self.s = nc, s
        self.idx_d, self.wsh_d, self.wgath_d = idx_d, wsh_d, wgath_d
        if sb is None:
            sb = lambda n, sh, dt: nc.alloc_sbuf_tensor(n, sh, dt)
        self.lut_s = sb("dq_lut", [P, 16], F32)
        self.ib = sb("dq_ib", [P, 2, 512], I32)
        self.wb = sb("dq_wb", [P, 512], BF16)
        self.tb = sb("dq_tb", [P, 512], BF16)
        self.n = 0
        self.piece = 0
        self.triggered = False
        s.step("sp", w=[("dqlut",)], dma=True,
               emit=lambda q: q.dma_start(out=self.lut_s[:, :], in_=lut_d.ap()[:, :]))
        s.step("sp", w=[("dqib", 0)], dma=True,
               emit=lambda q: q.dma_start(out=self.ib[:, 0, :], in_=idx_d.ap()[0]))

    def bg(self):
        if self.n >= SHARD:
            return
        n = self.n
        self.n += 1
        s, ib, wb, tb, lut_s = self.s, self.ib, self.wb, self.tb, self.lut_s
        slot = n % 2
        def chain(q, slot=slot):
            ins = [q.tensor_scalar(wb[:, :], ib[:, slot, :], 0.0, lut_s[:, 0:1],
                                   ALU.is_equal, ALU.mult)]
            for j in range(1, 16):
                ins.append(q.tensor_scalar(tb[:, :], ib[:, slot, :], float(j),
                                           lut_s[:, j:j + 1], ALU.is_equal, ALU.mult))
                ins.append(q.tensor_tensor(wb[:, :], wb[:, :], tb[:, :], ALU.add))
            return ins
        s.step("dve", r=[("dqib", slot), ("dqlut",)], w=[("dqwb",), ("dqtb",)], emit=chain)
        if n + 1 < SHARD:
            s.step("sp", w=[("dqib", (n + 1) % 2)], dma=True,
                   emit=lambda q, n=n: q.dma_start(out=self.ib[:, (n + 1) % 2, :],
                                                   in_=self.idx_d.ap()[n + 1]))
        s.step("sp", r=[("dqwb",)], w=[("dqst", n)], dma=True,
               emit=lambda q, n=n: q.dma_start(out=self.wsh_d.ap()[n], in_=wb[:, :]))
        if self.n >= SHARD:
            self.trigger()

    def trigger(self):
        if self.triggered:
            return
        self.triggered = True
        self.s.step("pool", r=[("dqst", n) for n in range(SHARD)], w=[("wgath",)], cc=True,
                    emit=lambda q: q.collective_compute(
                        "AllGather", ALU.bypass, [[0, 1, 2, 3, 4, 5, 6, 7]],
                        [self.wsh_d.ap().opt()], [self.wgath_d.ap().opt()]))

    def finish(self):
        while self.n < SHARD:
            self.bg()
        self.trigger()


def build_fused():
    nc = bass.Bass("TRN2", target_bir_lowering=False)
    din = lambda n, sh, dt: nc.dram_tensor(n, sh, dt, kind="ExternalInput")
    x_own = din("x_own", [T, D], F32)
    x_halo = din("x_halo", [512, D], F32)
    dram0 = {
        "wq": din("wq_h", [NH, 128, FC, 128], F32),
        "wk": din("wk_h", [NH, 128, FC, 128], F32),
        "wv": din("wv_g", [4, 128, FC, 512], F32),
        "wo": din("wo_oc", [4, 128, FC, 512], F32),
        "masks": din("masks", [8, 128, 640], BF16),
    }
    gT_d = din("gT_h", [NIC0, 128, FC, 128], U8)
    uT_d = din("uT_h", [NIC0, 128, FC, 128], U8)
    dT_d = din("dT_h", [NIC0, 4, 128, 512], U8)
    gs_d = din("gs_c", [128, NIC0], F32)
    us_d = din("us_c", [128, NIC0], F32)
    dsrep_d = din("dsrep", [128, D], F32)
    idx_d = din("l1idx_shard", [SHARD, 128, 512], I32)
    lut_d = din("lutrep", [128, 16], F32)
    hd = {
        "wk": din("l1wk", [NH, 128, FC, 128], F32),
        "wv": din("l1wv", [4, 128, FC, 512], F32),
        "wq": din("l1wq", [NH, 128, FC, 128], F32),
        "wo": din("l1wo", [4, 128, FC, 512], F32),
        "wg1": din("wg1_s", [128, FC, 128], F32),
        "wg2": din("wg2_s", [128, D], F32),
        "nw": din("nwrep", [128, D], F32),
        "cos": din("cos2", [128, T], BF16),
        "sin": din("sin2", [128, T], BF16),
        "dk": din("dkcol", [128, NH], F32),
        "dq": din("dqrep", [NH, 128, 128], BF16),
        "Dm": din("D_s", [NH, 128, 128], BF16),
        "lf": din("lamflag", [128, 128], F32),
        "flag2": din("flag2c", [128, 1], F32),
    }
    xout = nc.dram_tensor("xout", [T, D], F32, kind="ExternalOutput")
    attn_st = nc.dram_tensor("attn_st", [NH, 128, T], BF16)
    h_st = nc.dram_tensor("h_st", [NIC0, 128, T], BF16)
    wsh_d = nc.dram_tensor("wshard", [SHARD, 128, 512], BF16)
    wgath_d = nc.dram_tensor("w_gath", [8 * SHARD, 128, 512], BF16, addr_space="Shared")
    hd["krot_st"] = nc.dram_tensor("krot", [NH, 128, T], BF16)
    hd["vst"] = nc.dram_tensor("vst", [4, 8, 128, 512], BF16)
    hd["pst"] = nc.dram_tensor("pst", [NH, 8, 128, 128], BF16)
    hd["contrib"] = nc.dram_tensor("contrib", [NH, 128, 128], F32)
    hd["rst"] = nc.dram_tensor("rst", [NH, 128, 128], F32)
    hd["xg_st"] = nc.dram_tensor("xg_st", [NH, 8, 128, 128], BF16)
    hd["oft_st"] = nc.dram_tensor("oft_st", [NH, 128, T], BF16)

    s, pb, xres, xnT, id_s = _mk_common(nc)
    phD = ExitStack()
    sbD = lambda n, sh, dt: phD.enter_context(nc.sbuf_tensor(n, sh, dt))
    deq = Deq(nc, s, idx_d, lut_d, wsh_d, wgath_d, sbD)
    with ExitStack() as ph0:
        haloT = ph0.enter_context(nc.sbuf_tensor("haloT", [P, FC, 512], BF16))
        emit_stage0(nc, s, x_own.ap(), xres, xnT, id_s, x_halo.ap(), haloT)
        emit_l0_attn(nc, s, xres, xnT, haloT, id_s, pb, attn_st, dram0)
    def wg_load(s2, ic):
        s2.step("pool", w=[("wg", ic % 2)], dma=True,
                emit=lambda q, ic=ic: q.dma_start(out=s2._wg[:, ic % 2, :, :], in_=gT_d.ap()[ic]))
    def wu_load(s2, ic):
        s2.step("pool", w=[("wu", ic % 2)], dma=True,
                emit=lambda q, ic=ic: q.dma_start(out=s2._wu[:, ic % 2, :, :], in_=uT_d.ap()[ic]))
    def wd_load(s2, ic, oc):
        s2.step("act", w=[("wdr", ic % 3)], dma=True,
                emit=lambda q, ic=ic, oc=oc: q.dma_start(out=s2._wdr[:, ic % 3, :], in_=dT_d.ap()[ic, oc]))
        s2.step("dve", r=[("wdr", ic % 3)], w=[("wd", ic % 3)],
                emit=lambda q, ic=ic: q.tensor_copy(s2._wd[:, ic % 3, :], s2._wdr[:, ic % 3, :]))
    wcfg = {
        "gu_shape": [P, 2, FC, 128],
        "gu_slice": lambda ws, slot, hc: ws[:, slot, hc, :],
        "wg_load": wg_load, "wu_load": wu_load, "wd_load": wd_load,
        "gs": gs_d, "us": us_d, "dsrep": dsrep_d, "wd_raw": True,
    }
    emit_qffn(nc, s, xres, xnT, pb, h_st, NIC0, True, wcfg, pfx="q0", bg=deq.bg,
              after_a=deq.finish)
    phD.close()
    emit_stage0(nc, s, None, xres, xnT, id_s, pfx="l2", load=False)
    g1, wg2_s = emit_l1g1(nc, s, pb, xnT, hd)
    with ExitStack() as phA:
        gen = l1ffnA_units(nc, s, phA, xnT, pb, h_st, wgath_d)
        def pump():
            try:
                next(gen)()
            except StopIteration:
                pass
        emit_l1pre(nc, s, pb, xnT, id_s, hd, fused=True, bg=pump)
        for u in gen:
            u()
    emit_l1ffnB(nc, s, xres, pb, h_st, wgath_d)
    emit_l1attn(nc, s, pb, xres, xnT, id_s, g1, wg2_s, hd, fused=True)
    _store_x(nc, s, xres, xout)
    return nc


FUSED = True


def host_prepare(inp):
    H = {}
    f32 = np.float32
    cc = np.ascontiguousarray
    wqkv = np.asarray(inp["L0_attn_w_qkv"], f32).reshape(3, NH, 128, FC, 128)
    H["wq_h"] = cc(wqkv[0].transpose(0, 3, 2, 1))
    H["wk_h"] = cc(wqkv[1].transpose(0, 3, 2, 1))
    H["wv_g"] = cc(wqkv[2].reshape(4, 4, 128, FC, 128).transpose(0, 4, 3, 1, 2)
                   .reshape(4, 128, FC, 512))
    H["wo_oc"] = cc(np.asarray(inp["L0_attn_w_out"], f32).reshape(4, 512, FC, 128)
                    .transpose(0, 3, 2, 1))
    H["gT_h"] = cc(np.asarray(inp["L0_ffn_gate_w"], np.int32).astype(np.uint8)
                   .reshape(NIC0, 128, FC, 128).transpose(0, 3, 2, 1))
    H["uT_h"] = cc(np.asarray(inp["L0_ffn_up_w"], np.int32).astype(np.uint8)
                   .reshape(NIC0, 128, FC, 128).transpose(0, 3, 2, 1))
    H["dT_h"] = cc(np.asarray(inp["L0_ffn_down_w"], np.int32).astype(np.uint8)
                   .reshape(4, 512, NIC0, 128).transpose(2, 0, 3, 1))
    H["gs_c"] = cc(np.asarray(inp["L0_ffn_gate_scale"], f32).reshape(NIC0, 128).T)
    H["us_c"] = cc(np.asarray(inp["L0_ffn_up_scale"], f32).reshape(NIC0, 128).T)
    H["dsrep"] = cc(np.broadcast_to(np.asarray(inp["L0_ffn_down_scale"], f32)[None, :], (128, D)))
    H["l1wq"] = cc(np.asarray(inp["L1_attn_Wq"], f32).reshape(FC, 128, NH, 128).transpose(2, 1, 0, 3))
    H["l1wk"] = cc(np.asarray(inp["L1_attn_Wk"], f32).reshape(FC, 128, NH, 128).transpose(2, 1, 0, 3))
    H["l1wv"] = cc(np.asarray(inp["L1_attn_Wv"], f32).reshape(FC, 128, 4, 512).transpose(2, 1, 0, 3))
    H["l1wo"] = cc(np.asarray(inp["L1_attn_Wo"], f32).reshape(FC, 128, 4, 512).transpose(2, 1, 0, 3))
    H["wg1_s"] = cc(np.asarray(inp["L1_attn_Wg1"], f32).reshape(FC, 128, 128).transpose(1, 0, 2))
    H["wg2_s"] = cc(np.asarray(inp["L1_attn_Wg2"], f32))
    H["nwrep"] = cc(np.broadcast_to(np.asarray(inp["L1_attn_norm_weight"], f32)[None, :], (128, D)))
    H["lutrep"] = cc(np.broadcast_to(np.asarray(inp["L1_ffn_lut"], f32)[None, :], (128, 16)))

    ll = -(2.0 ** (-8.0 * (np.arange(NH, dtype=np.float64) + 1.0) / NH))
    i = np.arange(128, dtype=np.float64)
    H["dkcol"] = np.exp(ll[None, :] * (127.0 - i[:, None])).astype(f32)
    dq = np.exp(ll[:, None] * (i[None, :] + 1.0)).astype(f32)
    H["dqrep"] = cc(np.broadcast_to(dq[:, None, :], (NH, 128, 128))).astype(BF)
    diff = i[:, None] - i[None, :]
    Dm = np.where(diff[None, :, :] >= 0, np.exp(ll[:, None, None] * diff[None, :, :]), 0.0)
    H["D_s"] = Dm.astype(f32).astype(BF)

    gidxT = np.asarray(inp["L1_ffn_gate_idx"], np.int32).T
    uidxT = np.asarray(inp["L1_ffn_up_idx"], np.int32).T
    didxT = np.asarray(inp["L1_ffn_down_idx"], np.int32).T
    def tile_gu(m):
        return m.reshape(FC, 128, NIC1, 128).transpose(2, 1, 0, 3).reshape(NIC1, 128, D)
    gt, ut = tile_gu(gidxT), tile_gu(uidxT)
    dt = didxT.reshape(NIC1, 128, 4, 512).transpose(0, 2, 1, 3)
    blocks = np.zeros((8 * SHARD, 128, 512), np.int32)
    b = 0
    for src in (gt, ut):
        for ic in range(NIC1):
            for jb in range(4):
                blocks[b] = src[ic, :, jb * 512:(jb + 1) * 512]
                b += 1
    for ic in range(NIC1):
        for oc in range(4):
            blocks[b] = dt[ic, oc]
            b += 1
    H["blocks"] = blocks

    x = np.asarray(inp["L0_attn_x"], f32)
    percore = []
    ii = np.arange(128)[:, None]
    jj = np.arange(640)[None, :]
    th = 10000.0 ** (-2.0 * np.arange(64, dtype=np.float64) / 128.0)
    lam_c = np.exp(np.float32(128.0) * ll.astype(f32)).astype(np.float64)
    for c in range(8):
        bq, half = c // 2, c % 2
        pos0 = half * 1024
        pc = {}
        pc["x_own"] = cc(x[bq, pos0:pos0 + 1024, :])
        hlo = pos0 - 512 if half == 1 else 0
        pc["x_halo"] = cc(x[bq, hlo:hlo + 512, :])
        m = np.zeros((8, 128, 640), np.float32)
        for tt in range(8):
            g = pos0 - 512 + tt * 128 + jj
            valid = (jj >= ii) & (jj <= ii + 512) & (g >= 0)
            m[tt] = np.where(valid, 0.0, NEG)
        pc["masks"] = m.astype(BF)
        tpos = (pos0 + np.arange(1024, dtype=np.float64))[None, :]
        ang = np.concatenate([th, th])[:, None] * tpos
        pc["cos2"] = np.cos(ang).astype(f32).astype(BF)
        pc["sin2"] = np.sin(ang).astype(f32).astype(BF)
        lf = np.zeros((128,), np.float64)
        for h in range(NH):
            for cch in range(8):
                lf[h * 8 + cch] = (lam_c[h] ** cch) * float(half)
        pc["lamflag"] = cc(np.broadcast_to(lf.astype(f32)[None, :], (128, 128)))
        pc["l1idx_shard"] = cc(blocks[c * SHARD:(c + 1) * SHARD])
        percore.append(pc)
    H["percore"] = percore
    return H


def kernel(**inputs):
    H = host_prepare(inputs)
    cores = list(range(8))
    if FUSED:
        ncF = build_fused()
        shared = {k: H[k] for k in ("gT_h", "uT_h", "dT_h", "gs_c", "us_c", "dsrep",
                                    "wq_h", "wk_h", "wv_g", "wo_oc", "l1wq", "l1wk",
                                    "l1wv", "l1wo", "wg1_s", "wg2_s", "nwrep",
                                    "lutrep", "dkcol", "dqrep", "D_s")}
        imF = []
        for c in cores:
            pc = H["percore"][c]
            m = dict(shared)
            m["x_own"] = pc["x_own"]
            m["x_halo"] = pc["x_halo"]
            m["masks"] = pc["masks"]
            m["cos2"] = pc["cos2"]
            m["sin2"] = pc["sin2"]
            m["lamflag"] = pc["lamflag"]
            m["l1idx_shard"] = pc["l1idx_shard"]
            m["flag2c"] = np.full((128, 1), 1.0 - (c % 2), np.float32)
            imF.append(m)
        res = run_bass_kernel_spmd(ncF, imF, core_ids=cores)
        out = np.concatenate([res.results[c]["xout"] for c in cores], axis=0)
        return out.reshape(4, 2048, 2048)
    ncA = build_deq()
    imA = [{"l1idx_shard": H["percore"][c]["l1idx_shard"], "lutrep": H["lutrep"]} for c in cores]
    resA = run_bass_kernel_spmd(ncA, imA, core_ids=cores)
    w_gath = np.concatenate([resA.results[c]["wshard"] for c in cores], axis=0)

    ncB = build_l0()
    shared = {k: H[k] for k in ("gT_h", "uT_h", "dT_h", "gs_c", "us_c", "dsrep")}
    shared["wq_h"] = H["wq_h"]; shared["wk_h"] = H["wk_h"]
    shared["wv_g"] = H["wv_g"]; shared["wo_oc"] = H["wo_oc"]
    imB = []
    for c in cores:
        m = dict(shared)
        m["x_own"] = H["percore"][c]["x_own"]
        m["x_halo"] = H["percore"][c]["x_halo"]
        m["masks"] = H["percore"][c]["masks"]
        imB.append(m)
    resB = run_bass_kernel_spmd(ncB, imB, core_ids=cores)
    x1 = [resB.results[c]["x1"] for c in cores]

    ncC = build_l1pre()
    imC = [{"x1": x1[c], "l1wk": H["l1wk"], "l1wv": H["l1wv"],
            "cos2": H["percore"][c]["cos2"], "sin2": H["percore"][c]["sin2"],
            "dkcol": H["dkcol"]} for c in cores]
    resC = run_bass_kernel_spmd(ncC, imC, core_ids=cores)

    ncD = build_l1main()
    imD = [{"x1": x1[c],
            "krot": resC.results[c]["krot"], "vst": resC.results[c]["vst"],
            "pst": resC.results[c]["pst"], "Rstate": resC.results[c ^ 1]["S_fin"],
            "l1wq": H["l1wq"], "l1wo": H["l1wo"], "wg1_s": H["wg1_s"],
            "wg2_s": H["wg2_s"], "nwrep": H["nwrep"],
            "cos2": H["percore"][c]["cos2"], "sin2": H["percore"][c]["sin2"],
            "dqrep": H["dqrep"], "D_s": H["D_s"], "lamflag": H["percore"][c]["lamflag"],
            "w_gath": w_gath} for c in cores]
    resD = run_bass_kernel_spmd(ncD, imD, core_ids=cores)
    out = np.concatenate([resD.results[c]["xout"] for c in cores], axis=0)
    return out.reshape(4, 2048, 2048)
